# version 07
# baseline (speedup 1.0000x reference)
import numpy as np
import concourse.bass as bass
import concourse.mybir as mybir
from concourse.bass_utils import run_bass_kernel_spmd

F32 = mybir.dt.float32
BF16 = mybir.dt.bfloat16
ALU = mybir.AluOpType
AF = mybir.ActivationFunctionType
AXX = mybir.AxisListType.X

D = 1024
HID = 4096
NE = 8
TOK = 4096
OWN = 2048
NT = 32
NTO = 16
NP = 16
EXPS = 12.0
EPS = 1e-5
INVD = 1.0 / D
INVN = 1.0 / (4096.0 * 1024.0)

R_WQKV = 0
R_QT = (24576, 42016)
R_KT = (28672, 46112)
R_VB = (32768, 50208)
R_PT = 34848
R_OUTS = 36896
R_XNAT = 37920
R_YSCR = 39968
R_OSTG = 34848
R_WI = 0
R_WO = 16384
R_M = 32768
AR_N = 53248

BANKS4 = ("S0a", "S0b", "S0c", "S0d")
DEAL_CNT = (5, 5, 5, 5, 5, 5, 5, 1)
DEAL_SLOTS = (2, 8, 14, 20, 26)
PROC = list(range(8, 16)) + list(range(8))
TILES = list(range(16, 32)) + list(range(16))


class _DryInst:
    def then_inc(self, sem, n=1):
        return self


_DRY_INST = _DryInst()


class _Dry:
    def __getattr__(self, name):
        def f(*a, **k):
            return _DRY_INST
        return f


def _g_decode(g):
    if g < 16:
        return 0, g
    if g < 32:
        return 1, g - 16
    return 2, g - 32


def _ab_schedule():
    seq = []
    for half in (0, 1):
        first = 8 * half
        for g in range(36):
            seq.append(("qkvg", first, g))
        for pj in range(first, first + 8):
            gbase = 0
            for qc in range(8):
                if pj < first + 7:
                    for g in range(gbase, gbase + DEAL_CNT[qc]):
                        seq.append(("qkvg", pj + 1, g))
                gbase += DEAL_CNT[qc]
                for jb in range(4):
                    seq.append(("atr", pj, qc, jb))
    seq.append(("sum2x",))
    seq.append(("bcx",))
    for tt in range(NTO):
        seq.append(("glogx", tt))
    return {k: ("A" if i % 2 == 0 else "B") for i, k in enumerate(seq)}


BANKOF = _ab_schedule()


def _moe_a_bank(i, cb, hh, tc):
    return BANKS4[(i * 64 + cb * 8 + hh * 4 + tc) % 4]


def _moe_b_bank(i, dc, tt):
    return ("OTa", "OTb", "A", "B")[(i * 32 + dc * NTO + tt) % 4]


def _emit(T, rec, MS, cnt):
    AR = T["AR"]
    ARF = T["ARF"]
    xnTb = T["xnTb"]
    yat = T["yat"]
    yffn = T["yffn"]
    xstg = T["xstg"]
    wgsb = T["wgsb"]
    identsb = T["identsb"]
    onescol = T["onescol"]
    onesrow = T["onesrow"]
    consts = T["consts"]
    stats = T["stats"]
    st12 = T["st12"]
    lnm = T["lnm"]
    lnr = T["lnr"]
    gts = T["gts"]
    lgt = T["lgt"]
    gsc = T["gsc"]
    pS0, pS1, pOT = T["pS0"], T["pS1"], T["pOT"]
    S = T["sems"]
    g, sy, pe, dv, ac = T["g"], T["sy"], T["pe"], T["dv"], T["ac"]

    BK = {
        "A": T["pA"][:, 0:512], "B": T["pB"][:, 0:512],
        "S0a": pS0[:, 0:512], "S0b": pS0[:, 512:1024],
        "S0c": pS0[:, 1024:1536], "S0d": pS0[:, 1536:2048],
        "OTa": pOT[:, 0:512], "OTb": pOT[:, 512:1024],
    }
    BKBF = {
        "A": T["pa_bf"][:, 0:1024], "B": T["pb_bf"][:, 0:1024],
        "S0a": T["ps0_bf"][:, 0:1024], "S0b": T["ps0_bf"][:, 1024:2048],
        "S0c": T["ps0_bf"][:, 2048:3072], "S0d": T["ps0_bf"][:, 3072:4096],
        "OTa": T["pot_bf"][:, 0:1024], "OTb": T["pot_bf"][:, 1024:2048],
    }

    def inc(inst, semname, label=None, n=1):
        inst.then_inc(S[semname], n)
        cnt[semname] = cnt.get(semname, 0) + n
        if label is not None:
            if rec:
                MS[label] = (semname, cnt[semname])
            else:
                assert MS[label] == (semname, cnt[semname]), (
                    label, MS[label], (semname, cnt[semname]))

    def alias(dst_label, src_label):
        if rec:
            MS[dst_label] = MS[src_label]

    def W(eng, label):
        if rec and label not in MS:
            return
        sn, v = MS[label]
        eng.wait_ge(S[sn], v)

    OWN_ = {}

    def bacq(eng, name):
        lb = OWN_.get(name)
        if lb is not None:
            W(eng, lb)

    def brel(name, label):
        OWN_[name] = label

    NLOAD = 16 + 16 + 16 + 16 + NTO

    def x_row_tile(idx):
        if idx < 16:
            return TILES[idx]
        if idx < 32:
            pj, th = divmod(idx - 16, 2)
            return 2 * PROC[pj] + th
        if idx < 48:
            return TILES[16 + (idx - 32)]
        if idx < 64:
            pj, th = divmod(idx - 48, 2)
            return 2 * PROC[8 + pj] + th
        return idx - 64

    def x_consumers(idx):
        if idx < 16:
            return [("p0xn", idx), ("p0sq", idx)]
        if idx < 32:
            pj, th = divmod(idx - 16, 2)
            return [("xnat", pj, th)]
        if idx < 48:
            seq = 16 + (idx - 32)
            return [("p0xn", seq), ("p0sq", seq)]
        if idx < 64:
            pj, th = divmod(idx - 48, 2)
            return [("xnat", 8 + pj, th)]
        return [("fin", idx - 64)]

    x_load_sem = []
    lx = [0, 0]
    for i in range(NLOAD):
        sl = i % 2
        lx[sl] += 16
        x_load_sem.append(("sLX%d" % sl, lx[sl]))

    def emit_xload(idx):
        sl = idx % 2
        if idx >= 2:
            for lb in x_consumers(idx - 2):
                W(g, lb)
        tt = x_row_tile(idx)
        inc(g.dma_start(out=xstg[:, sl * D:(sl + 1) * D],
                        in_=T["xb"][tt * 128:(tt + 1) * 128, :]),
            "sLX%d" % sl, ("xld", idx), 16)

    def emit_gpsimd():
        g.memset(consts[:, 0:1], -EXPS)
        g.memset(consts[:, 1:2], EPS)
        g.memset(onescol[:], 1.0)
        g.memset(onesrow[:], 1.0)
        g.memset(stats[:, 0:2], 0.0)
        for par in range(2):
            for kb in range(32):
                off = R_VB[par] + kb * 65 + 64
                g.memset(AR[:, off: off + 1], 1.0)
        inc(g.memset(st12[:, 0:8], 0.0), "sGP", ("init",))

        inc(g.dma_start(out=identsb[:], in_=T["idin"]), "sID", ("idld",), 16)
        for m, w in enumerate((T["wqT"], T["wkT"], T["wvT"])):
            src = w.rearrange("(kt p) n -> p kt n", p=128)
            dst = AR[:, R_WQKV + m * 8192: R_WQKV + (m + 1) * 8192].rearrange(
                "p (kt n) -> p kt n", kt=8)
            for hvf in range(2):
                inc(g.dma_start(out=dst[:, hvf * 4:(hvf + 1) * 4, :],
                                in_=src[:, hvf * 4:(hvf + 1) * 4, :]),
                    "sLW", None, 16)
        inc(g.dma_start(
            out=wgsb[:].rearrange("p (kt n) -> p kt n", kt=8),
            in_=T["wgT"].rearrange("(kt p) n -> p kt n", p=128)),
            "sLW", ("wld",), 16)

        for idx in range(64):
            emit_xload(idx)
        W(g, ("atrlast",))
        W(g, ("statslast",))
        W(g, ("explast",))

        for Q in range(32):
            e, q = Q // 4, Q % 4
            slot = Q % 2
            if Q >= 2:
                W(g, ("mmlast", Q - 2))
            for kk, (ten, base) in enumerate(((T["winN"], R_WI),
                                             (T["woutT"], R_WO))):
                srcq = ten[e, q * 1024:(q + 1) * 1024, :]
                srcq = srcq.rearrange("(kt p) n -> p kt n", p=128)
                dst = AR[:, base + slot * 8192: base + (slot + 1) * 8192]
                dst = dst.rearrange("p (kt n) -> p kt n", kt=8)
                for hvf in range(2):
                    inc(g.dma_start(out=dst[:, hvf * 4:(hvf + 1) * 4, :],
                                    in_=srcq[:, hvf * 4:(hvf + 1) * 4, :]),
                        "sWI%d" % slot, ("mqld", Q, 2 * kk + hvf), 16)
        for idx in range(64, 64 + NTO):
            emit_xload(idx)

    def act_phase0(seq0):
        for s_ in range(16):
            seq = seq0 + s_
            idx = seq if seq < 16 else 32 + (seq - 16)
            sl = idx % 2
            W(ac, ("xld", idx))
            if seq >= 1:
                W(ac, ("p0var", seq - 1))
            ac.drain()
            inst = ac.activation(pOT[:, 0:1024], xstg[:, sl * D:(sl + 1) * D],
                                 AF.Square, accum_out=gsc[:, 0:1])
            inc(inst, "sAC", ("p0sq", seq))
            W(ac, ("p0var", seq))
            ac.drain()
            ac.activation(gsc[:, 5:6], gsc[:, 4:5], AF.Ln, bias=consts[:, 1:2])
            ac.drain()
            tt = TILES[seq]
            inc(ac.activation(lnr[:, tt:tt + 1], gsc[:, 5:6], AF.Exp,
                              scale=-0.5), "sAC", ("p0rst", seq))

    def emit_act():
        ac.wait_ge(S["sGP"], 1)
        def _eprev(pj, qc, k, back):
            gk = ((pj * 8 + qc) * 32 + k) // 2 - back
            if gk < 0:
                return None
            return (gk // 128, (gk // 16) % 8, (gk % 16) * 2)

        for half in range(2):
            act_phase0(16 * half)
            for pj in range(8 * half, 8 * half + 8):
                for qc in range(8):
                    for k in range(0, 32, 2):
                        W(ac, ("st", pj, qc, k))
                        W(ac, ("st", pj, qc, k + 1))
                        pv = _eprev(pj, qc, k, 2)
                        if pv is not None and pv[0] >= 8 * half:
                            W(ac, ("av", pv[0], pv[1], pv[2] + 1))
                        sl4 = k % 4
                        inst = ac.activation(
                            AR[:, R_PT + ((k // 2) % 2) * 1024:
                               R_PT + ((k // 2) % 2 + 1) * 1024],
                            pS0[:, sl4 * 512: (sl4 + 2) * 512],
                            AF.Exp, bias=consts[:, 0:1])
                        inc(inst, "sAC", ("exp", pj, qc, k))
                    W(ac, ("av", pj, qc, 31))
                    if qc <= 1 and pj > 0:
                        W(ac, ("statsacc", pj - 1, 1))
                    if qc > 1:
                        W(ac, ("atr", pj, qc - 2, 3))
                    ac.drain()
                    inc(ac.copy(AR[0:65, R_OUTS + (qc % 2) * 512:
                                   R_OUTS + (qc % 2) * 512 + 512],
                                pOT[0:65, (qc % 2) * 512:
                                    (qc % 2) * 512 + 512]),
                        "sAC", ("outse", pj, qc))
        alias(("explast",), ("exp", NP - 1, 7, 30))

        W(ac, ("var1",))
        ac.drain()
        ac.activation(st12[0:1, 4:5], st12[0:1, 5:6], AF.Ln,
                      bias=consts[0:1, 1:2])
        ac.drain()
        inc(ac.activation(st12[0:1, 6:7], st12[0:1, 4:5], AF.Exp,
                          scale=-0.5), "sAC", ("rsig",))

        for tt in range(NTO):
            W(ac, ("gm", tt))
            ac.drain()
            ac.activation(gsc[:, 24:32], lgt[:], AF.Exp, bias=gsc[:, 1:2])
            ac.drain()
            inc(ac.activation(gsc[:, 3:4], gsc[:, 2:3], AF.Exp,
                              bias=gsc[:, 1:2]), "sAC", ("ge", tt))

    def emit_pe():
        def pe_qkv_group(pj, gidx):
            j = PROC[pj]
            toff = (256 * j) % 2048
            par = pj % 2
            m, sub = _g_decode(gidx)
            bk = BANKOF[("qkvg", pj, gidx)]
            bacq(pe, bk)
            last = None
            if m < 2:
                for kt in range(8):
                    lhs = AR[:, R_WQKV + m * 8192 + kt * 1024 + sub * 64:
                             R_WQKV + m * 8192 + kt * 1024 + (sub + 1) * 64]
                    rhs = xnTb[:, kt * 2048 + toff: kt * 2048 + toff + 256]
                    last = pe.matmul(BK[bk][0:64, 0:256], lhs, rhs,
                                     start=(kt == 0), stop=(kt == 7))
            else:
                th, dc = sub // 2, sub % 2
                for kt in range(8):
                    lhs = xnTb[:, kt * 2048 + toff + th * 128:
                               kt * 2048 + toff + (th + 1) * 128]
                    rhs = AR[:, R_WQKV + 16384 + kt * 1024 + dc * 512:
                             R_WQKV + 16384 + kt * 1024 + (dc + 1) * 512]
                    last = pe.matmul(BK[bk][:, 0:512], lhs, rhs,
                                     start=(kt == 0), stop=(kt == 7))
            inc(last, "sPE", ("qkv", pj, gidx))
            brel(bk, ("qkve", pj, gidx))

        def av_step(pj, qc, k):
            par = pj % 2
            ot = "OTa" if qc % 2 == 0 else "OTb"
            if k == 0:
                bacq(pe, ot)
            lhs = AR[:, R_VB[par] + k * 65: R_VB[par] + k * 65 + 65]
            rhs = AR[:, R_PT + ((k // 2) % 2) * 1024 + (k % 2) * 512:
                     R_PT + ((k // 2) % 2) * 1024 + (k % 2) * 512 + 512]
            last = pe.matmul(pOT[0:65, (qc % 2) * 512: (qc % 2) * 512 + 512],
                             lhs, rhs, start=(k == 0), stop=(k == 31))
            inc(last, "sPE", ("av", pj, qc, k))
            if k == 31:
                brel(ot, ("outse", pj, qc))

        def pe_phase0(seq0):
            for s_ in range(16):
                seq = seq0 + s_
                idx = seq if seq < 16 else 32 + (seq - 16)
                sl = idx % 2
                W(pe, ("p0xn", seq))
                for jb in range(8):
                    gtc = seq * 8 + jb
                    bk = BANKS4[gtc % 4]
                    bacq(pe, bk)
                    src = AR[:, R_XNAT + sl * 1024 + jb * 128:
                             R_XNAT + sl * 1024 + (jb + 1) * 128]
                    inc(pe.transpose(BKBF[bk][:, 0:128], src, identsb[:]),
                        "sPE", ("p0tr", gtc))
                    brel(bk, ("p0ev", gtc))

        pe.wait_ge(S["sGP"], 1)
        W(pe, ("idld",))
        for half in range(2):
            pe_phase0(16 * half)
            if half == 0:
                W(pe, ("wld",))
            W(pe, ("p0ev", 16 * half * 8 + 127))
            brel("OTa", ("p0sq", 16 * half + 15))
            brel("OTb", ("p0sq", 16 * half + 15))
            first = 8 * half
            for gidx in range(36):
                pe_qkv_group(first, gidx)
            for pj in range(first, first + 8):
                W(pe, ("qkve", pj, 35))
                bg = ([(pj + 1, gg) for gg in range(36)]
                      if pj < first + 7 else [])
                bgi = 0

                def st_step(pj, qc, k, explicit_wait):
                    sl4 = k % 4
                    sb = BANKS4[sl4]
                    if explicit_wait:
                        bacq(pe, sb)
                    par = pj % 2
                    lhs = AR[0:64, R_KT[par] + (k // 2) * 256
                             + (k % 2) * 128:
                             R_KT[par] + (k // 2) * 256 + (k % 2) * 128
                             + 128]
                    rhs = AR[0:64, R_QT[par] + qc * 512:
                             R_QT[par] + qc * 512 + 512]
                    last = pe.matmul(BK[sb][0:128, 0:512], lhs, rhs,
                                     start=True, stop=True)
                    inc(last, "sPE", ("st", pj, qc, k))
                    brel(sb, ("exp", pj, qc, k - (k % 2)))

                for qc in range(8):
                    dealt = 0
                    for k in range(4):
                        st_step(pj, qc, k, True)
                    for k in range(0, 32, 2):
                        if dealt < DEAL_CNT[qc] and bgi < len(bg) \
                                and k in DEAL_SLOTS:
                            pe_qkv_group(*bg[bgi])
                            bgi += 1
                            dealt += 1
                        W(pe, ("exp", pj, qc, k))
                        av_step(pj, qc, k)
                        av_step(pj, qc, k + 1)
                        for kk in (k + 4, k + 5):
                            if kk < 32:
                                st_step(pj, qc, kk, False)
                    for jb in range(4):
                        bk = BANKOF[("atr", pj, qc, jb)]
                        bacq(pe, bk)
                        if jb == 0:
                            W(pe, ("outse", pj, qc))
                        src = AR[0:65, R_OUTS + (qc % 2) * 512 + jb * 128:
                                 R_OUTS + (qc % 2) * 512 + (jb + 1) * 128]
                        inc(pe.transpose(BKBF[bk][:, 0:65], src,
                                         identsb[0:65, 0:65]),
                            "sPE", ("atr", pj, qc, jb))
                        brel(bk, ("ntt", pj, qc, jb))
        alias(("atrlast",), ("atr", NP - 1, 7, 3))

        W(pe, ("statsacc", NP - 1, 1))
        bk = BANKOF[("sum2x",)]
        bacq(pe, bk)
        inc(pe.matmul(BK[bk][0:1, 0:2], onescol[:], stats[:, 0:2],
                      start=True, stop=True), "sPE", ("sum2",))
        brel(bk, ("sum2e",))
        W(pe, ("ab",))
        bk = BANKOF[("bcx",)]
        bacq(pe, bk)
        inc(pe.matmul(BK[bk][0:128, 0:2], onesrow[:], st12[0:1, 6:8],
                      start=True, stop=True), "sPE", ("bc",))
        brel(bk, ("bcev",))

        for tt in range(NTO):
            bk = BANKOF[("glogx", tt)]
            bacq(pe, bk)
            last = None
            for kt in range(8):
                lhs = xnTb[:, kt * 2048 + tt * 128: kt * 2048 + (tt + 1) * 128]
                rhs = wgsb[:, kt * 8:(kt + 1) * 8]
                last = pe.matmul(BK[bk][:, 0:8], lhs, rhs,
                                 start=(kt == 0), stop=(kt == 7))
            inc(last, "sPE", ("glog", tt))
            brel(bk, ("gloge", tt))

        for e in range(NE):
            for q in range(4):
                Q = 4 * e + q
                W(pe, ("mqld", Q, 3))
                slot = Q % 2
                for ic in range(8):
                    for jc in range(2):
                        bk = ("S0a", "S0b")[(Q * 16 + ic * 2 + jc) % 2]
                        bacq(pe, bk)
                        last = None
                        for kt in range(8):
                            lhs = AR[:, R_WI + slot * 8192 + kt * 1024
                                     + ic * 128:
                                     R_WI + slot * 8192 + kt * 1024
                                     + (ic + 1) * 128]
                            rhs = AR[:, R_WO + slot * 8192 + kt * 1024
                                     + jc * 512:
                                     R_WO + slot * 8192 + kt * 1024
                                     + (jc + 1) * 512]
                            last = pe.matmul(BK[bk][:, 0:512], lhs, rhs,
                                             start=(kt == 0), stop=(kt == 7))
                        inc(last, "sPE", ("mm", Q, ic, jc))
                        brel(bk, ("mme", Q, ic, jc))
                alias(("mmlast", Q), ("mm", Q, 7, 1))
            W(pe, ("mme", 4 * e + 3, 7, 1))
            for dc in range(2):
                for tt in range(NTO):
                    bk = _moe_b_bank(e, dc, tt)
                    bacq(pe, bk)
                    last = None
                    for kt in range(8):
                        lhs = xnTb[:, kt * 2048 + tt * 128:
                                   kt * 2048 + (tt + 1) * 128]
                        rhs = AR[:, R_M + (e % 2) * 8192 + kt * 1024
                                 + dc * 512:
                                 R_M + (e % 2) * 8192 + kt * 1024
                                 + (dc + 1) * 512]
                        last = pe.matmul(BK[bk][:, 0:512], lhs, rhs,
                                         start=(kt == 0), stop=(kt == 7))
                    inc(last, "sPE", ("yph", e, dc, tt))
                    brel(bk, ("ybe", e, dc, tt))

    def emit_dve():
        def dve_qkv_evac(pj, gidx):
            par = pj % 2
            m, sub = _g_decode(gidx)
            bk = BANKOF[("qkvg", pj, gidx)]
            if gidx == 0 and pj >= 2:
                W(dv, ("st", pj - 2, 3, 31))
                W(dv, ("av", pj - 2, 3, 31))
            W(dv, ("qkv", pj, gidx))
            if m < 2:
                base = R_QT[par] if m == 0 else R_KT[par]
                inc(dv.tensor_copy(
                    AR[0:64, base + sub * 256: base + (sub + 1) * 256],
                    BK[bk][0:64, 0:256]), "sDV", ("qkve", pj, gidx))
            else:
                th, dc = sub // 2, sub % 2
                last = None
                for gg in range(8):
                    kb = 2 * (dc * 8 + gg) + th
                    off = R_VB[par] + kb * 65
                    last = dv.tensor_copy(AR[:, off: off + 64],
                                          BK[bk][:, gg * 64:(gg + 1) * 64])
                inc(last, "sDV", ("qkve", pj, gidx))

        def dve_phase0(seq0):
            for s_ in range(16):
                seq = seq0 + s_
                idx = seq if seq < 16 else 32 + (seq - 16)
                sl = idx % 2
                tt = TILES[seq]
                W(dv, ("xld", idx))
                if seq >= 2:
                    W(dv, ("p0tr", (seq - 2) * 8 + 7))
                dv.drain()
                dv.reduce_sum(gsc[:, 1:2], xstg[:, sl * D:(sl + 1) * D],
                              axis=AXX)
                dv.drain()
                dv.tensor_scalar(lnm[:, tt:tt + 1], gsc[:, 1:2], -INVD, None,
                                 ALU.mult)
                W(dv, ("p0sq", seq))
                dv.tensor_scalar(gsc[:, 2:3], gsc[:, 0:1], INVD, None,
                                 ALU.mult)
                dv.drain()
                dv.tensor_mul(gsc[:, 3:4], lnm[:, tt:tt + 1],
                              lnm[:, tt:tt + 1])
                dv.drain()
                inc(dv.tensor_tensor(gsc[:, 4:5], gsc[:, 2:3], gsc[:, 3:4],
                                     ALU.subtract), "sDV", ("p0var", seq))
                W(dv, ("p0rst", seq))
                inc(dv.tensor_scalar(
                    AR[:, R_XNAT + sl * 1024: R_XNAT + (sl + 1) * 1024],
                    xstg[:, sl * D:(sl + 1) * D],
                    lnm[:, tt:tt + 1], lnr[:, tt:tt + 1], ALU.add, ALU.mult),
                    "sDV", ("p0xn", seq))
                for jb in range(8):
                    gtc = seq * 8 + jb
                    bk = BANKS4[gtc % 4]
                    W(dv, ("p0tr", gtc))
                    dcol = jb * 2048 + (tt % 16) * 128
                    inc(dv.tensor_copy(xnTb[:, dcol: dcol + 128],
                                       BKBF[bk][:, 0:128]),
                        "sDV", ("p0ev", gtc))

        dv.wait_ge(S["sGP"], 1)
        for half in range(2):
            dve_phase0(16 * half)
            first = 8 * half
            for gidx in range(36):
                dve_qkv_evac(first, gidx)
            for pj in range(first, first + 8):
                j = PROC[pj]
                dv.drain()
                for th in range(2):
                    idx = (16 if half == 0 else 48) + 2 * (pj - first) + th
                    sn, v = x_load_sem[idx]
                    dv.wait_ge(S[sn], v)
                    tt = 2 * j + th
                    slx = idx % 2
                    inc(dv.tensor_scalar(
                        AR[:, R_XNAT + th * 1024: R_XNAT + (th + 1) * 1024],
                        xstg[:, slx * D:(slx + 1) * D],
                        lnm[:, tt:tt + 1], lnr[:, tt:tt + 1],
                        ALU.add, ALU.mult), "sDV", ("xnat", pj, th))
                gbase = 0
                for qc in range(8):
                    if pj < first + 7:
                        for gg in range(gbase, gbase + DEAL_CNT[qc]):
                            if gg < 36:
                                dve_qkv_evac(pj + 1, gg)
                    gbase += DEAL_CNT[qc]
                    for jb in range(4):
                        W(dv, ("atr", pj, qc, jb))
                        bkn = BANKOF[("atr", pj, qc, jb)]
                        gq = qc * 2 + jb // 2
                        hb = jb % 2
                        dv.drain()
                        dv.reciprocal(gsc[:, 6:7], BKBF[bkn][:, 64:65])
                        dv.drain()
                        if j < 8:
                            ydst = yat[:, (2 * j + hb) * 1024 + gq * 64:
                                       (2 * j + hb) * 1024 + gq * 64 + 64]
                        else:
                            ydst = AR[:, R_YSCR + hb * 1024 + gq * 64:
                                      R_YSCR + hb * 1024 + gq * 64 + 64]
                        xn_in = AR[:, R_XNAT + hb * 1024 + gq * 64:
                                   R_XNAT + hb * 1024 + gq * 64 + 64]
                        inc(dv.scalar_tensor_tensor(
                            ydst, BKBF[bkn][:, 0:64], gsc[:, 6:7], xn_in,
                            ALU.mult, ALU.add), "sDV", ("ntt", pj, qc, jb))
                dv.drain()
                for th in range(2):
                    if j < 8:
                        src = yat[:, (2 * j + th) * 1024:
                                  (2 * j + th + 1) * 1024]
                    else:
                        src = AR[:, R_YSCR + th * 1024:
                                 R_YSCR + (th + 1) * 1024]
                    dv.reduce_sum(gsc[:, 7:8], src, axis=AXX)
                    dv.drain()
                    dv.tensor_add(stats[:, 0:1], stats[:, 0:1], gsc[:, 7:8])
                    dv.drain()
                    dv.tensor_mul(AR[:, R_OUTS: R_OUTS + 1024], src, src)
                    dv.drain()
                    dv.reduce_sum(gsc[:, 7:8], AR[:, R_OUTS: R_OUTS + 1024],
                                  axis=AXX)
                    dv.drain()
                    inc(dv.tensor_add(stats[:, 1:2], stats[:, 1:2],
                                      gsc[:, 7:8]),
                        "sDV", ("statsacc", pj, th))
                    dv.drain()
        alias(("statslast",), ("statsacc", NP - 1, 1))

        W(dv, ("sum2",))
        dv.drain()
        inc(dv.tensor_copy(st12[0:1, 0:2], BK[BANKOF[("sum2x",)]][0:1, 0:2]),
            "sDV", ("sum2e",))
        dv.drain()
        dv.tensor_scalar(st12[0:1, 2:3], st12[0:1, 0:1], INVN, None, ALU.mult)
        dv.drain()
        dv.tensor_scalar(st12[0:1, 3:4], st12[0:1, 1:2], INVN, None, ALU.mult)
        dv.drain()
        dv.tensor_mul(st12[0:1, 4:5], st12[0:1, 2:3], st12[0:1, 2:3])
        dv.drain()
        inc(dv.tensor_tensor(st12[0:1, 5:6], st12[0:1, 3:4], st12[0:1, 4:5],
                             ALU.subtract), "sDV", ("var1",))
        W(dv, ("rsig",))
        dv.drain()
        dv.tensor_mul(st12[0:1, 7:8], st12[0:1, 2:3], st12[0:1, 6:7])
        dv.drain()
        inc(dv.tensor_scalar(st12[0:1, 7:8], st12[0:1, 7:8], -1.0, None,
                             ALU.mult), "sDV", ("ab",))
        W(dv, ("bc",))
        dv.drain()
        inc(dv.tensor_copy(stats[:, 2:4], BK[BANKOF[("bcx",)]][:, 0:2]),
            "sDV", ("bcev",))

        for tt in range(NTO):
            W(dv, ("glog", tt))
            dv.drain()
            inc(dv.tensor_copy(lgt[:], BK[BANKOF[("glogx", tt)]][:, 0:8]),
                "sDV", ("gloge", tt))
            dv.drain()
            dv.reduce_max(gsc[:, 0:1], lgt[:], axis=AXX)
            dv.drain()
            dv.tensor_scalar(gsc[:, 1:2], gsc[:, 0:1], -1.0, None, ALU.mult)
            dv.drain()
            dv.tensor_scalar(gsc[:, 8:16], lgt[:], gsc[:, 0:1], None,
                             ALU.is_equal)
            dv.drain()
            dv.scalar_tensor_tensor(gsc[:, 16:24], gsc[:, 8:16], -1e30,
                                    lgt[:], ALU.mult, ALU.add)
            dv.drain()
            inc(dv.reduce_max(gsc[:, 2:3], gsc[:, 16:24], axis=AXX),
                "sDV", ("gm", tt))
            W(dv, ("ge", tt))
            dv.drain()
            dv.tensor_scalar(gsc[:, 4:5], gsc[:, 3:4], 1.0, None, ALU.add)
            dv.drain()
            dv.reciprocal(gsc[:, 5:6], gsc[:, 4:5])
            dv.drain()
            dv.tensor_scalar(gsc[:, 8:16], lgt[:], gsc[:, 2:3], None,
                             ALU.is_ge)
            dv.drain()
            dv.tensor_mul(gsc[:, 24:32], gsc[:, 24:32], gsc[:, 8:16])
            dv.drain()
            inc(dv.tensor_scalar(gts[:, tt * 8:(tt + 1) * 8],
                                 gsc[:, 24:32], gsc[:, 5:6], None, ALU.mult),
                "sDV", ("gate", tt))
            dv.drain()

        for e in range(NE):
            for q in range(4):
                Q = 4 * e + q
                dv.drain()
                for ic in range(8):
                    for jc in range(2):
                        if q == 0 and ic == 0 and jc == 0 and e >= 2:
                            W(dv, ("yph", e - 2, 1, NTO - 1))
                        W(dv, ("mm", Q, ic, jc))
                        bkn = ("S0a", "S0b")[(Q * 16 + ic * 2 + jc) % 2]
                        msl = AR[:, R_M + (e % 2) * 8192 + ic * 1024
                                 + jc * 512:
                                 R_M + (e % 2) * 8192 + ic * 1024
                                 + (jc + 1) * 512]
                        if q == 0:
                            inst = dv.tensor_copy(msl, BK[bkn][:, 0:512])
                        else:
                            inst = dv.tensor_tensor(msl, BK[bkn][:, 0:512],
                                                    msl, ALU.add)
                        inc(inst, "sDV", ("mme", Q, ic, jc))
            dv.drain()
            for dc in range(2):
                for tt in range(NTO):
                    W(dv, ("yph", e, dc, tt))
                    bkn = _moe_b_bank(e, dc, tt)
                    ysl = yffn[:, tt * 1024 + dc * 512:
                               tt * 1024 + (dc + 1) * 512]
                    gcol = gts[:, tt * 8 + e: tt * 8 + e + 1]
                    if e == 0:
                        inst = dv.tensor_scalar(ysl, BK[bkn][:, 0:512], gcol,
                                                None, ALU.mult)
                    else:
                        inst = dv.scalar_tensor_tensor(
                            ysl, BK[bkn][:, 0:512], gcol, ysl,
                            ALU.mult, ALU.add)
                    inc(inst, "sDV", ("ybe", e, dc, tt))

        W(dv, ("yph", NE - 1, 1, NTO - 1))
        dv.drain()
        for tt in range(NTO):
            W(dv, ("ybe", NE - 1, 1, tt))
            idx = 64 + tt
            sn, v = x_load_sem[idx]
            dv.wait_ge(S[sn], v)
            if tt >= 2:
                dv.wait_ge(S["sOUT%d" % (tt % 2)], 16 * (tt // 2))
            sl = idx % 2
            ost = ARF[:, R_OSTG // 2 + (tt % 2) * 1024:
                      R_OSTG // 2 + ((tt % 2) + 1) * 1024]
            dv.drain()
            dv.tensor_scalar(ost, yat[:, tt * 1024:(tt + 1) * 1024],
                             stats[:, 2:3], stats[:, 3:4], ALU.mult, ALU.add)
            dv.drain()
            dv.tensor_add(ost, ost, yffn[:, tt * 1024:(tt + 1) * 1024])
            dv.drain()
            inc(dv.tensor_add(ost, ost, xstg[:, sl * D:(sl + 1) * D]),
                "sDV", ("fin", tt))

    def emit_sync():
        for tt in range(NTO):
            W(sy, ("fin", tt))
            inc(sy.dma_start(
                out=T["out"][tt * 128:(tt + 1) * 128, :],
                in_=ARF[:, R_OSTG // 2 + (tt % 2) * 1024:
                        R_OSTG // 2 + ((tt % 2) + 1) * 1024]),
                "sOUT%d" % (tt % 2), ("outst", tt), 16)
        sy.wait_ge(S["sOUT0"], 16 * (NTO // 2))
        sy.wait_ge(S["sOUT1"], 16 * (NTO // 2))

    emit_gpsimd()
    emit_act()
    emit_pe()
    emit_dve()
    emit_sync()


def _build():
    from contextlib import ExitStack
    nc = bass.Bass("TRN2", target_bir_lowering=False, debug=False)
    dram = dict(
        xb=nc.dram_tensor("xb", [TOK, D], F32, kind="ExternalInput").ap(),
        wqT=nc.dram_tensor("wqT", [D, D], F32, kind="ExternalInput").ap(),
        wkT=nc.dram_tensor("wkT", [D, D], F32, kind="ExternalInput").ap(),
        wvT=nc.dram_tensor("wvT", [D, D], F32, kind="ExternalInput").ap(),
        wgT=nc.dram_tensor("wgT", [D, NE], F32, kind="ExternalInput").ap(),
        winN=nc.dram_tensor("winN", [NE, HID, D], F32,
                            kind="ExternalInput").ap(),
        woutT=nc.dram_tensor("woutT", [NE, HID, D], F32,
                             kind="ExternalInput").ap(),
        idin=nc.dram_tensor("idin", [128, 128], F32,
                            kind="ExternalInput").ap(),
        out=nc.dram_tensor("out", [OWN, D], F32, kind="ExternalOutput").ap(),
    )

    with ExitStack() as _st:
        AR_t = _st.enter_context(nc.sbuf_tensor("AR", [128, AR_N], BF16))
        xnTb = _st.enter_context(nc.sbuf_tensor("xnTb", [128, 8 * 2048], BF16))
        yat = _st.enter_context(nc.sbuf_tensor("yat", [128, NTO * D], BF16))
        yffn = _st.enter_context(nc.sbuf_tensor("yffn", [128, NTO * D], BF16))
        xstg = _st.enter_context(nc.sbuf_tensor("xstg", [128, 2 * D], BF16))
        wgsb = _st.enter_context(nc.sbuf_tensor("wgsb", [128, 8 * NE], BF16))
        identsb = _st.enter_context(
            nc.sbuf_tensor("identsb", [128, 128], BF16))
        onescol = _st.enter_context(nc.sbuf_tensor("onescol", [128, 1], F32))
        onesrow = _st.enter_context(nc.sbuf_tensor("onesrow", [1, 128], F32))
        consts = _st.enter_context(nc.sbuf_tensor("consts", [128, 4], F32))
        stats = _st.enter_context(nc.sbuf_tensor("stats", [128, 8], F32))
        st12 = _st.enter_context(nc.sbuf_tensor("st12", [1, 8], F32))
        lnm = _st.enter_context(nc.sbuf_tensor("lnm", [128, NT], F32))
        lnr = _st.enter_context(nc.sbuf_tensor("lnr", [128, NT], F32))
        gts = _st.enter_context(nc.sbuf_tensor("gts", [128, NTO * NE], F32))
        lgt = _st.enter_context(nc.sbuf_tensor("lgt", [128, NE], F32))
        gsc = _st.enter_context(nc.sbuf_tensor("gsc", [128, 40], F32))
        P_A = _st.enter_context(nc.psum_tensor("P_A", [128, 512], F32))
        P_B = _st.enter_context(nc.psum_tensor("P_B", [128, 512], F32))
        P_S0 = _st.enter_context(nc.psum_tensor("P_S0", [128, 2048], F32))
        P_OT = _st.enter_context(nc.psum_tensor("P_OT", [128, 1024], F32))
        semnames = ("sLX0", "sLX1", "sLW", "sID", "sWI0", "sWI1", "sWO0",
                    "sWO1", "sOUT0", "sOUT1", "sPE", "sDV", "sAC", "sGP")
        sems = {n: _st.enter_context(nc.semaphore(n)) for n in semnames}
        block = _st.enter_context(nc.Block())

        base = dict(dram)
        base.update(
            AR=AR_t.ap(), ARF=AR_t.bitcast(F32).ap(),
            xnTb=xnTb.ap(), yat=yat.ap(), yffn=yffn.ap(), xstg=xstg.ap(),
            wgsb=wgsb.ap(), identsb=identsb.ap(), onescol=onescol.ap(),
            onesrow=onesrow.ap(), consts=consts.ap(), stats=stats.ap(),
            st12=st12.ap(), lnm=lnm.ap(), lnr=lnr.ap(), gts=gts.ap(),
            lgt=lgt.ap(), gsc=gsc.ap(),
            pA=P_A.ap(), pB=P_B.ap(), pS0=P_S0.ap(), pS1=P_S0.ap(),
            pOT=P_OT.ap(),
            pa_bf=P_A.bitcast(BF16).ap(), pb_bf=P_B.bitcast(BF16).ap(),
            ps0_bf=P_S0.bitcast(BF16).ap(), ps1_bf=P_S0.bitcast(BF16).ap(),
            pot_bf=P_OT.bitcast(BF16).ap(),
            sems=sems,
        )

        dry = _Dry()
        MS = {}
        Trec = dict(base, g=dry, sy=dry, pe=dry, dv=dry, ac=dry)
        _emit(Trec, True, MS, {})

        def one(real_name, eng):
            Tx = dict(base, g=dry, sy=dry, pe=dry, dv=dry, ac=dry)
            Tx[real_name] = eng
            _emit(Tx, False, MS, {})

        @block.gpsimd
        def _(eng):
            one("g", eng)

        @block.sync
        def _(eng):
            one("sy", eng)

        @block.tensor
        def _(eng):
            one("pe", eng)

        @block.vector
        def _(eng):
            one("dv", eng)

        @block.scalar
        def _(eng):
            one("ac", eng)

    return nc


def kernel(L0_attn_x, L0_attn_Wq, L0_attn_Wk, L0_attn_Wv,
           L0_ffn_w_gate, L0_ffn_w_in, L0_ffn_w_out):
    x = np.ascontiguousarray(np.asarray(L0_attn_x, dtype=np.float32))
    xf = x.reshape(16384, D)
    wqT = np.ascontiguousarray(np.asarray(L0_attn_Wq, np.float32).T)
    wkT = np.ascontiguousarray(np.asarray(L0_attn_Wk, np.float32).T)
    wvT = np.ascontiguousarray(np.asarray(L0_attn_Wv, np.float32).T)
    wgT = np.ascontiguousarray(np.asarray(L0_ffn_w_gate, np.float32).T)
    winN = np.ascontiguousarray(np.asarray(L0_ffn_w_in, np.float32))
    woutT = np.ascontiguousarray(
        np.asarray(L0_ffn_w_out, np.float32).transpose(0, 2, 1))
    ident = np.eye(128, dtype=np.float32)

    nc = _build()
    in_maps = []
    for c in range(8):
        own = xf[c * 2048:(c + 1) * 2048]
        pc = c ^ 1
        par = xf[pc * 2048:(pc + 1) * 2048]
        xb = np.ascontiguousarray(np.concatenate([own, par], axis=0))
        in_maps.append({
            "xb": xb, "wqT": wqT, "wkT": wkT, "wvT": wvT, "wgT": wgT,
            "winN": winN, "woutT": woutT, "idin": ident,
        })
    res = run_bass_kernel_spmd(nc, in_maps, core_ids=list(range(8)))
    out = np.empty((16384, D), np.float32)
    for c in range(8):
        out[c * 2048:(c + 1) * 2048] = np.asarray(res.results[c]["out"],
                                                  np.float32)
    return out.reshape(4, 4096, D)


# version 08
# speedup vs baseline: 1.0424x; 1.0424x over previous
import numpy as np
import concourse.bass as bass
import concourse.mybir as mybir
from concourse.bass_utils import run_bass_kernel_spmd

F32 = mybir.dt.float32
BF16 = mybir.dt.bfloat16
ALU = mybir.AluOpType
AF = mybir.ActivationFunctionType
AXX = mybir.AxisListType.X

D = 1024
HID = 4096
NE = 8
TOK = 4096
OWN = 2048
NT = 32
NTO = 16
NP = 16
EXPS = 12.0
EPS = 1e-5
INVD = 1.0 / D
INVN = 1.0 / (4096.0 * 1024.0)

R_WQKV = 0
R_QT = (24576, 42016)
R_KT = (28672, 46112)
R_VB = (32768, 50208)
R_PT = 34848
R_OUTS = 36896
R_XNAT = 37920
R_YSCR = 52288
R_OSTG = 34848
R_WI = 0
R_WO = 16384
R_M = 32768
AR_N = 53248

BANKS4 = ("S0a", "S0b", "S1a", "S1b")
PROC = list(range(8, 16)) + list(range(8))
TILES = list(range(16, 32)) + list(range(16))


class _DryInst:
    def then_inc(self, sem, n=1):
        return self


_DRY_INST = _DryInst()


class _Dry:
    def __getattr__(self, name):
        def f(*a, **k):
            return _DRY_INST
        return f


def _g_decode(g):
    if g < 16:
        return 0, g
    if g < 32:
        return 1, g - 16
    return 2, g - 32


def _pref_list(pA, pB):
    return ([(pA, g) for g in range(32)] + [(pB, g) for g in range(32)])


def _ab_schedule():
    seq = []
    for half in (0, 1):
        first = 8 * half
        for pj in (first, first + 1):
            for g in range(36):
                seq.append(("qkvg", pj, g))
        for p in range(4):
            pjA, pjB = first + 2 * p, first + 2 * p + 1
            pref = (_pref_list(pjA + 2, pjB + 2) if p < 3 else [])
            for sq in range(8):
                if p < 3:
                    for ent in pref[sq * 8: sq * 8 + 8]:
                        seq.append(("qkvg",) + ent)
                for jb in range(4):
                    seq.append(("atr", pjA, sq, jb))
                for jb in range(4):
                    seq.append(("atr", pjB, sq, jb))
            if p < 3:
                for pj2 in (pjA + 2, pjB + 2):
                    for g in range(32, 36):
                        seq.append(("qkvg", pj2, g))
    seq.append(("sum2x",))
    seq.append(("bcx",))
    for tt in range(NTO):
        seq.append(("glogx", tt))
    return {k: ("A" if i % 2 == 0 else "B") for i, k in enumerate(seq)}


BANKOF = _ab_schedule()


def _moe_a_bank(i, cb, hh, tc):
    return BANKS4[(i * 64 + cb * 8 + hh * 4 + tc) % 4]


def _moe_b_bank(i, dc, tt):
    return ("OTa", "OTb", "A", "B")[(i * 32 + dc * NTO + tt) % 4]


def _emit(T, rec, MS, cnt):
    AR = T["AR"]
    ARF = T["ARF"]
    xnTb = T["xnTb"]
    yat = T["yat"]
    yffn = T["yffn"]
    xstg = T["xstg"]
    wgsb = T["wgsb"]
    identsb = T["identsb"]
    onescol = T["onescol"]
    onesrow = T["onesrow"]
    consts = T["consts"]
    stats = T["stats"]
    st12 = T["st12"]
    lnm = T["lnm"]
    lnr = T["lnr"]
    gts = T["gts"]
    lgt = T["lgt"]
    gsc = T["gsc"]
    pS0, pS1, pOT = T["pS0"], T["pS1"], T["pOT"]
    S = T["sems"]
    g, sy, pe, dv, ac = T["g"], T["sy"], T["pe"], T["dv"], T["ac"]

    BK = {
        "A": T["pA"][:, 0:512], "B": T["pB"][:, 0:512],
        "S0a": pS0[:, 0:512], "S0b": pS0[:, 512:1024],
        "S1a": pS1[:, 0:512], "S1b": pS1[:, 512:1024],
        "OTa": pOT[:, 0:512], "OTb": pOT[:, 512:1024],
    }
    BKBF = {
        "A": T["pa_bf"][:, 0:1024], "B": T["pb_bf"][:, 0:1024],
        "S0a": T["ps0_bf"][:, 0:1024], "S0b": T["ps0_bf"][:, 1024:2048],
        "S1a": T["ps1_bf"][:, 0:1024], "S1b": T["ps1_bf"][:, 1024:2048],
        "OTa": T["pot_bf"][:, 0:1024], "OTb": T["pot_bf"][:, 1024:2048],
    }

    def inc(inst, semname, label=None, n=1):
        inst.then_inc(S[semname], n)
        cnt[semname] = cnt.get(semname, 0) + n
        if label is not None:
            if rec:
                MS[label] = (semname, cnt[semname])
            else:
                assert MS[label] == (semname, cnt[semname]), (
                    label, MS[label], (semname, cnt[semname]))

    def alias(dst_label, src_label):
        if rec:
            MS[dst_label] = MS[src_label]

    def W(eng, label):
        if rec and label not in MS:
            return
        sn, v = MS[label]
        eng.wait_ge(S[sn], v)

    OWN_ = {}

    def bacq(eng, name):
        lb = OWN_.get(name)
        if lb is not None:
            W(eng, lb)

    def brel(name, label):
        OWN_[name] = label

    NLOAD = 16 + 16 + 16 + 16 + NTO

    def x_row_tile(idx):
        if idx < 16:
            return TILES[idx]
        if idx < 32:
            pj, th = divmod(idx - 16, 2)
            return 2 * PROC[pj] + th
        if idx < 48:
            return TILES[16 + (idx - 32)]
        if idx < 64:
            pj, th = divmod(idx - 48, 2)
            return 2 * PROC[8 + pj] + th
        return idx - 64

    def x_consumers(idx):
        if idx < 16:
            return [("p0xn", idx), ("p0sq", idx)]
        if idx < 32:
            pj, th = divmod(idx - 16, 2)
            return [("xnat", pj, th)]
        if idx < 48:
            seq = 16 + (idx - 32)
            return [("p0xn", seq), ("p0sq", seq)]
        if idx < 64:
            pj, th = divmod(idx - 48, 2)
            return [("xnat", 8 + pj, th)]
        return [("fin", idx - 64)]

    x_load_sem = []
    lx = [0, 0]
    for i in range(NLOAD):
        sl = i % 2
        lx[sl] += 16
        x_load_sem.append(("sLX%d" % sl, lx[sl]))

    def emit_xload(idx):
        sl = idx % 2
        if idx >= 2:
            for lb in x_consumers(idx - 2):
                W(g, lb)
        tt = x_row_tile(idx)
        inc(g.dma_start(out=xstg[:, sl * D:(sl + 1) * D],
                        in_=T["xb"][tt * 128:(tt + 1) * 128, :]),
            "sLX%d" % sl, ("xld", idx), 16)

    def emit_gpsimd():
        g.memset(consts[:, 0:1], -EXPS)
        g.memset(consts[:, 1:2], EPS)
        g.memset(onescol[:], 1.0)
        g.memset(onesrow[:], 1.0)
        g.memset(stats[:, 0:2], 0.0)
        for par in range(2):
            for kb in range(32):
                off = R_VB[par] + kb * 65 + 64
                g.memset(AR[:, off: off + 1], 1.0)
        inc(g.memset(st12[:, 0:8], 0.0), "sGP", ("init",))

        inc(g.dma_start(out=identsb[:], in_=T["idin"]), "sID", ("idld",), 16)
        for m, w in enumerate((T["wqT"], T["wkT"], T["wvT"])):
            src = w.rearrange("(kt p) n -> p kt n", p=128)
            dst = AR[:, R_WQKV + m * 8192: R_WQKV + (m + 1) * 8192].rearrange(
                "p (kt n) -> p kt n", kt=8)
            for hvf in range(2):
                inc(g.dma_start(out=dst[:, hvf * 4:(hvf + 1) * 4, :],
                                in_=src[:, hvf * 4:(hvf + 1) * 4, :]),
                    "sLW", None, 16)
        inc(g.dma_start(
            out=wgsb[:].rearrange("p (kt n) -> p kt n", kt=8),
            in_=T["wgT"].rearrange("(kt p) n -> p kt n", p=128)),
            "sLW", ("wld",), 16)

        for idx in range(64):
            emit_xload(idx)
        W(g, ("atrlast",))
        W(g, ("statslast",))
        W(g, ("explast",))

        for Q in range(32):
            e, q = Q // 4, Q % 4
            slot = Q % 2
            if Q >= 2:
                W(g, ("mmlast", Q - 2))
            for kk, (ten, base) in enumerate(((T["winN"], R_WI),
                                             (T["woutT"], R_WO))):
                srcq = ten[e, q * 1024:(q + 1) * 1024, :]
                srcq = srcq.rearrange("(kt p) n -> p kt n", p=128)
                dst = AR[:, base + slot * 8192: base + (slot + 1) * 8192]
                dst = dst.rearrange("p (kt n) -> p kt n", kt=8)
                for hvf in range(2):
                    inc(g.dma_start(out=dst[:, hvf * 4:(hvf + 1) * 4, :],
                                    in_=srcq[:, hvf * 4:(hvf + 1) * 4, :]),
                        "sWI%d" % slot, ("mqld", Q, 2 * kk + hvf), 16)
        for idx in range(64, 64 + NTO):
            emit_xload(idx)

    def act_phase0(seq0):
        for s_ in range(16):
            seq = seq0 + s_
            idx = seq if seq < 16 else 32 + (seq - 16)
            sl = idx % 2
            W(ac, ("xld", idx))
            if seq >= 1:
                W(ac, ("p0var", seq - 1))
            ac.drain()
            inst = ac.activation(pOT[:, 0:1024], xstg[:, sl * D:(sl + 1) * D],
                                 AF.Square, accum_out=gsc[:, 0:1])
            inc(inst, "sAC", ("p0sq", seq))
            W(ac, ("p0var", seq))
            ac.drain()
            ac.activation(gsc[:, 5:6], gsc[:, 4:5], AF.Ln, bias=consts[:, 1:2])
            ac.drain()
            tt = TILES[seq]
            inc(ac.activation(lnr[:, tt:tt + 1], gsc[:, 5:6], AF.Exp,
                              scale=-0.5), "sAC", ("p0rst", seq))

    def emit_act():
        ac.wait_ge(S["sGP"], 1)
        for half in range(2):
            if half == 1:
                W(ac, ("outse", 6, 7))
                W(ac, ("outse", 7, 7))
            act_phase0(16 * half)
            first = 8 * half
            for p in range(4):
                pjA, pjB = first + 2 * p, first + 2 * p + 1
                for sq in range(8):
                    for kp in range(16):
                        for xi, pj in enumerate((pjA, pjB)):
                            k0 = 2 * kp
                            W(ac, ("st", pj, sq, k0))
                            W(ac, ("st", pj, sq, k0 + 1))
                            if k0 >= 2:
                                W(ac, ("av", pj, sq, k0 - 1))
                            elif sq > 0:
                                W(ac, ("av", pj, sq - 1, 31))
                            elif pj >= 2:
                                W(ac, ("av", pj - 2, 7, 31))
                            stb = pS0 if xi == 0 else pS1
                            inst = ac.activation(
                                AR[:, R_PT + xi * 1024:
                                   R_PT + (xi + 1) * 1024],
                                stb[:, 0:1024], AF.Exp, bias=consts[:, 0:1])
                            inc(inst, "sAC", ("exp", pj, sq, k0))
                    for xi, pj in enumerate((pjA, pjB)):
                        W(ac, ("av", pj, sq, 31))
                        if sq > 0:
                            W(ac, ("atr", pj, sq - 1, 3))
                        ac.drain()
                        inc(ac.copy(
                            AR[0:65, R_OUTS + xi * 512:
                               R_OUTS + xi * 512 + 512],
                            pOT[0:65, xi * 512: xi * 512 + 512]),
                            "sAC", ("outse", pj, sq))
        alias(("explast",), ("exp", NP - 1, 7, 30))

        W(ac, ("var1",))
        ac.drain()
        ac.activation(st12[0:1, 4:5], st12[0:1, 5:6], AF.Ln,
                      bias=consts[0:1, 1:2])
        ac.drain()
        inc(ac.activation(st12[0:1, 6:7], st12[0:1, 4:5], AF.Exp,
                          scale=-0.5), "sAC", ("rsig",))

        for tt in range(NTO):
            W(ac, ("gm", tt))
            ac.drain()
            ac.activation(gsc[:, 24:32], lgt[:], AF.Exp, bias=gsc[:, 1:2])
            ac.drain()
            inc(ac.activation(gsc[:, 3:4], gsc[:, 2:3], AF.Exp,
                              bias=gsc[:, 1:2]), "sAC", ("ge", tt))

    def emit_pe():
        def pe_qkv_group(pj, gidx, rh):
            j = PROC[pj]
            toff = (256 * j) % 2048
            m, sub = _g_decode(gidx)
            bk = BANKOF[("qkvg", pj, gidx)]
            bacq(pe, bk)
            last = None
            if m < 2:
                tp = (0, 64) if rh else None
                for kt in range(8):
                    lhs = AR[:, R_WQKV + m * 8192 + kt * 1024 + sub * 64:
                             R_WQKV + m * 8192 + kt * 1024 + (sub + 1) * 64]
                    rhs = xnTb[:, kt * 2048 + toff: kt * 2048 + toff + 256]
                    last = pe.matmul(BK[bk][rh:rh + 64, 0:256], lhs, rhs,
                                     start=(kt == 0), stop=(kt == 7),
                                     tile_position=tp)
            else:
                th, dc = sub // 2, sub % 2
                for kt in range(8):
                    lhs = xnTb[:, kt * 2048 + toff + th * 128:
                               kt * 2048 + toff + (th + 1) * 128]
                    rhs = AR[:, R_WQKV + 16384 + kt * 1024 + dc * 512:
                             R_WQKV + 16384 + kt * 1024 + (dc + 1) * 512]
                    last = pe.matmul(BK[bk][:, 0:512], lhs, rhs,
                                     start=(kt == 0), stop=(kt == 7))
            inc(last, "sPE", ("qkv", pj, gidx))
            brel(bk, ("qkve", pj, gidx))

        def av_step(pj, xi, sq, k):
            par = pj % 2
            ot = "OTa" if xi == 0 else "OTb"
            W(pe, ("exp", pj, sq, k - (k % 2)))
            if k == 0:
                bacq(pe, ot)
            lhs = AR[:, R_VB[par] + k * 65: R_VB[par] + k * 65 + 65]
            rhs = AR[:, R_PT + xi * 1024 + (k % 2) * 512:
                     R_PT + xi * 1024 + (k % 2) * 512 + 512]
            last = pe.matmul(pOT[0:65, xi * 512: xi * 512 + 512], lhs, rhs,
                             start=(k == 0), stop=(k == 31))
            inc(last, "sPE", ("av", pj, sq, k))
            if k == 31:
                brel(ot, ("outse", pj, sq))

        def st_step(pj, xi, sq, k, rh):
            par = pj % 2
            sub = ("S0a", "S0b") if xi == 0 else ("S1a", "S1b")
            sb = sub[k % 2]
            bacq(pe, sb)
            stb = pS0 if xi == 0 else pS1
            lhs = AR[rh:rh + 64, R_KT[par] + (k // 2) * 256 + (k % 2) * 128:
                     R_KT[par] + (k // 2) * 256 + (k % 2) * 128 + 128]
            rhs = AR[rh:rh + 64, R_QT[par] + sq * 512:
                     R_QT[par] + sq * 512 + 512]
            last = pe.matmul(stb[0:128, (k % 2) * 512: (k % 2) * 512 + 512],
                             lhs, rhs, start=True, stop=True)
            inc(last, "sPE", ("st", pj, sq, k))
            brel(sb, ("exp", pj, sq, k - (k % 2)))

        def pe_phase0(seq0):
            for s_ in range(16):
                seq = seq0 + s_
                idx = seq if seq < 16 else 32 + (seq - 16)
                sl = idx % 2
                W(pe, ("p0xn", seq))
                for jb in range(8):
                    gtc = seq * 8 + jb
                    bk = BANKS4[gtc % 4]
                    bacq(pe, bk)
                    src = AR[:, R_XNAT + sl * 1024 + jb * 128:
                             R_XNAT + sl * 1024 + (jb + 1) * 128]
                    inc(pe.transpose(BKBF[bk][:, 0:128], src, identsb[:]),
                        "sPE", ("p0tr", gtc))
                    brel(bk, ("p0ev", gtc))

        pe.wait_ge(S["sGP"], 1)
        W(pe, ("idld",))
        for half in range(2):
            pe_phase0(16 * half)
            if half == 0:
                W(pe, ("wld",))
            W(pe, ("p0ev", 16 * half * 8 + 127))
            brel("OTa", ("p0sq", 16 * half + 15))
            brel("OTb", ("p0sq", 16 * half + 15))
            first = 8 * half
            for pj in (first, first + 1):
                for gidx in range(36):
                    pe_qkv_group(pj, gidx, 0)
            for p in range(4):
                pjA, pjB = first + 2 * p, first + 2 * p + 1
                rh = (p % 2) * 64
                rh2 = 64 - rh
                W(pe, ("qkve", pjA, 35))
                W(pe, ("qkve", pjB, 35))
                pref = (_pref_list(pjA + 2, pjB + 2) if p < 3 else [])
                for sq in range(8):
                    deals = pref[sq * 8: sq * 8 + 8]
                    di = 0
                    for kp in range(16):
                        if kp % 2 == 1 and di < len(deals):
                            pe_qkv_group(deals[di][0], deals[di][1], rh2)
                            di += 1
                        k0 = 2 * kp
                        for xi, pj in enumerate((pjA, pjB)):
                            if k0 >= 2:
                                av_step(pj, xi, sq, k0 - 2)
                                av_step(pj, xi, sq, k0 - 1)
                            st_step(pj, xi, sq, k0, rh)
                            st_step(pj, xi, sq, k0 + 1, rh)
                    for xi, pj in enumerate((pjA, pjB)):
                        av_step(pj, xi, sq, 30)
                        av_step(pj, xi, sq, 31)
                    for xi, pj in enumerate((pjA, pjB)):
                        for jb in range(4):
                            bk = BANKOF[("atr", pj, sq, jb)]
                            bacq(pe, bk)
                            if jb == 0:
                                W(pe, ("outse", pj, sq))
                            srcv = AR[0:65, R_OUTS + xi * 512 + jb * 128:
                                      R_OUTS + xi * 512 + (jb + 1) * 128]
                            inc(pe.transpose(BKBF[bk][:, 0:65], srcv,
                                             identsb[0:65, 0:65]),
                                "sPE", ("atr", pj, sq, jb))
                            brel(bk, ("ntt", pj, sq, jb))
                if p < 3:
                    for pj2 in (pjA + 2, pjB + 2):
                        for g in range(32, 36):
                            pe_qkv_group(pj2, g, 0)
        alias(("atrlast",), ("atr", NP - 1, 7, 3))

        W(pe, ("statsall",))
        bk = BANKOF[("sum2x",)]
        bacq(pe, bk)
        inc(pe.matmul(BK[bk][0:1, 0:2], onescol[:], stats[:, 0:2],
                      start=True, stop=True), "sPE", ("sum2",))
        brel(bk, ("sum2e",))
        W(pe, ("ab",))
        bk = BANKOF[("bcx",)]
        bacq(pe, bk)
        inc(pe.matmul(BK[bk][0:128, 0:2], onesrow[:], st12[0:1, 6:8],
                      start=True, stop=True), "sPE", ("bc",))
        brel(bk, ("bcev",))

        for tt in range(NTO):
            bk = BANKOF[("glogx", tt)]
            bacq(pe, bk)
            last = None
            for kt in range(8):
                lhs = xnTb[:, kt * 2048 + tt * 128: kt * 2048 + (tt + 1) * 128]
                rhs = wgsb[:, kt * 8:(kt + 1) * 8]
                last = pe.matmul(BK[bk][:, 0:8], lhs, rhs,
                                 start=(kt == 0), stop=(kt == 7))
            inc(last, "sPE", ("glog", tt))
            brel(bk, ("gloge", tt))

        for e in range(NE):
            for q in range(4):
                Q = 4 * e + q
                W(pe, ("mqld", Q, 3))
                slot = Q % 2
                for ic in range(8):
                    for jc in range(2):
                        bk = ("S0a", "S0b")[(Q * 16 + ic * 2 + jc) % 2]
                        bacq(pe, bk)
                        last = None
                        for kt in range(8):
                            lhs = AR[:, R_WI + slot * 8192 + kt * 1024
                                     + ic * 128:
                                     R_WI + slot * 8192 + kt * 1024
                                     + (ic + 1) * 128]
                            rhs = AR[:, R_WO + slot * 8192 + kt * 1024
                                     + jc * 512:
                                     R_WO + slot * 8192 + kt * 1024
                                     + (jc + 1) * 512]
                            last = pe.matmul(BK[bk][:, 0:512], lhs, rhs,
                                             start=(kt == 0), stop=(kt == 7))
                        inc(last, "sPE", ("mm", Q, ic, jc))
                        brel(bk, ("mme", Q, ic, jc))
                alias(("mmlast", Q), ("mm", Q, 7, 1))
            W(pe, ("mme", 4 * e + 3, 7, 1))
            for dc in range(2):
                for tt in range(NTO):
                    bk = _moe_b_bank(e, dc, tt)
                    bacq(pe, bk)
                    last = None
                    for kt in range(8):
                        lhs = xnTb[:, kt * 2048 + tt * 128:
                                   kt * 2048 + (tt + 1) * 128]
                        rhs = AR[:, R_M + (e % 2) * 8192 + kt * 1024
                                 + dc * 512:
                                 R_M + (e % 2) * 8192 + kt * 1024
                                 + (dc + 1) * 512]
                        last = pe.matmul(BK[bk][:, 0:512], lhs, rhs,
                                         start=(kt == 0), stop=(kt == 7))
                    inc(last, "sPE", ("yph", e, dc, tt))
                    brel(bk, ("ybe", e, dc, tt))

    def emit_dve():
        def dve_qkv_evac(pj, gidx):
            par = pj % 2
            rh = ((pj // 2) % 2) * 64
            m, sub = _g_decode(gidx)
            bk = BANKOF[("qkvg", pj, gidx)]
            if gidx == 0 and pj >= 4:
                W(dv, ("st", pj - 4, 7, 31))
            if gidx == 32 and pj >= 2:
                W(dv, ("av", pj - 2, 7, 31))
            W(dv, ("qkv", pj, gidx))
            if m < 2:
                base = R_QT[par] if m == 0 else R_KT[par]
                inc(dv.tensor_copy(
                    AR[rh:rh + 64, base + sub * 256: base + (sub + 1) * 256],
                    BK[bk][rh:rh + 64, 0:256]), "sDV", ("qkve", pj, gidx))
            else:
                th, dc = sub // 2, sub % 2
                last = None
                for gg in range(8):
                    kb = 2 * (dc * 8 + gg) + th
                    off = R_VB[par] + kb * 65
                    last = dv.tensor_copy(AR[:, off: off + 64],
                                          BK[bk][:, gg * 64:(gg + 1) * 64])
                inc(last, "sDV", ("qkve", pj, gidx))

        def dve_phase0(seq0):
            for s_ in range(16):
                seq = seq0 + s_
                idx = seq if seq < 16 else 32 + (seq - 16)
                sl = idx % 2
                tt = TILES[seq]
                W(dv, ("xld", idx))
                if seq >= 2:
                    W(dv, ("p0tr", (seq - 2) * 8 + 7))
                dv.drain()
                dv.reduce_sum(gsc[:, 1:2], xstg[:, sl * D:(sl + 1) * D],
                              axis=AXX)
                dv.drain()
                dv.tensor_scalar(lnm[:, tt:tt + 1], gsc[:, 1:2], -INVD, None,
                                 ALU.mult)
                W(dv, ("p0sq", seq))
                dv.tensor_scalar(gsc[:, 2:3], gsc[:, 0:1], INVD, None,
                                 ALU.mult)
                dv.drain()
                dv.tensor_mul(gsc[:, 3:4], lnm[:, tt:tt + 1],
                              lnm[:, tt:tt + 1])
                dv.drain()
                inc(dv.tensor_tensor(gsc[:, 4:5], gsc[:, 2:3], gsc[:, 3:4],
                                     ALU.subtract), "sDV", ("p0var", seq))
                W(dv, ("p0rst", seq))
                inc(dv.tensor_scalar(
                    AR[:, R_XNAT + sl * 1024: R_XNAT + (sl + 1) * 1024],
                    xstg[:, sl * D:(sl + 1) * D],
                    lnm[:, tt:tt + 1], lnr[:, tt:tt + 1], ALU.add, ALU.mult),
                    "sDV", ("p0xn", seq))
                for jb in range(8):
                    gtc = seq * 8 + jb
                    bk = BANKS4[gtc % 4]
                    W(dv, ("p0tr", gtc))
                    dcol = jb * 2048 + (tt % 16) * 128
                    inc(dv.tensor_copy(xnTb[:, dcol: dcol + 128],
                                       BKBF[bk][:, 0:128]),
                        "sDV", ("p0ev", gtc))

        dv.wait_ge(S["sGP"], 1)
        def stats_part(pj, xi, sq):
            j = PROC[pj]
            dv.drain()
            for th in range(2):
                if j < 8:
                    srcv = yat[:, (2 * j + th) * 1024 + sq * 128:
                               (2 * j + th) * 1024 + (sq + 1) * 128]
                else:
                    srcv = AR[:, R_YSCR + xi * 256 + th * 128:
                              R_YSCR + xi * 256 + (th + 1) * 128]
                dv.reduce_sum(gsc[:, 7:8], srcv, axis=AXX)
                dv.drain()
                dv.tensor_add(stats[:, 0:1], stats[:, 0:1], gsc[:, 7:8])
                dv.drain()
                dv.tensor_mul(AR[:, R_YSCR + 512: R_YSCR + 640], srcv, srcv)
                dv.drain()
                dv.reduce_sum(gsc[:, 7:8],
                              AR[:, R_YSCR + 512: R_YSCR + 640], axis=AXX)
                dv.drain()
                inc(dv.tensor_add(stats[:, 1:2], stats[:, 1:2],
                                  gsc[:, 7:8]),
                    "sDV", ("statsacc", pj, sq, th))
                dv.drain()

        for half in range(2):
            dve_phase0(16 * half)
            first = 8 * half
            for pj in (first, first + 1):
                for gidx in range(36):
                    dve_qkv_evac(pj, gidx)
            for p in range(4):
                pjA, pjB = first + 2 * p, first + 2 * p + 1
                pref = (_pref_list(pjA + 2, pjB + 2) if p < 3 else [])
                dv.drain()
                for xi, pj in enumerate((pjA, pjB)):
                    j = PROC[pj]
                    for th in range(2):
                        idx = (16 if half == 0 else 48) \
                            + 2 * (pj - first) + th
                        sn, v = x_load_sem[idx]
                        dv.wait_ge(S[sn], v)
                        tt = 2 * j + th
                        slx = idx % 2
                        inc(dv.tensor_scalar(
                            AR[:, R_XNAT + (pj % 2) * 2048 + th * 1024:
                               R_XNAT + (pj % 2) * 2048 + (th + 1) * 1024],
                            xstg[:, slx * D:(slx + 1) * D],
                            lnm[:, tt:tt + 1], lnr[:, tt:tt + 1],
                            ALU.add, ALU.mult), "sDV", ("xnat", pj, th))
                for sq in range(8):
                    if p < 3:
                        for ent in pref[sq * 8: sq * 8 + 8]:
                            dve_qkv_evac(ent[0], ent[1])
                    for xi, pj in enumerate((pjA, pjB)):
                        j = PROC[pj]
                        for jb in range(4):
                            W(dv, ("atr", pj, sq, jb))
                            bkn = BANKOF[("atr", pj, sq, jb)]
                            gq = sq * 2 + jb // 2
                            hb = jb % 2
                            dv.drain()
                            dv.reciprocal(gsc[:, 6:7], BKBF[bkn][:, 64:65])
                            dv.drain()
                            if j < 8:
                                ydst = yat[:, (2 * j + hb) * 1024 + gq * 64:
                                           (2 * j + hb) * 1024 + gq * 64
                                           + 64]
                            else:
                                ydst = AR[:, R_YSCR + xi * 256 + hb * 128
                                          + (jb // 2) * 64:
                                          R_YSCR + xi * 256 + hb * 128
                                          + (jb // 2) * 64 + 64]
                            xn_in = AR[:, R_XNAT + (pj % 2) * 2048
                                       + hb * 1024 + gq * 64:
                                       R_XNAT + (pj % 2) * 2048 + hb * 1024
                                       + gq * 64 + 64]
                            inc(dv.scalar_tensor_tensor(
                                ydst, BKBF[bkn][:, 0:64], gsc[:, 6:7],
                                xn_in, ALU.mult, ALU.add), "sDV",
                                ("ntt", pj, sq, jb))
                        stats_part(pj, xi, sq)
                if p < 3:
                    for pj2 in (pjA + 2, pjB + 2):
                        for g in range(32, 36):
                            dve_qkv_evac(pj2, g)
        alias(("statsall",), ("statsacc", NP - 1, 7, 1))
        alias(("statslast",), ("statsacc", NP - 1, 7, 1))

        W(dv, ("sum2",))
        dv.drain()
        inc(dv.tensor_copy(st12[0:1, 0:2], BK[BANKOF[("sum2x",)]][0:1, 0:2]),
            "sDV", ("sum2e",))
        dv.drain()
        dv.tensor_scalar(st12[0:1, 2:3], st12[0:1, 0:1], INVN, None, ALU.mult)
        dv.drain()
        dv.tensor_scalar(st12[0:1, 3:4], st12[0:1, 1:2], INVN, None, ALU.mult)
        dv.drain()
        dv.tensor_mul(st12[0:1, 4:5], st12[0:1, 2:3], st12[0:1, 2:3])
        dv.drain()
        inc(dv.tensor_tensor(st12[0:1, 5:6], st12[0:1, 3:4], st12[0:1, 4:5],
                             ALU.subtract), "sDV", ("var1",))
        W(dv, ("rsig",))
        dv.drain()
        dv.tensor_mul(st12[0:1, 7:8], st12[0:1, 2:3], st12[0:1, 6:7])
        dv.drain()
        inc(dv.tensor_scalar(st12[0:1, 7:8], st12[0:1, 7:8], -1.0, None,
                             ALU.mult), "sDV", ("ab",))
        W(dv, ("bc",))
        dv.drain()
        inc(dv.tensor_copy(stats[:, 2:4], BK[BANKOF[("bcx",)]][:, 0:2]),
            "sDV", ("bcev",))

        for tt in range(NTO):
            W(dv, ("glog", tt))
            dv.drain()
            inc(dv.tensor_copy(lgt[:], BK[BANKOF[("glogx", tt)]][:, 0:8]),
                "sDV", ("gloge", tt))
            dv.drain()
            dv.reduce_max(gsc[:, 0:1], lgt[:], axis=AXX)
            dv.drain()
            dv.tensor_scalar(gsc[:, 1:2], gsc[:, 0:1], -1.0, None, ALU.mult)
            dv.drain()
            dv.tensor_scalar(gsc[:, 8:16], lgt[:], gsc[:, 0:1], None,
                             ALU.is_equal)
            dv.drain()
            dv.scalar_tensor_tensor(gsc[:, 16:24], gsc[:, 8:16], -1e30,
                                    lgt[:], ALU.mult, ALU.add)
            dv.drain()
            inc(dv.reduce_max(gsc[:, 2:3], gsc[:, 16:24], axis=AXX),
                "sDV", ("gm", tt))
            W(dv, ("ge", tt))
            dv.drain()
            dv.tensor_scalar(gsc[:, 4:5], gsc[:, 3:4], 1.0, None, ALU.add)
            dv.drain()
            dv.reciprocal(gsc[:, 5:6], gsc[:, 4:5])
            dv.drain()
            dv.tensor_scalar(gsc[:, 8:16], lgt[:], gsc[:, 2:3], None,
                             ALU.is_ge)
            dv.drain()
            dv.tensor_mul(gsc[:, 24:32], gsc[:, 24:32], gsc[:, 8:16])
            dv.drain()
            inc(dv.tensor_scalar(gts[:, tt * 8:(tt + 1) * 8],
                                 gsc[:, 24:32], gsc[:, 5:6], None, ALU.mult),
                "sDV", ("gate", tt))
            dv.drain()

        for e in range(NE):
            for q in range(4):
                Q = 4 * e + q
                dv.drain()
                for ic in range(8):
                    for jc in range(2):
                        if q == 0 and ic == 0 and jc == 0 and e >= 2:
                            W(dv, ("yph", e - 2, 1, NTO - 1))
                        W(dv, ("mm", Q, ic, jc))
                        bkn = ("S0a", "S0b")[(Q * 16 + ic * 2 + jc) % 2]
                        msl = AR[:, R_M + (e % 2) * 8192 + ic * 1024
                                 + jc * 512:
                                 R_M + (e % 2) * 8192 + ic * 1024
                                 + (jc + 1) * 512]
                        if q == 0:
                            inst = dv.tensor_copy(msl, BK[bkn][:, 0:512])
                        else:
                            inst = dv.tensor_tensor(msl, BK[bkn][:, 0:512],
                                                    msl, ALU.add)
                        inc(inst, "sDV", ("mme", Q, ic, jc))
            dv.drain()
            for dc in range(2):
                for tt in range(NTO):
                    W(dv, ("yph", e, dc, tt))
                    bkn = _moe_b_bank(e, dc, tt)
                    ysl = yffn[:, tt * 1024 + dc * 512:
                               tt * 1024 + (dc + 1) * 512]
                    gcol = gts[:, tt * 8 + e: tt * 8 + e + 1]
                    if e == 0:
                        inst = dv.tensor_scalar(ysl, BK[bkn][:, 0:512], gcol,
                                                None, ALU.mult)
                    else:
                        inst = dv.scalar_tensor_tensor(
                            ysl, BK[bkn][:, 0:512], gcol, ysl,
                            ALU.mult, ALU.add)
                    inc(inst, "sDV", ("ybe", e, dc, tt))

        W(dv, ("yph", NE - 1, 1, NTO - 1))
        dv.drain()
        for tt in range(NTO):
            W(dv, ("ybe", NE - 1, 1, tt))
            idx = 64 + tt
            sn, v = x_load_sem[idx]
            dv.wait_ge(S[sn], v)
            if tt >= 2:
                dv.wait_ge(S["sOUT%d" % (tt % 2)], 16 * (tt // 2))
            sl = idx % 2
            ost = ARF[:, R_OSTG // 2 + (tt % 2) * 1024:
                      R_OSTG // 2 + ((tt % 2) + 1) * 1024]
            dv.drain()
            dv.tensor_scalar(ost, yat[:, tt * 1024:(tt + 1) * 1024],
                             stats[:, 2:3], stats[:, 3:4], ALU.mult, ALU.add)
            dv.drain()
            dv.tensor_add(ost, ost, yffn[:, tt * 1024:(tt + 1) * 1024])
            dv.drain()
            inc(dv.tensor_add(ost, ost, xstg[:, sl * D:(sl + 1) * D]),
                "sDV", ("fin", tt))

    def emit_sync():
        for tt in range(NTO):
            W(sy, ("fin", tt))
            inc(sy.dma_start(
                out=T["out"][tt * 128:(tt + 1) * 128, :],
                in_=ARF[:, R_OSTG // 2 + (tt % 2) * 1024:
                        R_OSTG // 2 + ((tt % 2) + 1) * 1024]),
                "sOUT%d" % (tt % 2), ("outst", tt), 16)
        sy.wait_ge(S["sOUT0"], 16 * (NTO // 2))
        sy.wait_ge(S["sOUT1"], 16 * (NTO // 2))

    emit_gpsimd()
    emit_act()
    emit_pe()
    emit_dve()
    emit_sync()


def _build():
    from contextlib import ExitStack
    nc = bass.Bass("TRN2", target_bir_lowering=False, debug=False)
    dram = dict(
        xb=nc.dram_tensor("xb", [TOK, D], F32, kind="ExternalInput").ap(),
        wqT=nc.dram_tensor("wqT", [D, D], F32, kind="ExternalInput").ap(),
        wkT=nc.dram_tensor("wkT", [D, D], F32, kind="ExternalInput").ap(),
        wvT=nc.dram_tensor("wvT", [D, D], F32, kind="ExternalInput").ap(),
        wgT=nc.dram_tensor("wgT", [D, NE], F32, kind="ExternalInput").ap(),
        winN=nc.dram_tensor("winN", [NE, HID, D], F32,
                            kind="ExternalInput").ap(),
        woutT=nc.dram_tensor("woutT", [NE, HID, D], F32,
                             kind="ExternalInput").ap(),
        idin=nc.dram_tensor("idin", [128, 128], F32,
                            kind="ExternalInput").ap(),
        out=nc.dram_tensor("out", [OWN, D], F32, kind="ExternalOutput").ap(),
    )

    with ExitStack() as _st:
        AR_t = _st.enter_context(nc.sbuf_tensor("AR", [128, AR_N], BF16))
        xnTb = _st.enter_context(nc.sbuf_tensor("xnTb", [128, 8 * 2048], BF16))
        yat = _st.enter_context(nc.sbuf_tensor("yat", [128, NTO * D], BF16))
        yffn = _st.enter_context(nc.sbuf_tensor("yffn", [128, NTO * D], BF16))
        xstg = _st.enter_context(nc.sbuf_tensor("xstg", [128, 2 * D], BF16))
        wgsb = _st.enter_context(nc.sbuf_tensor("wgsb", [128, 8 * NE], BF16))
        identsb = _st.enter_context(
            nc.sbuf_tensor("identsb", [128, 128], BF16))
        onescol = _st.enter_context(nc.sbuf_tensor("onescol", [128, 1], F32))
        onesrow = _st.enter_context(nc.sbuf_tensor("onesrow", [1, 128], F32))
        consts = _st.enter_context(nc.sbuf_tensor("consts", [128, 4], F32))
        stats = _st.enter_context(nc.sbuf_tensor("stats", [128, 8], F32))
        st12 = _st.enter_context(nc.sbuf_tensor("st12", [1, 8], F32))
        lnm = _st.enter_context(nc.sbuf_tensor("lnm", [128, NT], F32))
        lnr = _st.enter_context(nc.sbuf_tensor("lnr", [128, NT], F32))
        gts = _st.enter_context(nc.sbuf_tensor("gts", [128, NTO * NE], F32))
        lgt = _st.enter_context(nc.sbuf_tensor("lgt", [128, NE], F32))
        gsc = _st.enter_context(nc.sbuf_tensor("gsc", [128, 40], F32))
        P_A = _st.enter_context(nc.psum_tensor("P_A", [128, 512], F32))
        P_B = _st.enter_context(nc.psum_tensor("P_B", [128, 512], F32))
        P_S0 = _st.enter_context(nc.psum_tensor("P_S0", [128, 1024], F32))
        P_S1 = _st.enter_context(nc.psum_tensor("P_S1", [128, 1024], F32))
        P_OT = _st.enter_context(nc.psum_tensor("P_OT", [128, 1024], F32))
        semnames = ("sLX0", "sLX1", "sLW", "sID", "sWI0", "sWI1", "sWO0",
                    "sWO1", "sOUT0", "sOUT1", "sPE", "sDV", "sAC", "sGP")
        sems = {n: _st.enter_context(nc.semaphore(n)) for n in semnames}
        block = _st.enter_context(nc.Block())

        base = dict(dram)
        base.update(
            AR=AR_t.ap(), ARF=AR_t.bitcast(F32).ap(),
            xnTb=xnTb.ap(), yat=yat.ap(), yffn=yffn.ap(), xstg=xstg.ap(),
            wgsb=wgsb.ap(), identsb=identsb.ap(), onescol=onescol.ap(),
            onesrow=onesrow.ap(), consts=consts.ap(), stats=stats.ap(),
            st12=st12.ap(), lnm=lnm.ap(), lnr=lnr.ap(), gts=gts.ap(),
            lgt=lgt.ap(), gsc=gsc.ap(),
            pA=P_A.ap(), pB=P_B.ap(), pS0=P_S0.ap(), pS1=P_S1.ap(),
            pOT=P_OT.ap(),
            pa_bf=P_A.bitcast(BF16).ap(), pb_bf=P_B.bitcast(BF16).ap(),
            ps0_bf=P_S0.bitcast(BF16).ap(), ps1_bf=P_S1.bitcast(BF16).ap(),
            pot_bf=P_OT.bitcast(BF16).ap(),
            sems=sems,
        )

        dry = _Dry()
        MS = {}
        Trec = dict(base, g=dry, sy=dry, pe=dry, dv=dry, ac=dry)
        _emit(Trec, True, MS, {})

        def one(real_name, eng):
            Tx = dict(base, g=dry, sy=dry, pe=dry, dv=dry, ac=dry)
            Tx[real_name] = eng
            _emit(Tx, False, MS, {})

        @block.gpsimd
        def _(eng):
            one("g", eng)

        @block.sync
        def _(eng):
            one("sy", eng)

        @block.tensor
        def _(eng):
            one("pe", eng)

        @block.vector
        def _(eng):
            one("dv", eng)

        @block.scalar
        def _(eng):
            one("ac", eng)

    return nc


def kernel(L0_attn_x, L0_attn_Wq, L0_attn_Wk, L0_attn_Wv,
           L0_ffn_w_gate, L0_ffn_w_in, L0_ffn_w_out):
    x = np.ascontiguousarray(np.asarray(L0_attn_x, dtype=np.float32))
    xf = x.reshape(16384, D)
    wqT = np.ascontiguousarray(np.asarray(L0_attn_Wq, np.float32).T)
    wkT = np.ascontiguousarray(np.asarray(L0_attn_Wk, np.float32).T)
    wvT = np.ascontiguousarray(np.asarray(L0_attn_Wv, np.float32).T)
    wgT = np.ascontiguousarray(np.asarray(L0_ffn_w_gate, np.float32).T)
    winN = np.ascontiguousarray(np.asarray(L0_ffn_w_in, np.float32))
    woutT = np.ascontiguousarray(
        np.asarray(L0_ffn_w_out, np.float32).transpose(0, 2, 1))
    ident = np.eye(128, dtype=np.float32)

    nc = _build()
    in_maps = []
    for c in range(8):
        own = xf[c * 2048:(c + 1) * 2048]
        pc = c ^ 1
        par = xf[pc * 2048:(pc + 1) * 2048]
        xb = np.ascontiguousarray(np.concatenate([own, par], axis=0))
        in_maps.append({
            "xb": xb, "wqT": wqT, "wkT": wkT, "wvT": wvT, "wgT": wgT,
            "winN": winN, "woutT": woutT, "idin": ident,
        })
    res = run_bass_kernel_spmd(nc, in_maps, core_ids=list(range(8)))
    out = np.empty((16384, D), np.float32)
    for c in range(8):
        out[c * 2048:(c + 1) * 2048] = np.asarray(res.results[c]["out"],
                                                  np.float32)
    return out.reshape(4, 4096, D)


# version 09
# speedup vs baseline: 1.0727x; 1.0290x over previous
import numpy as np
import concourse.bass as bass
import concourse.mybir as mybir
from concourse.bass_utils import run_bass_kernel_spmd

F32 = mybir.dt.float32
BF16 = mybir.dt.bfloat16
ALU = mybir.AluOpType
AF = mybir.ActivationFunctionType
AXX = mybir.AxisListType.X

D = 1024
HID = 4096
NE = 8
TOK = 4096
OWN = 2048
NT = 32
NTO = 16
NP = 16
EXPS = 12.0
EPS = 1e-5
INVD = 1.0 / D
INVN = 1.0 / (4096.0 * 1024.0)

R_WQKV = 0
R_QT = (24576, 42016)
R_KT = (28672, 46112)
R_VB = (32768, 50208)
R_PT = 34848
R_OUTS = 36896
R_XNAT = 37920
R_YSCR = 39968
R_OSTG = 34848
R_WI = 0
R_WO = 16384
R_M = 32768
AR_N = 53248

BANKS4 = ("S0a", "S0b", "S1a", "S1b")
PROC = list(range(8, 16)) + list(range(8))
TILES = list(range(16, 32)) + list(range(16))


class _DryInst:
    def then_inc(self, sem, n=1):
        return self


_DRY_INST = _DryInst()


class _Dry:
    def __getattr__(self, name):
        def f(*a, **k):
            return _DRY_INST
        return f


def _g_decode(g):
    if g < 16:
        return 0, g
    if g < 32:
        return 1, g - 16
    return 2, g - 32


DL15 = ([("glog", tt) for tt in range(NTO)]
        + [("mmq", 0, 0, 0), ("mmq", 0, 0, 1), ("mmq", 0, 1, 0),
           ("mmq", 0, 1, 1), ("mmq", 1, 0, 0), ("mmq", 1, 0, 1),
           ("mmq", 1, 1, 0), ("mmq", 1, 1, 1)])
DEALT_MM = {(0, 0, 0), (0, 0, 1), (0, 1, 0), (0, 1, 1),
            (1, 0, 0), (1, 0, 1), (1, 1, 0), (1, 1, 1)}


def _dl15_qc(qc):
    return list(range(9 * qc, min(9 * qc + 9, len(DL15))))


def _ab_schedule():
    seq = []
    for half in (0, 1):
        first = 8 * half
        for g in range(36):
            seq.append(("qkvg", first, g))
        for pj in range(first, first + 8):
            for qc in range(4):
                if pj < first + 7:
                    for g in range(9 * qc, 9 * qc + 9):
                        seq.append(("qkvg", pj + 1, g))
                elif pj == 15:
                    for i in _dl15_qc(qc):
                        seq.append(("dl", i))
                for jb in range(8):
                    seq.append(("atr", pj, qc, jb))
    seq.append(("sum2x",))
    seq.append(("bcx",))
    return {k: ("A" if i % 2 == 0 else "B") for i, k in enumerate(seq)}


BANKOF = _ab_schedule()


def _moe_a_bank(i, cb, hh, tc):
    return BANKS4[(i * 64 + cb * 8 + hh * 4 + tc) % 4]


def _moe_b_bank(i, dc, tt):
    return ("OTa", "OTb", "A", "B")[(i * 32 + dc * NTO + tt) % 4]


def _emit(T, rec, MS, cnt):
    AR = T["AR"]
    ARF = T["ARF"]
    xnTb = T["xnTb"]
    yat = T["yat"]
    yffn = T["yffn"]
    xstg = T["xstg"]
    wgsb = T["wgsb"]
    identsb = T["identsb"]
    onescol = T["onescol"]
    onesrow = T["onesrow"]
    consts = T["consts"]
    stats = T["stats"]
    st12 = T["st12"]
    lnm = T["lnm"]
    lnr = T["lnr"]
    gts = T["gts"]
    lgt = T["lgt"]
    gsc = T["gsc"]
    pS0, pS1, pOT = T["pS0"], T["pS1"], T["pOT"]
    S = T["sems"]
    g, sy, pe, dv, ac = T["g"], T["sy"], T["pe"], T["dv"], T["ac"]

    BK = {
        "A": T["pA"][:, 0:512], "B": T["pB"][:, 0:512],
        "S0a": pS0[:, 0:512], "S0b": pS0[:, 512:1024],
        "S1a": pS1[:, 0:512], "S1b": pS1[:, 512:1024],
        "OTa": pOT[:, 0:512], "OTb": pOT[:, 512:1024],
    }
    BKBF = {
        "A": T["pa_bf"][:, 0:1024], "B": T["pb_bf"][:, 0:1024],
        "S0a": T["ps0_bf"][:, 0:1024], "S0b": T["ps0_bf"][:, 1024:2048],
        "S1a": T["ps1_bf"][:, 0:1024], "S1b": T["ps1_bf"][:, 1024:2048],
        "OTa": T["pot_bf"][:, 0:1024], "OTb": T["pot_bf"][:, 1024:2048],
    }

    def inc(inst, semname, label=None, n=1):
        inst.then_inc(S[semname], n)
        cnt[semname] = cnt.get(semname, 0) + n
        if label is not None:
            if rec:
                MS[label] = (semname, cnt[semname])
            else:
                assert MS[label] == (semname, cnt[semname]), (
                    label, MS[label], (semname, cnt[semname]))

    def alias(dst_label, src_label):
        if rec:
            MS[dst_label] = MS[src_label]

    def W(eng, label):
        if rec and label not in MS:
            return
        sn, v = MS[label]
        eng.wait_ge(S[sn], v)

    OWN_ = {}

    def bacq(eng, name):
        lb = OWN_.get(name)
        if lb is not None:
            W(eng, lb)

    def brel(name, label):
        OWN_[name] = label

    NLOAD = 16 + 16 + 16 + 16 + NTO

    def x_row_tile(idx):
        if idx < 16:
            return TILES[idx]
        if idx < 32:
            pj, th = divmod(idx - 16, 2)
            return 2 * PROC[pj] + th
        if idx < 48:
            return TILES[16 + (idx - 32)]
        if idx < 64:
            pj, th = divmod(idx - 48, 2)
            return 2 * PROC[8 + pj] + th
        return idx - 64

    def x_consumers(idx):
        if idx < 16:
            return [("p0xn", idx), ("p0sq", idx)]
        if idx < 32:
            pj, th = divmod(idx - 16, 2)
            return [("xnat", pj, th)]
        if idx < 48:
            seq = 16 + (idx - 32)
            return [("p0xn", seq), ("p0sq", seq)]
        if idx < 64:
            pj, th = divmod(idx - 48, 2)
            return [("xnat", 8 + pj, th)]
        return [("fin", idx - 64)]

    x_load_sem = []
    lx = [0, 0]
    for i in range(NLOAD):
        sl = i % 2
        lx[sl] += 16
        x_load_sem.append(("sLX%d" % sl, lx[sl]))

    def emit_xload(idx):
        sl = idx % 2
        if idx >= 2:
            for lb in x_consumers(idx - 2):
                W(g, lb)
        tt = x_row_tile(idx)
        inc(g.dma_start(out=xstg[:, sl * D:(sl + 1) * D],
                        in_=T["xb"][tt * 128:(tt + 1) * 128, :]),
            "sLX%d" % sl, ("xld", idx), 16)

    def emit_gpsimd():
        g.memset(consts[:, 0:1], -EXPS)
        g.memset(consts[:, 1:2], EPS)
        g.memset(onescol[:], 1.0)
        g.memset(onesrow[:], 1.0)
        g.memset(stats[:, 0:2], 0.0)
        for par in range(2):
            for kb in range(32):
                off = R_VB[par] + kb * 65 + 64
                g.memset(AR[:, off: off + 1], 1.0)
        inc(g.memset(st12[:, 0:8], 0.0), "sGP", ("init",))

        inc(g.dma_start(out=identsb[:], in_=T["idin"]), "sID", ("idld",), 16)
        for m, w in enumerate((T["wqT"], T["wkT"], T["wvT"])):
            src = w.rearrange("(kt p) n -> p kt n", p=128)
            dst = AR[:, R_WQKV + m * 8192: R_WQKV + (m + 1) * 8192].rearrange(
                "p (kt n) -> p kt n", kt=8)
            for hvf in range(2):
                inc(g.dma_start(out=dst[:, hvf * 4:(hvf + 1) * 4, :],
                                in_=src[:, hvf * 4:(hvf + 1) * 4, :]),
                    "sLW", None, 16)
        inc(g.dma_start(
            out=wgsb[:].rearrange("p (kt n) -> p kt n", kt=8),
            in_=T["wgT"].rearrange("(kt p) n -> p kt n", p=128)),
            "sLW", ("wld",), 16)

        for idx in range(64):
            emit_xload(idx)

        def mq_load(Q):
            e, q = Q // 4, Q % 4
            slot = Q % 2
            for kk, (ten, base) in enumerate(((T["winN"], R_WI),
                                             (T["woutT"], R_WO))):
                srcq = ten[e, q * 1024:(q + 1) * 1024, :]
                srcq = srcq.rearrange("(kt p) n -> p kt n", p=128)
                dst = AR[:, base + slot * 8192: base + (slot + 1) * 8192]
                dst = dst.rearrange("p (kt n) -> p kt n", kt=8)
                for hvf in range(2):
                    inc(g.dma_start(out=dst[:, hvf * 4:(hvf + 1) * 4, :],
                                    in_=srcq[:, hvf * 4:(hvf + 1) * 4, :]),
                        "sWI%d" % slot, ("mqld", Q, 2 * kk + hvf), 16)

        W(g, ("qkv", 15, 35))
        W(g, ("st", 14, 3, 31))
        mq_load(0)
        mq_load(1)
        W(g, ("atrlast",))
        W(g, ("statslast",))
        W(g, ("explast",))
        for Q in range(2, 32):
            W(g, ("mmlast", Q - 2))
            mq_load(Q)
        for idx in range(64, 64 + NTO):
            emit_xload(idx)

    def act_phase0(seq0):
        for s_ in range(16):
            seq = seq0 + s_
            idx = seq if seq < 16 else 32 + (seq - 16)
            sl = idx % 2
            W(ac, ("xld", idx))
            if seq >= 1:
                W(ac, ("p0var", seq - 1))
            ac.drain()
            inst = ac.activation(pOT[:, 0:1024], xstg[:, sl * D:(sl + 1) * D],
                                 AF.Square, accum_out=gsc[:, 0:1])
            inc(inst, "sAC", ("p0sq", seq))
            W(ac, ("p0var", seq))
            ac.drain()
            ac.activation(gsc[:, 5:6], gsc[:, 4:5], AF.Ln, bias=consts[:, 1:2])
            ac.drain()
            tt = TILES[seq]
            inc(ac.activation(lnr[:, tt:tt + 1], gsc[:, 5:6], AF.Exp,
                              scale=-0.5), "sAC", ("p0rst", seq))

    def emit_act():
        ac.wait_ge(S["sGP"], 1)
        for half in range(2):
            act_phase0(16 * half)
            for pj in range(8 * half, 8 * half + 8):
                for qc in range(4):
                    for k in range(32):
                        gk = (pj * 4 + qc) * 32 + k
                        W(ac, ("st", pj, qc, k))
                        if gk >= 2:
                            g2 = gk - 2
                            W(ac, ("av", g2 // 128, (g2 // 32) % 4, g2 % 32))
                        sl = k % 2
                        stb = pS0 if sl == 0 else pS1
                        inst = ac.activation(
                            AR[:, R_PT + sl * 1024: R_PT + (sl + 1) * 1024],
                            stb[:, 0:1024], AF.Exp, bias=consts[:, 0:1])
                        inc(inst, "sAC", ("exp", pj, qc, k))
                    W(ac, ("av", pj, qc, 31))
                    if qc == 0 and pj > 0:
                        W(ac, ("statsacc", pj - 1, 1))
                    if qc > 0:
                        W(ac, ("atr", pj, qc - 1, 7))
                    ac.drain()
                    inc(ac.copy(AR[0:65, R_OUTS: R_OUTS + 1024],
                                pOT[0:65, 0:1024]), "sAC", ("outse", pj, qc))
        alias(("explast",), ("exp", NP - 1, 3, 31))

        W(ac, ("var1",))
        ac.drain()
        ac.activation(st12[0:1, 4:5], st12[0:1, 5:6], AF.Ln,
                      bias=consts[0:1, 1:2])
        ac.drain()
        inc(ac.activation(st12[0:1, 6:7], st12[0:1, 4:5], AF.Exp,
                          scale=-0.5), "sAC", ("rsig",))

        for tt in range(NTO):
            W(ac, ("gm", tt))
            ac.drain()
            ac.activation(gsc[:, 24:32], lgt[:], AF.Exp, bias=gsc[:, 1:2])
            ac.drain()
            inc(ac.activation(gsc[:, 3:4], gsc[:, 2:3], AF.Exp,
                              bias=gsc[:, 1:2]), "sAC", ("ge", tt))

    def emit_pe():
        def pe_qkv_group(pj, gidx):
            j = PROC[pj]
            toff = (256 * j) % 2048
            par = pj % 2
            m, sub = _g_decode(gidx)
            bk = BANKOF[("qkvg", pj, gidx)]
            bacq(pe, bk)
            last = None
            if m < 2:
                for kt in range(8):
                    lhs = AR[:, R_WQKV + m * 8192 + kt * 1024 + sub * 64:
                             R_WQKV + m * 8192 + kt * 1024 + sub * 64 + 65]
                    rhs = xnTb[:, kt * 2048 + toff: kt * 2048 + toff + 256]
                    last = pe.matmul(BK[bk][0:65, 0:256], lhs, rhs,
                                     start=(kt == 0), stop=(kt == 7))
            else:
                th, dc = sub // 2, sub % 2
                for kt in range(8):
                    lhs = xnTb[:, kt * 2048 + toff + th * 128:
                               kt * 2048 + toff + (th + 1) * 128]
                    rhs = AR[:, R_WQKV + 16384 + kt * 1024 + dc * 512:
                             R_WQKV + 16384 + kt * 1024 + (dc + 1) * 512]
                    last = pe.matmul(BK[bk][:, 0:512], lhs, rhs,
                                     start=(kt == 0), stop=(kt == 7))
            inc(last, "sPE", ("qkv", pj, gidx))
            brel(bk, ("qkve", pj, gidx))

        def av_step(pj, qc, k):
            sl = k % 2
            par = pj % 2
            W(pe, ("exp", pj, qc, k))
            if k == 0:
                bacq(pe, "OTa")
                bacq(pe, "OTb")
            last = None
            for hq in range(2):
                lhs = AR[:, R_VB[par] + k * 65: R_VB[par] + k * 65 + 65]
                rhs = AR[:, R_PT + sl * 1024 + hq * 512:
                         R_PT + sl * 1024 + (hq + 1) * 512]
                last = pe.matmul(pOT[0:65, hq * 512:(hq + 1) * 512], lhs, rhs,
                                 start=(k == 0), stop=(k == 31))
            inc(last, "sPE", ("av", pj, qc, k))
            if k == 31:
                brel("OTa", ("outse", pj, qc))
                brel("OTb", ("outse", pj, qc))

        def pe_deal15(di):
            item = DL15[di]
            bk = BANKOF[("dl", di)]
            if item[0] == "glog":
                tt = item[1]
                if di == 0:
                    W(pe, ("wld",))
                bacq(pe, bk)
                last = None
                for kt in range(8):
                    lhs = xnTb[:, kt * 2048 + tt * 128:
                               kt * 2048 + (tt + 1) * 128]
                    rhs = wgsb[:, kt * 8:(kt + 1) * 8]
                    last = pe.matmul(BK[bk][:, 0:8], lhs, rhs,
                                     start=(kt == 0), stop=(kt == 7))
                inc(last, "sPE", ("glog", tt))
                brel(bk, ("gloge", tt))
            else:
                _, Q, ic, jc = item
                slot = Q % 2
                if ic == 0 and jc == 0:
                    W(pe, ("mqld", Q, 3))
                bacq(pe, bk)
                last = None
                for kt in range(8):
                    lhs = AR[:, R_WI + slot * 8192 + kt * 1024 + ic * 128:
                             R_WI + slot * 8192 + kt * 1024 + (ic + 1) * 128]
                    rhs = AR[:, R_WO + slot * 8192 + kt * 1024 + jc * 512:
                             R_WO + slot * 8192 + kt * 1024 + (jc + 1) * 512]
                    last = pe.matmul(BK[bk][:, 0:512], lhs, rhs,
                                     start=(kt == 0), stop=(kt == 7))
                inc(last, "sPE", ("mm", Q, ic, jc))
                brel(bk, ("mme", Q, ic, jc))

        def pe_phase0(seq0):
            for s_ in range(16):
                seq = seq0 + s_
                idx = seq if seq < 16 else 32 + (seq - 16)
                sl = idx % 2
                W(pe, ("p0xn", seq))
                for jb in range(8):
                    gtc = seq * 8 + jb
                    bk = BANKS4[gtc % 4]
                    bacq(pe, bk)
                    src = AR[:, R_XNAT + sl * 1024 + jb * 128:
                             R_XNAT + sl * 1024 + (jb + 1) * 128]
                    inc(pe.transpose(BKBF[bk][:, 0:128], src, identsb[:]),
                        "sPE", ("p0tr", gtc))
                    brel(bk, ("p0ev", gtc))

        pe.wait_ge(S["sGP"], 1)
        W(pe, ("idld",))
        for half in range(2):
            pe_phase0(16 * half)
            if half == 0:
                W(pe, ("wld",))
            W(pe, ("p0ev", 16 * half * 8 + 127))
            brel("OTa", ("p0sq", 16 * half + 15))
            brel("OTb", ("p0sq", 16 * half + 15))
            first = 8 * half
            for gidx in range(36):
                pe_qkv_group(first, gidx)
            for pj in range(first, first + 8):
                W(pe, ("qkve", pj, 35))
                bg = ([(pj + 1, gg) for gg in range(36)]
                      if pj < first + 7 else [])
                bgi = 0
                for qc in range(4):
                    dealt = 0
                    for k in range(32):
                        sl = k % 2
                        sb = ("S0a", "S0b") if sl == 0 else ("S1a", "S1b")
                        bacq(pe, sb[0])
                        bacq(pe, sb[1])
                        stb = pS0 if sl == 0 else pS1
                        par = pj % 2
                        last = None
                        for hq in range(2):
                            lhs = AR[0:64, R_KT[par] + (k // 2) * 256
                                     + (k % 2) * 128:
                                     R_KT[par] + (k // 2) * 256
                                     + (k % 2) * 128 + 128]
                            rhs = AR[0:64, R_QT[par] + qc * 1024 + hq * 512:
                                     R_QT[par] + qc * 1024 + (hq + 1) * 512]
                            last = pe.matmul(
                                stb[0:128, hq * 512:(hq + 1) * 512], lhs, rhs,
                                start=True, stop=True)
                        inc(last, "sPE", ("st", pj, qc, k))
                        brel(sb[0], ("exp", pj, qc, k))
                        brel(sb[1], ("exp", pj, qc, k))
                        if dealt < 9 and bgi < len(bg) and k % 3 == 1:
                            pe_qkv_group(*bg[bgi])
                            bgi += 1
                            dealt += 1
                        elif pj == 15 and dealt < 9 and k % 3 == 1:
                            di = 9 * qc + dealt
                            if di < len(DL15):
                                pe_deal15(di)
                                dealt += 1
                        if k >= 1:
                            av_step(pj, qc, k - 1)
                    av_step(pj, qc, 31)
                    for jb in range(8):
                        bk = BANKOF[("atr", pj, qc, jb)]
                        bacq(pe, bk)
                        if jb == 0:
                            W(pe, ("outse", pj, qc))
                        src = AR[0:65, R_OUTS + jb * 128:
                                 R_OUTS + (jb + 1) * 128]
                        inc(pe.transpose(BKBF[bk][:, 0:65], src,
                                         identsb[0:65, 0:65]),
                            "sPE", ("atr", pj, qc, jb))
                        brel(bk, ("ntt", pj, qc, jb))
        alias(("atrlast",), ("atr", NP - 1, 3, 7))

        W(pe, ("statsacc", NP - 1, 1))
        bk = BANKOF[("sum2x",)]
        bacq(pe, bk)
        inc(pe.matmul(BK[bk][0:1, 0:2], onescol[:], stats[:, 0:2],
                      start=True, stop=True), "sPE", ("sum2",))
        brel(bk, ("sum2e",))
        W(pe, ("ab",))
        bk = BANKOF[("bcx",)]
        bacq(pe, bk)
        inc(pe.matmul(BK[bk][0:128, 0:2], onesrow[:], st12[0:1, 6:8],
                      start=True, stop=True), "sPE", ("bc",))
        brel(bk, ("bcev",))

        for e in range(NE):
            for q in range(4):
                Q = 4 * e + q
                W(pe, ("mqld", Q, 3))
                slot = Q % 2
                for ic in range(8):
                    for jc in range(2):
                        if (Q, ic, jc) in DEALT_MM:
                            continue
                        bk = ("S0a", "S0b")[(Q * 16 + ic * 2 + jc) % 2]
                        bacq(pe, bk)
                        last = None
                        for kt in range(8):
                            lhs = AR[:, R_WI + slot * 8192 + kt * 1024
                                     + ic * 128:
                                     R_WI + slot * 8192 + kt * 1024
                                     + (ic + 1) * 128]
                            rhs = AR[:, R_WO + slot * 8192 + kt * 1024
                                     + jc * 512:
                                     R_WO + slot * 8192 + kt * 1024
                                     + (jc + 1) * 512]
                            last = pe.matmul(BK[bk][:, 0:512], lhs, rhs,
                                             start=(kt == 0), stop=(kt == 7))
                        inc(last, "sPE", ("mm", Q, ic, jc))
                        brel(bk, ("mme", Q, ic, jc))
                alias(("mmlast", Q), ("mm", Q, 7, 1))
            W(pe, ("mme", 4 * e + 3, 7, 1))
            for dc in range(2):
                for tt in range(NTO):
                    bk = _moe_b_bank(e, dc, tt)
                    bacq(pe, bk)
                    last = None
                    for kt in range(8):
                        lhs = xnTb[:, kt * 2048 + tt * 128:
                                   kt * 2048 + (tt + 1) * 128]
                        rhs = AR[:, R_M + (e % 2) * 8192 + kt * 1024
                                 + dc * 512:
                                 R_M + (e % 2) * 8192 + kt * 1024
                                 + (dc + 1) * 512]
                        last = pe.matmul(BK[bk][:, 0:512], lhs, rhs,
                                         start=(kt == 0), stop=(kt == 7))
                    inc(last, "sPE", ("yph", e, dc, tt))
                    brel(bk, ("ybe", e, dc, tt))

    def emit_dve():
        def dve_qkv_evac(pj, gidx):
            par = pj % 2
            m, sub = _g_decode(gidx)
            bk = BANKOF[("qkvg", pj, gidx)]
            if gidx == 0 and pj >= 2:
                W(dv, ("st", pj - 2, 3, 31))
                W(dv, ("av", pj - 2, 3, 31))
            W(dv, ("qkv", pj, gidx))
            if m < 2:
                base = R_QT[par] if m == 0 else R_KT[par]
                inc(dv.tensor_copy(
                    AR[0:64, base + sub * 256: base + (sub + 1) * 256],
                    BK[bk][0:64, 0:256]), "sDV", ("qkve", pj, gidx))
            else:
                th, dc = sub // 2, sub % 2
                last = None
                for gg in range(8):
                    kb = 2 * (dc * 8 + gg) + th
                    off = R_VB[par] + kb * 65
                    last = dv.tensor_copy(AR[:, off: off + 64],
                                          BK[bk][:, gg * 64:(gg + 1) * 64])
                inc(last, "sDV", ("qkve", pj, gidx))

        def dve_deal15(i):
            item = DL15[i]
            bk = BANKOF[("dl", i)]
            if item[0] == "glog":
                tt = item[1]
                W(dv, ("glog", tt))
                dv.drain()
                inc(dv.tensor_copy(
                    ARF[:, R_YSCR // 2 + tt * 8: R_YSCR // 2 + tt * 8 + 8],
                    BK[bk][:, 0:8]), "sDV", ("gloge", tt))
            else:
                _, Q, ic, jc = item
                if i == NTO:
                    W(dv, ("av", 14, 3, 31))
                W(dv, ("mm", Q, ic, jc))
                msl = AR[:, R_M + ic * 1024 + jc * 512:
                         R_M + ic * 1024 + (jc + 1) * 512]
                dv.drain()
                if Q == 0:
                    inst = dv.tensor_copy(msl, BK[bk][:, 0:512])
                else:
                    inst = dv.tensor_tensor(msl, BK[bk][:, 0:512], msl,
                                            ALU.add)
                inc(inst, "sDV", ("mme", Q, ic, jc))

        def dve_phase0(seq0):
            for s_ in range(16):
                seq = seq0 + s_
                idx = seq if seq < 16 else 32 + (seq - 16)
                sl = idx % 2
                tt = TILES[seq]
                W(dv, ("xld", idx))
                if seq >= 2:
                    W(dv, ("p0tr", (seq - 2) * 8 + 7))
                dv.drain()
                dv.reduce_sum(gsc[:, 1:2], xstg[:, sl * D:(sl + 1) * D],
                              axis=AXX)
                dv.drain()
                dv.tensor_scalar(lnm[:, tt:tt + 1], gsc[:, 1:2], -INVD, None,
                                 ALU.mult)
                W(dv, ("p0sq", seq))
                dv.tensor_scalar(gsc[:, 2:3], gsc[:, 0:1], INVD, None,
                                 ALU.mult)
                dv.drain()
                dv.tensor_mul(gsc[:, 3:4], lnm[:, tt:tt + 1],
                              lnm[:, tt:tt + 1])
                dv.drain()
                inc(dv.tensor_tensor(gsc[:, 4:5], gsc[:, 2:3], gsc[:, 3:4],
                                     ALU.subtract), "sDV", ("p0var", seq))
                W(dv, ("p0rst", seq))
                inc(dv.tensor_scalar(
                    AR[:, R_XNAT + sl * 1024: R_XNAT + (sl + 1) * 1024],
                    xstg[:, sl * D:(sl + 1) * D],
                    lnm[:, tt:tt + 1], lnr[:, tt:tt + 1], ALU.add, ALU.mult),
                    "sDV", ("p0xn", seq))
                for jb in range(8):
                    gtc = seq * 8 + jb
                    bk = BANKS4[gtc % 4]
                    W(dv, ("p0tr", gtc))
                    dcol = jb * 2048 + (tt % 16) * 128
                    inc(dv.tensor_copy(xnTb[:, dcol: dcol + 128],
                                       BKBF[bk][:, 0:128]),
                        "sDV", ("p0ev", gtc))

        dv.wait_ge(S["sGP"], 1)
        for half in range(2):
            dve_phase0(16 * half)
            first = 8 * half
            for gidx in range(36):
                dve_qkv_evac(first, gidx)
            for pj in range(first, first + 8):
                j = PROC[pj]
                dv.drain()
                for th in range(2):
                    idx = (16 if half == 0 else 48) + 2 * (pj - first) + th
                    sn, v = x_load_sem[idx]
                    dv.wait_ge(S[sn], v)
                    tt = 2 * j + th
                    slx = idx % 2
                    inc(dv.tensor_scalar(
                        AR[:, R_XNAT + th * 1024: R_XNAT + (th + 1) * 1024],
                        xstg[:, slx * D:(slx + 1) * D],
                        lnm[:, tt:tt + 1], lnr[:, tt:tt + 1],
                        ALU.add, ALU.mult), "sDV", ("xnat", pj, th))
                for qc in range(4):
                    if pj < first + 7:
                        for gg in range(9 * qc, 9 * qc + 9):
                            dve_qkv_evac(pj + 1, gg)
                    elif pj == 15:
                        for i in _dl15_qc(qc):
                            dve_deal15(i)
                    for jb in range(8):
                        W(dv, ("atr", pj, qc, jb))
                        bkn = BANKOF[("atr", pj, qc, jb)]
                        gq = qc * 4 + jb // 2
                        hb = jb % 2
                        dv.drain()
                        dv.reciprocal(gsc[:, 6:7], BKBF[bkn][:, 64:65])
                        dv.drain()
                        if j < 8:
                            ydst = yat[:, (2 * j + hb) * 1024 + gq * 64:
                                       (2 * j + hb) * 1024 + gq * 64 + 64]
                        else:
                            ydst = AR[:, R_YSCR + hb * 1024 + gq * 64:
                                      R_YSCR + hb * 1024 + gq * 64 + 64]
                        xn_in = AR[:, R_XNAT + hb * 1024 + gq * 64:
                                   R_XNAT + hb * 1024 + gq * 64 + 64]
                        inc(dv.scalar_tensor_tensor(
                            ydst, BKBF[bkn][:, 0:64], gsc[:, 6:7], xn_in,
                            ALU.mult, ALU.add), "sDV", ("ntt", pj, qc, jb))
                dv.drain()
                for th in range(2):
                    if j < 8:
                        src = yat[:, (2 * j + th) * 1024:
                                  (2 * j + th + 1) * 1024]
                    else:
                        src = AR[:, R_YSCR + th * 1024:
                                 R_YSCR + (th + 1) * 1024]
                    dv.reduce_sum(gsc[:, 7:8], src, axis=AXX)
                    dv.drain()
                    dv.tensor_add(stats[:, 0:1], stats[:, 0:1], gsc[:, 7:8])
                    dv.drain()
                    dv.tensor_mul(AR[:, R_OUTS: R_OUTS + 1024], src, src)
                    dv.drain()
                    dv.reduce_sum(gsc[:, 7:8], AR[:, R_OUTS: R_OUTS + 1024],
                                  axis=AXX)
                    dv.drain()
                    inc(dv.tensor_add(stats[:, 1:2], stats[:, 1:2],
                                      gsc[:, 7:8]),
                        "sDV", ("statsacc", pj, th))
                    dv.drain()
        alias(("statslast",), ("statsacc", NP - 1, 1))

        W(dv, ("sum2",))
        dv.drain()
        inc(dv.tensor_copy(st12[0:1, 0:2], BK[BANKOF[("sum2x",)]][0:1, 0:2]),
            "sDV", ("sum2e",))
        dv.drain()
        dv.tensor_scalar(st12[0:1, 2:3], st12[0:1, 0:1], INVN, None, ALU.mult)
        dv.drain()
        dv.tensor_scalar(st12[0:1, 3:4], st12[0:1, 1:2], INVN, None, ALU.mult)
        dv.drain()
        dv.tensor_mul(st12[0:1, 4:5], st12[0:1, 2:3], st12[0:1, 2:3])
        dv.drain()
        inc(dv.tensor_tensor(st12[0:1, 5:6], st12[0:1, 3:4], st12[0:1, 4:5],
                             ALU.subtract), "sDV", ("var1",))
        W(dv, ("rsig",))
        dv.drain()
        dv.tensor_mul(st12[0:1, 7:8], st12[0:1, 2:3], st12[0:1, 6:7])
        dv.drain()
        inc(dv.tensor_scalar(st12[0:1, 7:8], st12[0:1, 7:8], -1.0, None,
                             ALU.mult), "sDV", ("ab",))
        W(dv, ("bc",))
        dv.drain()
        inc(dv.tensor_copy(stats[:, 2:4], BK[BANKOF[("bcx",)]][:, 0:2]),
            "sDV", ("bcev",))

        for tt in range(NTO):
            dv.drain()
            dv.tensor_copy(lgt[:], ARF[:, R_YSCR // 2 + tt * 8:
                                       R_YSCR // 2 + tt * 8 + 8])
            dv.drain()
            dv.reduce_max(gsc[:, 0:1], lgt[:], axis=AXX)
            dv.drain()
            dv.tensor_scalar(gsc[:, 1:2], gsc[:, 0:1], -1.0, None, ALU.mult)
            dv.drain()
            dv.tensor_scalar(gsc[:, 8:16], lgt[:], gsc[:, 0:1], None,
                             ALU.is_equal)
            dv.drain()
            dv.scalar_tensor_tensor(gsc[:, 16:24], gsc[:, 8:16], -1e30,
                                    lgt[:], ALU.mult, ALU.add)
            dv.drain()
            inc(dv.reduce_max(gsc[:, 2:3], gsc[:, 16:24], axis=AXX),
                "sDV", ("gm", tt))
            W(dv, ("ge", tt))
            dv.drain()
            dv.tensor_scalar(gsc[:, 4:5], gsc[:, 3:4], 1.0, None, ALU.add)
            dv.drain()
            dv.reciprocal(gsc[:, 5:6], gsc[:, 4:5])
            dv.drain()
            dv.tensor_scalar(gsc[:, 8:16], lgt[:], gsc[:, 2:3], None,
                             ALU.is_ge)
            dv.drain()
            dv.tensor_mul(gsc[:, 24:32], gsc[:, 24:32], gsc[:, 8:16])
            dv.drain()
            inc(dv.tensor_scalar(gts[:, tt * 8:(tt + 1) * 8],
                                 gsc[:, 24:32], gsc[:, 5:6], None, ALU.mult),
                "sDV", ("gate", tt))
            dv.drain()

        for e in range(NE):
            for q in range(4):
                Q = 4 * e + q
                dv.drain()
                if Q == 0:
                    W(dv, ("av", 15, 3, 31))
                    W(dv, ("exp", 15, 3, 31))
                    W(dv, ("outse", 15, 3))
                    W(dv, ("atr", 15, 3, 7))
                for ic in range(8):
                    for jc in range(2):
                        if (Q, ic, jc) in DEALT_MM:
                            continue
                        if q == 0 and ic == 0 and jc == 0 and e >= 2:
                            W(dv, ("yph", e - 2, 1, NTO - 1))
                        W(dv, ("mm", Q, ic, jc))
                        bkn = ("S0a", "S0b")[(Q * 16 + ic * 2 + jc) % 2]
                        msl = AR[:, R_M + (e % 2) * 8192 + ic * 1024
                                 + jc * 512:
                                 R_M + (e % 2) * 8192 + ic * 1024
                                 + (jc + 1) * 512]
                        if q == 0:
                            inst = dv.tensor_copy(msl, BK[bkn][:, 0:512])
                        else:
                            inst = dv.tensor_tensor(msl, BK[bkn][:, 0:512],
                                                    msl, ALU.add)
                        inc(inst, "sDV", ("mme", Q, ic, jc))
            dv.drain()
            for dc in range(2):
                for tt in range(NTO):
                    W(dv, ("yph", e, dc, tt))
                    bkn = _moe_b_bank(e, dc, tt)
                    ysl = yffn[:, tt * 1024 + dc * 512:
                               tt * 1024 + (dc + 1) * 512]
                    gcol = gts[:, tt * 8 + e: tt * 8 + e + 1]
                    if e == 0:
                        inst = dv.tensor_scalar(ysl, BK[bkn][:, 0:512], gcol,
                                                None, ALU.mult)
                    else:
                        inst = dv.scalar_tensor_tensor(
                            ysl, BK[bkn][:, 0:512], gcol, ysl,
                            ALU.mult, ALU.add)
                    inc(inst, "sDV", ("ybe", e, dc, tt))

        W(dv, ("yph", NE - 1, 1, NTO - 1))
        dv.drain()
        for tt in range(NTO):
            W(dv, ("ybe", NE - 1, 1, tt))
            idx = 64 + tt
            sn, v = x_load_sem[idx]
            dv.wait_ge(S[sn], v)
            if tt >= 2:
                dv.wait_ge(S["sOUT%d" % (tt % 2)], 16 * (tt // 2))
            sl = idx % 2
            ost = ARF[:, R_OSTG // 2 + (tt % 2) * 1024:
                      R_OSTG // 2 + ((tt % 2) + 1) * 1024]
            dv.drain()
            dv.tensor_scalar(ost, yat[:, tt * 1024:(tt + 1) * 1024],
                             stats[:, 2:3], stats[:, 3:4], ALU.mult, ALU.add)
            dv.drain()
            dv.tensor_add(ost, ost, yffn[:, tt * 1024:(tt + 1) * 1024])
            dv.drain()
            inc(dv.tensor_add(ost, ost, xstg[:, sl * D:(sl + 1) * D]),
                "sDV", ("fin", tt))

    def emit_sync():
        for tt in range(NTO):
            W(sy, ("fin", tt))
            inc(sy.dma_start(
                out=T["out"][tt * 128:(tt + 1) * 128, :],
                in_=ARF[:, R_OSTG // 2 + (tt % 2) * 1024:
                        R_OSTG // 2 + ((tt % 2) + 1) * 1024]),
                "sOUT%d" % (tt % 2), ("outst", tt), 16)
        sy.wait_ge(S["sOUT0"], 16 * (NTO // 2))
        sy.wait_ge(S["sOUT1"], 16 * (NTO // 2))

    emit_gpsimd()
    emit_act()
    emit_pe()
    emit_dve()
    emit_sync()


def _build():
    from contextlib import ExitStack
    nc = bass.Bass("TRN2", target_bir_lowering=False, debug=False)
    dram = dict(
        xb=nc.dram_tensor("xb", [TOK, D], F32, kind="ExternalInput").ap(),
        wqT=nc.dram_tensor("wqT", [D, D], F32, kind="ExternalInput").ap(),
        wkT=nc.dram_tensor("wkT", [D, D], F32, kind="ExternalInput").ap(),
        wvT=nc.dram_tensor("wvT", [D, D], F32, kind="ExternalInput").ap(),
        wgT=nc.dram_tensor("wgT", [D, NE], F32, kind="ExternalInput").ap(),
        winN=nc.dram_tensor("winN", [NE, HID, D], F32,
                            kind="ExternalInput").ap(),
        woutT=nc.dram_tensor("woutT", [NE, HID, D], F32,
                             kind="ExternalInput").ap(),
        idin=nc.dram_tensor("idin", [128, 128], F32,
                            kind="ExternalInput").ap(),
        out=nc.dram_tensor("out", [OWN, D], F32, kind="ExternalOutput").ap(),
    )

    with ExitStack() as _st:
        AR_t = _st.enter_context(nc.sbuf_tensor("AR", [128, AR_N], BF16))
        xnTb = _st.enter_context(nc.sbuf_tensor("xnTb", [128, 8 * 2048], BF16))
        yat = _st.enter_context(nc.sbuf_tensor("yat", [128, NTO * D], BF16))
        yffn = _st.enter_context(nc.sbuf_tensor("yffn", [128, NTO * D], BF16))
        xstg = _st.enter_context(nc.sbuf_tensor("xstg", [128, 2 * D], BF16))
        wgsb = _st.enter_context(nc.sbuf_tensor("wgsb", [128, 8 * NE], BF16))
        identsb = _st.enter_context(
            nc.sbuf_tensor("identsb", [128, 128], BF16))
        onescol = _st.enter_context(nc.sbuf_tensor("onescol", [128, 1], F32))
        onesrow = _st.enter_context(nc.sbuf_tensor("onesrow", [1, 128], F32))
        consts = _st.enter_context(nc.sbuf_tensor("consts", [128, 4], F32))
        stats = _st.enter_context(nc.sbuf_tensor("stats", [128, 8], F32))
        st12 = _st.enter_context(nc.sbuf_tensor("st12", [1, 8], F32))
        lnm = _st.enter_context(nc.sbuf_tensor("lnm", [128, NT], F32))
        lnr = _st.enter_context(nc.sbuf_tensor("lnr", [128, NT], F32))
        gts = _st.enter_context(nc.sbuf_tensor("gts", [128, NTO * NE], F32))
        lgt = _st.enter_context(nc.sbuf_tensor("lgt", [128, NE], F32))
        gsc = _st.enter_context(nc.sbuf_tensor("gsc", [128, 40], F32))
        P_A = _st.enter_context(nc.psum_tensor("P_A", [128, 512], F32))
        P_B = _st.enter_context(nc.psum_tensor("P_B", [128, 512], F32))
        P_S0 = _st.enter_context(nc.psum_tensor("P_S0", [128, 1024], F32))
        P_S1 = _st.enter_context(nc.psum_tensor("P_S1", [128, 1024], F32))
        P_OT = _st.enter_context(nc.psum_tensor("P_OT", [128, 1024], F32))
        semnames = ("sLX0", "sLX1", "sLW", "sID", "sWI0", "sWI1", "sWO0",
                    "sWO1", "sOUT0", "sOUT1", "sPE", "sDV", "sAC", "sGP")
        sems = {n: _st.enter_context(nc.semaphore(n)) for n in semnames}
        block = _st.enter_context(nc.Block())

        base = dict(dram)
        base.update(
            AR=AR_t.ap(), ARF=AR_t.bitcast(F32).ap(),
            xnTb=xnTb.ap(), yat=yat.ap(), yffn=yffn.ap(), xstg=xstg.ap(),
            wgsb=wgsb.ap(), identsb=identsb.ap(), onescol=onescol.ap(),
            onesrow=onesrow.ap(), consts=consts.ap(), stats=stats.ap(),
            st12=st12.ap(), lnm=lnm.ap(), lnr=lnr.ap(), gts=gts.ap(),
            lgt=lgt.ap(), gsc=gsc.ap(),
            pA=P_A.ap(), pB=P_B.ap(), pS0=P_S0.ap(), pS1=P_S1.ap(),
            pOT=P_OT.ap(),
            pa_bf=P_A.bitcast(BF16).ap(), pb_bf=P_B.bitcast(BF16).ap(),
            ps0_bf=P_S0.bitcast(BF16).ap(), ps1_bf=P_S1.bitcast(BF16).ap(),
            pot_bf=P_OT.bitcast(BF16).ap(),
            sems=sems,
        )

        dry = _Dry()
        MS = {}
        Trec = dict(base, g=dry, sy=dry, pe=dry, dv=dry, ac=dry)
        _emit(Trec, True, MS, {})

        def one(real_name, eng):
            Tx = dict(base, g=dry, sy=dry, pe=dry, dv=dry, ac=dry)
            Tx[real_name] = eng
            _emit(Tx, False, MS, {})

        @block.gpsimd
        def _(eng):
            one("g", eng)

        @block.sync
        def _(eng):
            one("sy", eng)

        @block.tensor
        def _(eng):
            one("pe", eng)

        @block.vector
        def _(eng):
            one("dv", eng)

        @block.scalar
        def _(eng):
            one("ac", eng)

    return nc


def kernel(L0_attn_x, L0_attn_Wq, L0_attn_Wk, L0_attn_Wv,
           L0_ffn_w_gate, L0_ffn_w_in, L0_ffn_w_out):
    x = np.ascontiguousarray(np.asarray(L0_attn_x, dtype=np.float32))
    xf = x.reshape(16384, D)
    wqT = np.ascontiguousarray(np.asarray(L0_attn_Wq, np.float32).T)
    wkT = np.ascontiguousarray(np.asarray(L0_attn_Wk, np.float32).T)
    wvT = np.ascontiguousarray(np.asarray(L0_attn_Wv, np.float32).T)
    wgT = np.ascontiguousarray(np.asarray(L0_ffn_w_gate, np.float32).T)
    winN = np.ascontiguousarray(np.asarray(L0_ffn_w_in, np.float32))
    woutT = np.ascontiguousarray(
        np.asarray(L0_ffn_w_out, np.float32).transpose(0, 2, 1))
    ident = np.eye(128, dtype=np.float32)

    nc = _build()
    in_maps = []
    for c in range(8):
        own = xf[c * 2048:(c + 1) * 2048]
        pc = c ^ 1
        par = xf[pc * 2048:(pc + 1) * 2048]
        xb = np.ascontiguousarray(np.concatenate([own, par], axis=0))
        in_maps.append({
            "xb": xb, "wqT": wqT, "wkT": wkT, "wvT": wvT, "wgT": wgT,
            "winN": winN, "woutT": woutT, "idin": ident,
        })
    res = run_bass_kernel_spmd(nc, in_maps, core_ids=list(range(8)))
    out = np.empty((16384, D), np.float32)
    for c in range(8):
        out[c * 2048:(c + 1) * 2048] = np.asarray(res.results[c]["out"],
                                                  np.float32)
    return out.reshape(4, 4096, D)


# version 10
# speedup vs baseline: 1.0939x; 1.0198x over previous
import numpy as np
import concourse.bass as bass
import concourse.mybir as mybir
from concourse.bass_utils import run_bass_kernel_spmd

F32 = mybir.dt.float32
BF16 = mybir.dt.bfloat16
ALU = mybir.AluOpType
AF = mybir.ActivationFunctionType
AXX = mybir.AxisListType.X

D = 1024
HID = 4096
NE = 8
TOK = 4096
OWN = 2048
NT = 32
NTO = 16
NP = 16
EXPS = 12.0
EPS = 1e-5
INVD = 1.0 / D
INVN = 1.0 / (4096.0 * 1024.0)

R_WQKV = 0
R_QT = (24576, 42016)
R_KT = (28672, 46112)
R_VB = (32768, 50208)
R_PT = 34848
R_OUTS = 36896
R_XNAT = 37920
R_YSCR = 39968
R_OSTG = 34848
R_WI = 0
R_WO = 16384
R_M = 32768
AR_N = 53248

BANKS4 = ("S0a", "S0b", "S1a", "S1b")
PROC = list(range(8, 16)) + list(range(8))
TILES = list(range(16, 32)) + list(range(16))


class _DryInst:
    def then_inc(self, sem, n=1):
        return self


_DRY_INST = _DryInst()


class _Dry:
    def __getattr__(self, name):
        def f(*a, **k):
            return _DRY_INST
        return f


def _g_decode(g):
    if g < 16:
        return 0, g
    if g < 32:
        return 1, g - 16
    return 2, g - 32


DL15 = ([("glog", tt) for tt in range(NTO)]
        + [("mmq", 0, 0, 0), ("mmq", 0, 0, 1), ("mmq", 0, 1, 0),
           ("mmq", 0, 1, 1), ("mmq", 1, 0, 0), ("mmq", 1, 0, 1),
           ("mmq", 1, 1, 0), ("mmq", 1, 1, 1)])
DEALT_MM = {(0, 0, 0), (0, 0, 1), (0, 1, 0), (0, 1, 1),
            (1, 0, 0), (1, 0, 1), (1, 1, 0), (1, 1, 1)}


def _dl15_qc(qc):
    return list(range(9 * qc, min(9 * qc + 9, len(DL15))))


def _ab_schedule():
    seq = []
    for half in (0, 1):
        first = 8 * half
        for g in range(36):
            seq.append(("qkvg", first, g))
        for pj in range(first, first + 8):
            for qc in range(4):
                if pj < first + 7:
                    for g in range(9 * qc, 9 * qc + 9):
                        seq.append(("qkvg", pj + 1, g))
                elif pj == 15:
                    for i in _dl15_qc(qc):
                        seq.append(("dl", i))
                for jb in range(8):
                    seq.append(("atr", pj, qc, jb))
    seq.append(("sum2x",))
    seq.append(("bcx",))
    return {k: ("A" if i % 2 == 0 else "B") for i, k in enumerate(seq)}


BANKOF = _ab_schedule()


def _moe_a_bank(i, cb, hh, tc):
    return BANKS4[(i * 64 + cb * 8 + hh * 4 + tc) % 4]


def _moe_b_bank(i, dc, tt):
    return ("OTa", "OTb", "A", "B")[(i * 32 + dc * NTO + tt) % 4]


def _emit(T, rec, MS, cnt):
    AR = T["AR"]
    ARF = T["ARF"]
    xnTb = T["xnTb"]
    yat = T["yat"]
    yffn = T["yffn"]
    xstg = T["xstg"]
    wgsb = T["wgsb"]
    identsb = T["identsb"]
    onescol = T["onescol"]
    onesrow = T["onesrow"]
    consts = T["consts"]
    stats = T["stats"]
    st12 = T["st12"]
    lnm = T["lnm"]
    lnr = T["lnr"]
    gts = T["gts"]
    lgt = T["lgt"]
    gsc = T["gsc"]
    pS0, pS1, pOT = T["pS0"], T["pS1"], T["pOT"]
    S = T["sems"]
    g, sy, pe, dv, ac = T["g"], T["sy"], T["pe"], T["dv"], T["ac"]

    BK = {
        "A": T["pA"][:, 0:512], "B": T["pB"][:, 0:512],
        "S0a": pS0[:, 0:512], "S0b": pS0[:, 512:1024],
        "S1a": pS1[:, 0:512], "S1b": pS1[:, 512:1024],
        "OTa": pOT[:, 0:512], "OTb": pOT[:, 512:1024],
    }
    BKBF = {
        "A": T["pa_bf"][:, 0:1024], "B": T["pb_bf"][:, 0:1024],
        "S0a": T["ps0_bf"][:, 0:1024], "S0b": T["ps0_bf"][:, 1024:2048],
        "S1a": T["ps1_bf"][:, 0:1024], "S1b": T["ps1_bf"][:, 1024:2048],
        "OTa": T["pot_bf"][:, 0:1024], "OTb": T["pot_bf"][:, 1024:2048],
    }

    def inc(inst, semname, label=None, n=1):
        inst.then_inc(S[semname], n)
        cnt[semname] = cnt.get(semname, 0) + n
        if label is not None:
            if rec:
                MS[label] = (semname, cnt[semname])
            else:
                assert MS[label] == (semname, cnt[semname]), (
                    label, MS[label], (semname, cnt[semname]))

    def alias(dst_label, src_label):
        if rec:
            MS[dst_label] = MS[src_label]

    def W(eng, label):
        if rec and label not in MS:
            return
        sn, v = MS[label]
        eng.wait_ge(S[sn], v)

    OWN_ = {}

    def bacq(eng, name):
        lb = OWN_.get(name)
        if lb is not None:
            W(eng, lb)

    def brel(name, label):
        OWN_[name] = label

    NLOAD = 16 + 16 + 16 + 16 + NTO

    def x_row_tile(idx):
        if idx < 16:
            return TILES[idx]
        if idx < 32:
            pj, th = divmod(idx - 16, 2)
            return 2 * PROC[pj] + th
        if idx < 48:
            return TILES[16 + (idx - 32)]
        if idx < 64:
            pj, th = divmod(idx - 48, 2)
            return 2 * PROC[8 + pj] + th
        return idx - 64

    def x_consumers(idx):
        if idx < 16:
            return [("p0xn", idx), ("p0sq", idx)]
        if idx < 32:
            pj, th = divmod(idx - 16, 2)
            return [("xnat", pj, th)]
        if idx < 48:
            seq = 16 + (idx - 32)
            return [("p0xn", seq), ("p0sq", seq)]
        if idx < 64:
            pj, th = divmod(idx - 48, 2)
            return [("xnat", 8 + pj, th)]
        return [("fin", idx - 64)]

    x_load_sem = []
    lx = [0, 0]
    for i in range(NLOAD):
        sl = i % 2
        lx[sl] += 16
        x_load_sem.append(("sLX%d" % sl, lx[sl]))

    def emit_xload(idx):
        sl = idx % 2
        if idx >= 2:
            for lb in x_consumers(idx - 2):
                W(g, lb)
        tt = x_row_tile(idx)
        inc(g.dma_start(out=xstg[:, sl * D:(sl + 1) * D],
                        in_=T["xb"][tt * 128:(tt + 1) * 128, :]),
            "sLX%d" % sl, ("xld", idx), 16)

    def emit_gpsimd():
        g.memset(consts[:, 0:1], -EXPS)
        g.memset(consts[:, 1:2], EPS)
        g.memset(onescol[:], 1.0)
        g.memset(onesrow[:], 1.0)
        g.memset(stats[:, 0:2], 0.0)
        for par in range(2):
            for kb in range(32):
                off = R_VB[par] + kb * 65 + 64
                g.memset(AR[:, off: off + 1], 1.0)
        inc(g.memset(st12[:, 0:8], 0.0), "sGP", ("init",))

        inc(g.dma_start(out=identsb[:], in_=T["idin"]), "sID", ("idld",), 16)
        for m, w in enumerate((T["wqT"], T["wkT"], T["wvT"])):
            src = w.rearrange("(kt p) n -> p kt n", p=128)
            dst = AR[:, R_WQKV + m * 8192: R_WQKV + (m + 1) * 8192].rearrange(
                "p (kt n) -> p kt n", kt=8)
            for hvf in range(2):
                inc(g.dma_start(out=dst[:, hvf * 4:(hvf + 1) * 4, :],
                                in_=src[:, hvf * 4:(hvf + 1) * 4, :]),
                    "sLW", None, 16)
        inc(g.dma_start(
            out=wgsb[:].rearrange("p (kt n) -> p kt n", kt=8),
            in_=T["wgT"].rearrange("(kt p) n -> p kt n", p=128)),
            "sLW", ("wld",), 16)

        for idx in range(64):
            emit_xload(idx)

        def mq_load(Q):
            e, q = Q // 4, Q % 4
            slot = Q % 2
            for kk, (ten, base) in enumerate(((T["winN"], R_WI),
                                             (T["woutT"], R_WO))):
                srcq = ten[e, q * 1024:(q + 1) * 1024, :]
                srcq = srcq.rearrange("(kt p) n -> p kt n", p=128)
                dst = AR[:, base + slot * 8192: base + (slot + 1) * 8192]
                dst = dst.rearrange("p (kt n) -> p kt n", kt=8)
                for hvf in range(2):
                    inc(g.dma_start(out=dst[:, hvf * 4:(hvf + 1) * 4, :],
                                    in_=srcq[:, hvf * 4:(hvf + 1) * 4, :]),
                        "sWI%d" % slot, ("mqld", Q, 2 * kk + hvf), 16)

        W(g, ("qkv", 15, 35))
        W(g, ("st", 14, 3, 31))
        mq_load(0)
        mq_load(1)
        W(g, ("atrlast",))
        W(g, ("statslast",))
        W(g, ("explast",))
        for Q in range(2, 32):
            W(g, ("mmlast", Q - 2))
            mq_load(Q)
        for idx in range(64, 64 + NTO):
            emit_xload(idx)

    def act_phase0(seq0):
        for s_ in range(16):
            seq = seq0 + s_
            idx = seq if seq < 16 else 32 + (seq - 16)
            sl = idx % 2
            W(ac, ("xld", idx))
            if seq >= 1:
                W(ac, ("p0var", seq - 1))
            ac.drain()
            inst = ac.activation(pOT[:, 0:1024], xstg[:, sl * D:(sl + 1) * D],
                                 AF.Square, accum_out=gsc[:, 0:1])
            inc(inst, "sAC", ("p0sq", seq))
            W(ac, ("p0var", seq))
            ac.drain()
            ac.activation(gsc[:, 5:6], gsc[:, 4:5], AF.Ln, bias=consts[:, 1:2])
            ac.drain()
            tt = TILES[seq]
            inc(ac.activation(lnr[:, tt:tt + 1], gsc[:, 5:6], AF.Exp,
                              scale=-0.5), "sAC", ("p0rst", seq))

    def emit_act():
        ac.wait_ge(S["sGP"], 1)
        for half in range(2):
            act_phase0(16 * half)
            for pj in range(8 * half, 8 * half + 8):
                for qc in range(4):
                    for k in range(32):
                        gk = (pj * 4 + qc) * 32 + k
                        W(ac, ("st", pj, qc, k))
                        if gk >= 2:
                            g2 = gk - 2
                            W(ac, ("av", g2 // 128, (g2 // 32) % 4, g2 % 32))
                        sl = k % 2
                        stb = pS0 if sl == 0 else pS1
                        inst = ac.activation(
                            AR[:, R_PT + sl * 1024: R_PT + (sl + 1) * 1024],
                            stb[:, 0:1024], AF.Exp, bias=consts[:, 0:1])
                        inc(inst, "sAC", ("exp", pj, qc, k))
                    W(ac, ("av", pj, qc, 31))
                    if qc == 0 and pj > 0:
                        W(ac, ("statsacc", pj - 1, 1))
                    if qc > 0:
                        W(ac, ("atr", pj, qc - 1, 7))
                    ac.drain()
                    inc(ac.copy(AR[0:65, R_OUTS: R_OUTS + 1024],
                                pOT[0:65, 0:1024]), "sAC", ("outse", pj, qc))
        alias(("explast",), ("exp", NP - 1, 3, 31))

        W(ac, ("var1",))
        ac.drain()
        ac.activation(st12[0:1, 4:5], st12[0:1, 5:6], AF.Ln,
                      bias=consts[0:1, 1:2])
        ac.drain()
        inc(ac.activation(st12[0:1, 6:7], st12[0:1, 4:5], AF.Exp,
                          scale=-0.5), "sAC", ("rsig",))

        for tt in range(NTO):
            W(ac, ("gm", tt))
            ac.drain()
            ac.activation(gsc[:, 24:32], lgt[:], AF.Exp, bias=gsc[:, 1:2])
            ac.drain()
            inc(ac.activation(gsc[:, 3:4], gsc[:, 2:3], AF.Exp,
                              bias=gsc[:, 1:2]), "sAC", ("ge", tt))

    def emit_pe():
        def pe_qkv_group(pj, gidx):
            j = PROC[pj]
            toff = (256 * j) % 2048
            par = pj % 2
            m, sub = _g_decode(gidx)
            bk = BANKOF[("qkvg", pj, gidx)]
            bacq(pe, bk)
            last = None
            if m < 2:
                for kt in range(8):
                    lhs = AR[:, R_WQKV + m * 8192 + kt * 1024 + sub * 64:
                             R_WQKV + m * 8192 + kt * 1024 + sub * 64 + 65]
                    rhs = xnTb[:, kt * 2048 + toff: kt * 2048 + toff + 256]
                    last = pe.matmul(BK[bk][0:65, 0:256], lhs, rhs,
                                     start=(kt == 0), stop=(kt == 7))
            else:
                th, dc = sub // 2, sub % 2
                for kt in range(8):
                    lhs = xnTb[:, kt * 2048 + toff + th * 128:
                               kt * 2048 + toff + (th + 1) * 128]
                    rhs = AR[:, R_WQKV + 16384 + kt * 1024 + dc * 512:
                             R_WQKV + 16384 + kt * 1024 + (dc + 1) * 512]
                    last = pe.matmul(BK[bk][:, 0:512], lhs, rhs,
                                     start=(kt == 0), stop=(kt == 7))
            inc(last, "sPE", ("qkv", pj, gidx))
            brel(bk, ("qkve", pj, gidx))

        def av_step(pj, qc, k):
            sl = k % 2
            par = pj % 2
            W(pe, ("exp", pj, qc, k))
            if k == 0:
                bacq(pe, "OTa")
                bacq(pe, "OTb")
            last = None
            for hq in range(2):
                lhs = AR[:, R_VB[par] + k * 65: R_VB[par] + k * 65 + 65]
                rhs = AR[:, R_PT + sl * 1024 + hq * 512:
                         R_PT + sl * 1024 + (hq + 1) * 512]
                last = pe.matmul(pOT[0:65, hq * 512:(hq + 1) * 512], lhs, rhs,
                                 start=(k == 0), stop=(k == 31))
            inc(last, "sPE", ("av", pj, qc, k))
            if k == 31:
                brel("OTa", ("outse", pj, qc))
                brel("OTb", ("outse", pj, qc))

        def pe_deal15(di):
            item = DL15[di]
            bk = BANKOF[("dl", di)]
            if item[0] == "glog":
                tt = item[1]
                if di == 0:
                    W(pe, ("wld",))
                bacq(pe, bk)
                last = None
                for kt in range(8):
                    lhs = xnTb[:, kt * 2048 + tt * 128:
                               kt * 2048 + (tt + 1) * 128]
                    rhs = wgsb[:, kt * 8:(kt + 1) * 8]
                    last = pe.matmul(BK[bk][:, 0:8], lhs, rhs,
                                     start=(kt == 0), stop=(kt == 7))
                inc(last, "sPE", ("glog", tt))
                brel(bk, ("gloge", tt))
            else:
                _, Q, ic, jc = item
                slot = Q % 2
                if ic == 0 and jc == 0:
                    W(pe, ("mqld", Q, 3))
                bacq(pe, bk)
                last = None
                for kt in range(8):
                    lhs = AR[:, R_WI + slot * 8192 + kt * 1024 + ic * 128:
                             R_WI + slot * 8192 + kt * 1024 + (ic + 1) * 128]
                    rhs = AR[:, R_WO + slot * 8192 + kt * 1024 + jc * 512:
                             R_WO + slot * 8192 + kt * 1024 + (jc + 1) * 512]
                    last = pe.matmul(BK[bk][:, 0:512], lhs, rhs,
                                     start=(kt == 0), stop=(kt == 7))
                inc(last, "sPE", ("mm", Q, ic, jc))
                brel(bk, ("mme", Q, ic, jc))

        def pe_phase0_slice(seq0, s_):
            seq = seq0 + s_
            idx = seq if seq < 16 else 32 + (seq - 16)
            sl = idx % 2
            W(pe, ("p0xn", seq))
            for jb in range(8):
                gtc = seq * 8 + jb
                bk = BANKS4[gtc % 4]
                bacq(pe, bk)
                src = AR[:, R_XNAT + sl * 1024 + jb * 128:
                         R_XNAT + sl * 1024 + (jb + 1) * 128]
                inc(pe.transpose(BKBF[bk][:, 0:128], src, identsb[:]),
                    "sPE", ("p0tr", gtc))
                brel(bk, ("p0ev", gtc))

        pe.wait_ge(S["sGP"], 1)
        W(pe, ("idld",))
        for half in range(2):
            first = 8 * half
            for s_ in range(16):
                pe_phase0_slice(16 * half, s_)
                if s_ >= 1:
                    if s_ == 1:
                        if half == 0:
                            W(pe, ("wld",))
                        W(pe, ("p0ev", (16 * half + 1) * 8 + 7))
                    for gidx in range(3 * (s_ - 1), min(36, 3 * s_)):
                        pe_qkv_group(first, gidx)
            W(pe, ("p0ev", 16 * half * 8 + 127))
            brel("OTa", ("p0sq", 16 * half + 15))
            brel("OTb", ("p0sq", 16 * half + 15))
            for pj in range(first, first + 8):
                W(pe, ("qkve", pj, 35))
                bg = ([(pj + 1, gg) for gg in range(36)]
                      if pj < first + 7 else [])
                bgi = 0
                for qc in range(4):
                    dealt = 0
                    for k in range(32):
                        sl = k % 2
                        sb = ("S0a", "S0b") if sl == 0 else ("S1a", "S1b")
                        bacq(pe, sb[0])
                        bacq(pe, sb[1])
                        stb = pS0 if sl == 0 else pS1
                        par = pj % 2
                        last = None
                        for hq in range(2):
                            lhs = AR[0:64, R_KT[par] + (k // 2) * 256
                                     + (k % 2) * 128:
                                     R_KT[par] + (k // 2) * 256
                                     + (k % 2) * 128 + 128]
                            rhs = AR[0:64, R_QT[par] + qc * 1024 + hq * 512:
                                     R_QT[par] + qc * 1024 + (hq + 1) * 512]
                            last = pe.matmul(
                                stb[0:128, hq * 512:(hq + 1) * 512], lhs, rhs,
                                start=True, stop=True)
                        inc(last, "sPE", ("st", pj, qc, k))
                        brel(sb[0], ("exp", pj, qc, k))
                        brel(sb[1], ("exp", pj, qc, k))
                        if dealt < 9 and bgi < len(bg) and k % 3 == 1:
                            pe_qkv_group(*bg[bgi])
                            bgi += 1
                            dealt += 1
                        elif pj == 15 and dealt < 9 and k % 3 == 1:
                            di = 9 * qc + dealt
                            if di < len(DL15):
                                pe_deal15(di)
                                dealt += 1
                        if k >= 1:
                            av_step(pj, qc, k - 1)
                    av_step(pj, qc, 31)
                    for jb in range(8):
                        bk = BANKOF[("atr", pj, qc, jb)]
                        bacq(pe, bk)
                        if jb == 0:
                            W(pe, ("outse", pj, qc))
                        src = AR[0:65, R_OUTS + jb * 128:
                                 R_OUTS + (jb + 1) * 128]
                        inc(pe.transpose(BKBF[bk][:, 0:65], src,
                                         identsb[0:65, 0:65]),
                            "sPE", ("atr", pj, qc, jb))
                        brel(bk, ("ntt", pj, qc, jb))
        alias(("atrlast",), ("atr", NP - 1, 3, 7))

        W(pe, ("statsacc", NP - 1, 1))
        bk = BANKOF[("sum2x",)]
        bacq(pe, bk)
        inc(pe.matmul(BK[bk][0:1, 0:2], onescol[:], stats[:, 0:2],
                      start=True, stop=True), "sPE", ("sum2",))
        brel(bk, ("sum2e",))
        W(pe, ("ab",))
        bk = BANKOF[("bcx",)]
        bacq(pe, bk)
        inc(pe.matmul(BK[bk][0:128, 0:2], onesrow[:], st12[0:1, 6:8],
                      start=True, stop=True), "sPE", ("bc",))
        brel(bk, ("bcev",))

        for e in range(NE):
            for q in range(4):
                Q = 4 * e + q
                W(pe, ("mqld", Q, 3))
                slot = Q % 2
                for ic in range(8):
                    for jc in range(2):
                        if (Q, ic, jc) in DEALT_MM:
                            continue
                        bk = ("S0a", "S0b")[(Q * 16 + ic * 2 + jc) % 2]
                        bacq(pe, bk)
                        last = None
                        for kt in range(8):
                            lhs = AR[:, R_WI + slot * 8192 + kt * 1024
                                     + ic * 128:
                                     R_WI + slot * 8192 + kt * 1024
                                     + (ic + 1) * 128]
                            rhs = AR[:, R_WO + slot * 8192 + kt * 1024
                                     + jc * 512:
                                     R_WO + slot * 8192 + kt * 1024
                                     + (jc + 1) * 512]
                            last = pe.matmul(BK[bk][:, 0:512], lhs, rhs,
                                             start=(kt == 0), stop=(kt == 7))
                        inc(last, "sPE", ("mm", Q, ic, jc))
                        brel(bk, ("mme", Q, ic, jc))
                alias(("mmlast", Q), ("mm", Q, 7, 1))
            W(pe, ("mme", 4 * e + 3, 7, 1))
            for dc in range(2):
                for tt in range(NTO):
                    bk = _moe_b_bank(e, dc, tt)
                    bacq(pe, bk)
                    last = None
                    for kt in range(8):
                        lhs = xnTb[:, kt * 2048 + tt * 128:
                                   kt * 2048 + (tt + 1) * 128]
                        rhs = AR[:, R_M + (e % 2) * 8192 + kt * 1024
                                 + dc * 512:
                                 R_M + (e % 2) * 8192 + kt * 1024
                                 + (dc + 1) * 512]
                        last = pe.matmul(BK[bk][:, 0:512], lhs, rhs,
                                         start=(kt == 0), stop=(kt == 7))
                    inc(last, "sPE", ("yph", e, dc, tt))
                    brel(bk, ("ybe", e, dc, tt))

    def emit_dve():
        def dve_qkv_evac(pj, gidx):
            par = pj % 2
            m, sub = _g_decode(gidx)
            bk = BANKOF[("qkvg", pj, gidx)]
            if gidx == 0 and pj >= 2:
                W(dv, ("st", pj - 2, 3, 31))
                W(dv, ("av", pj - 2, 3, 31))
            W(dv, ("qkv", pj, gidx))
            if m < 2:
                base = R_QT[par] if m == 0 else R_KT[par]
                inc(dv.tensor_copy(
                    AR[0:64, base + sub * 256: base + (sub + 1) * 256],
                    BK[bk][0:64, 0:256]), "sDV", ("qkve", pj, gidx))
            else:
                th, dc = sub // 2, sub % 2
                last = None
                for gg in range(8):
                    kb = 2 * (dc * 8 + gg) + th
                    off = R_VB[par] + kb * 65
                    last = dv.tensor_copy(AR[:, off: off + 64],
                                          BK[bk][:, gg * 64:(gg + 1) * 64])
                inc(last, "sDV", ("qkve", pj, gidx))

        def dve_deal15(i):
            item = DL15[i]
            bk = BANKOF[("dl", i)]
            if item[0] == "glog":
                tt = item[1]
                W(dv, ("glog", tt))
                dv.drain()
                inc(dv.tensor_copy(
                    ARF[:, R_YSCR // 2 + tt * 8: R_YSCR // 2 + tt * 8 + 8],
                    BK[bk][:, 0:8]), "sDV", ("gloge", tt))
            else:
                _, Q, ic, jc = item
                if i == NTO:
                    W(dv, ("av", 14, 3, 31))
                W(dv, ("mm", Q, ic, jc))
                msl = AR[:, R_M + ic * 1024 + jc * 512:
                         R_M + ic * 1024 + (jc + 1) * 512]
                dv.drain()
                if Q == 0:
                    inst = dv.tensor_copy(msl, BK[bk][:, 0:512])
                else:
                    inst = dv.tensor_tensor(msl, BK[bk][:, 0:512], msl,
                                            ALU.add)
                inc(inst, "sDV", ("mme", Q, ic, jc))

        def dve_phase0_slice(seq0, s_):
                seq = seq0 + s_
                idx = seq if seq < 16 else 32 + (seq - 16)
                sl = idx % 2
                tt = TILES[seq]
                W(dv, ("xld", idx))
                if seq >= 2:
                    W(dv, ("p0tr", (seq - 2) * 8 + 7))
                dv.drain()
                dv.reduce_sum(gsc[:, 1:2], xstg[:, sl * D:(sl + 1) * D],
                              axis=AXX)
                dv.drain()
                dv.tensor_scalar(lnm[:, tt:tt + 1], gsc[:, 1:2], -INVD, None,
                                 ALU.mult)
                W(dv, ("p0sq", seq))
                dv.tensor_scalar(gsc[:, 2:3], gsc[:, 0:1], INVD, None,
                                 ALU.mult)
                dv.drain()
                dv.tensor_mul(gsc[:, 3:4], lnm[:, tt:tt + 1],
                              lnm[:, tt:tt + 1])
                dv.drain()
                inc(dv.tensor_tensor(gsc[:, 4:5], gsc[:, 2:3], gsc[:, 3:4],
                                     ALU.subtract), "sDV", ("p0var", seq))
                W(dv, ("p0rst", seq))
                inc(dv.tensor_scalar(
                    AR[:, R_XNAT + sl * 1024: R_XNAT + (sl + 1) * 1024],
                    xstg[:, sl * D:(sl + 1) * D],
                    lnm[:, tt:tt + 1], lnr[:, tt:tt + 1], ALU.add, ALU.mult),
                    "sDV", ("p0xn", seq))
                for jb in range(8):
                    gtc = seq * 8 + jb
                    bk = BANKS4[gtc % 4]
                    W(dv, ("p0tr", gtc))
                    dcol = jb * 2048 + (tt % 16) * 128
                    inc(dv.tensor_copy(xnTb[:, dcol: dcol + 128],
                                       BKBF[bk][:, 0:128]),
                        "sDV", ("p0ev", gtc))

        dv.wait_ge(S["sGP"], 1)
        for half in range(2):
            first = 8 * half
            for s_ in range(16):
                dve_phase0_slice(16 * half, s_)
                if s_ >= 1:
                    for gidx in range(3 * (s_ - 1), min(36, 3 * s_)):
                        dve_qkv_evac(first, gidx)
            for pj in range(first, first + 8):
                j = PROC[pj]
                dv.drain()
                for th in range(2):
                    idx = (16 if half == 0 else 48) + 2 * (pj - first) + th
                    sn, v = x_load_sem[idx]
                    dv.wait_ge(S[sn], v)
                    tt = 2 * j + th
                    slx = idx % 2
                    inc(dv.tensor_scalar(
                        AR[:, R_XNAT + th * 1024: R_XNAT + (th + 1) * 1024],
                        xstg[:, slx * D:(slx + 1) * D],
                        lnm[:, tt:tt + 1], lnr[:, tt:tt + 1],
                        ALU.add, ALU.mult), "sDV", ("xnat", pj, th))
                for qc in range(4):
                    if pj < first + 7:
                        for gg in range(9 * qc, 9 * qc + 9):
                            dve_qkv_evac(pj + 1, gg)
                    elif pj == 15:
                        for i in _dl15_qc(qc):
                            dve_deal15(i)
                    for jb in range(8):
                        W(dv, ("atr", pj, qc, jb))
                        bkn = BANKOF[("atr", pj, qc, jb)]
                        gq = qc * 4 + jb // 2
                        hb = jb % 2
                        dv.drain()
                        dv.reciprocal(gsc[:, 6:7], BKBF[bkn][:, 64:65])
                        dv.drain()
                        if j < 8:
                            ydst = yat[:, (2 * j + hb) * 1024 + gq * 64:
                                       (2 * j + hb) * 1024 + gq * 64 + 64]
                        else:
                            ydst = AR[:, R_YSCR + hb * 1024 + gq * 64:
                                      R_YSCR + hb * 1024 + gq * 64 + 64]
                        xn_in = AR[:, R_XNAT + hb * 1024 + gq * 64:
                                   R_XNAT + hb * 1024 + gq * 64 + 64]
                        inc(dv.scalar_tensor_tensor(
                            ydst, BKBF[bkn][:, 0:64], gsc[:, 6:7], xn_in,
                            ALU.mult, ALU.add), "sDV", ("ntt", pj, qc, jb))
                dv.drain()
                for th in range(2):
                    if j < 8:
                        src = yat[:, (2 * j + th) * 1024:
                                  (2 * j + th + 1) * 1024]
                    else:
                        src = AR[:, R_YSCR + th * 1024:
                                 R_YSCR + (th + 1) * 1024]
                    dv.reduce_sum(gsc[:, 7:8], src, axis=AXX)
                    dv.drain()
                    dv.tensor_add(stats[:, 0:1], stats[:, 0:1], gsc[:, 7:8])
                    dv.drain()
                    dv.tensor_mul(AR[:, R_OUTS: R_OUTS + 1024], src, src)
                    dv.drain()
                    dv.reduce_sum(gsc[:, 7:8], AR[:, R_OUTS: R_OUTS + 1024],
                                  axis=AXX)
                    dv.drain()
                    inc(dv.tensor_add(stats[:, 1:2], stats[:, 1:2],
                                      gsc[:, 7:8]),
                        "sDV", ("statsacc", pj, th))
                    dv.drain()
        alias(("statslast",), ("statsacc", NP - 1, 1))

        W(dv, ("sum2",))
        dv.drain()
        inc(dv.tensor_copy(st12[0:1, 0:2], BK[BANKOF[("sum2x",)]][0:1, 0:2]),
            "sDV", ("sum2e",))
        dv.drain()
        dv.tensor_scalar(st12[0:1, 2:3], st12[0:1, 0:1], INVN, None, ALU.mult)
        dv.drain()
        dv.tensor_scalar(st12[0:1, 3:4], st12[0:1, 1:2], INVN, None, ALU.mult)
        dv.drain()
        dv.tensor_mul(st12[0:1, 4:5], st12[0:1, 2:3], st12[0:1, 2:3])
        dv.drain()
        inc(dv.tensor_tensor(st12[0:1, 5:6], st12[0:1, 3:4], st12[0:1, 4:5],
                             ALU.subtract), "sDV", ("var1",))
        W(dv, ("rsig",))
        dv.drain()
        dv.tensor_mul(st12[0:1, 7:8], st12[0:1, 2:3], st12[0:1, 6:7])
        dv.drain()
        inc(dv.tensor_scalar(st12[0:1, 7:8], st12[0:1, 7:8], -1.0, None,
                             ALU.mult), "sDV", ("ab",))
        W(dv, ("bc",))
        dv.drain()
        inc(dv.tensor_copy(stats[:, 2:4], BK[BANKOF[("bcx",)]][:, 0:2]),
            "sDV", ("bcev",))

        for tt in range(NTO):
            dv.drain()
            dv.tensor_copy(lgt[:], ARF[:, R_YSCR // 2 + tt * 8:
                                       R_YSCR // 2 + tt * 8 + 8])
            dv.drain()
            dv.reduce_max(gsc[:, 0:1], lgt[:], axis=AXX)
            dv.drain()
            dv.tensor_scalar(gsc[:, 1:2], gsc[:, 0:1], -1.0, None, ALU.mult)
            dv.drain()
            dv.tensor_scalar(gsc[:, 8:16], lgt[:], gsc[:, 0:1], None,
                             ALU.is_equal)
            dv.drain()
            dv.scalar_tensor_tensor(gsc[:, 16:24], gsc[:, 8:16], -1e30,
                                    lgt[:], ALU.mult, ALU.add)
            dv.drain()
            inc(dv.reduce_max(gsc[:, 2:3], gsc[:, 16:24], axis=AXX),
                "sDV", ("gm", tt))
            W(dv, ("ge", tt))
            dv.drain()
            dv.tensor_scalar(gsc[:, 4:5], gsc[:, 3:4], 1.0, None, ALU.add)
            dv.drain()
            dv.reciprocal(gsc[:, 5:6], gsc[:, 4:5])
            dv.drain()
            dv.tensor_scalar(gsc[:, 8:16], lgt[:], gsc[:, 2:3], None,
                             ALU.is_ge)
            dv.drain()
            dv.tensor_mul(gsc[:, 24:32], gsc[:, 24:32], gsc[:, 8:16])
            dv.drain()
            inc(dv.tensor_scalar(gts[:, tt * 8:(tt + 1) * 8],
                                 gsc[:, 24:32], gsc[:, 5:6], None, ALU.mult),
                "sDV", ("gate", tt))
            dv.drain()

        for e in range(NE):
            for q in range(4):
                Q = 4 * e + q
                dv.drain()
                if Q == 0:
                    W(dv, ("av", 15, 3, 31))
                    W(dv, ("exp", 15, 3, 31))
                    W(dv, ("outse", 15, 3))
                    W(dv, ("atr", 15, 3, 7))
                for ic in range(8):
                    for jc in range(2):
                        if (Q, ic, jc) in DEALT_MM:
                            continue
                        if q == 0 and ic == 0 and jc == 0 and e >= 2:
                            W(dv, ("yph", e - 2, 1, NTO - 1))
                        W(dv, ("mm", Q, ic, jc))
                        bkn = ("S0a", "S0b")[(Q * 16 + ic * 2 + jc) % 2]
                        msl = AR[:, R_M + (e % 2) * 8192 + ic * 1024
                                 + jc * 512:
                                 R_M + (e % 2) * 8192 + ic * 1024
                                 + (jc + 1) * 512]
                        if q == 0:
                            inst = dv.tensor_copy(msl, BK[bkn][:, 0:512])
                        else:
                            inst = dv.tensor_tensor(msl, BK[bkn][:, 0:512],
                                                    msl, ALU.add)
                        inc(inst, "sDV", ("mme", Q, ic, jc))
            dv.drain()
            for dc in range(2):
                for tt in range(NTO):
                    W(dv, ("yph", e, dc, tt))
                    bkn = _moe_b_bank(e, dc, tt)
                    ysl = yffn[:, tt * 1024 + dc * 512:
                               tt * 1024 + (dc + 1) * 512]
                    gcol = gts[:, tt * 8 + e: tt * 8 + e + 1]
                    if e == 0:
                        inst = dv.tensor_scalar(ysl, BK[bkn][:, 0:512], gcol,
                                                None, ALU.mult)
                    else:
                        inst = dv.scalar_tensor_tensor(
                            ysl, BK[bkn][:, 0:512], gcol, ysl,
                            ALU.mult, ALU.add)
                    inc(inst, "sDV", ("ybe", e, dc, tt))

        W(dv, ("yph", NE - 1, 1, NTO - 1))
        dv.drain()
        for tt in range(NTO):
            W(dv, ("ybe", NE - 1, 1, tt))
            idx = 64 + tt
            sn, v = x_load_sem[idx]
            dv.wait_ge(S[sn], v)
            if tt >= 2:
                dv.wait_ge(S["sOUT%d" % (tt % 2)], 16 * (tt // 2))
            sl = idx % 2
            ost = ARF[:, R_OSTG // 2 + (tt % 2) * 1024:
                      R_OSTG // 2 + ((tt % 2) + 1) * 1024]
            dv.drain()
            dv.tensor_scalar(ost, yat[:, tt * 1024:(tt + 1) * 1024],
                             stats[:, 2:3], stats[:, 3:4], ALU.mult, ALU.add)
            dv.drain()
            dv.tensor_add(ost, ost, yffn[:, tt * 1024:(tt + 1) * 1024])
            dv.drain()
            inc(dv.tensor_add(ost, ost, xstg[:, sl * D:(sl + 1) * D]),
                "sDV", ("fin", tt))

    def emit_sync():
        for tt in range(NTO):
            W(sy, ("fin", tt))
            inc(sy.dma_start(
                out=T["out"][tt * 128:(tt + 1) * 128, :],
                in_=ARF[:, R_OSTG // 2 + (tt % 2) * 1024:
                        R_OSTG // 2 + ((tt % 2) + 1) * 1024]),
                "sOUT%d" % (tt % 2), ("outst", tt), 16)
        sy.wait_ge(S["sOUT0"], 16 * (NTO // 2))
        sy.wait_ge(S["sOUT1"], 16 * (NTO // 2))

    emit_gpsimd()
    emit_act()
    emit_pe()
    emit_dve()
    emit_sync()


def _build():
    from contextlib import ExitStack
    nc = bass.Bass("TRN2", target_bir_lowering=False, debug=False)
    dram = dict(
        xb=nc.dram_tensor("xb", [TOK, D], F32, kind="ExternalInput").ap(),
        wqT=nc.dram_tensor("wqT", [D, D], F32, kind="ExternalInput").ap(),
        wkT=nc.dram_tensor("wkT", [D, D], F32, kind="ExternalInput").ap(),
        wvT=nc.dram_tensor("wvT", [D, D], F32, kind="ExternalInput").ap(),
        wgT=nc.dram_tensor("wgT", [D, NE], F32, kind="ExternalInput").ap(),
        winN=nc.dram_tensor("winN", [NE, HID, D], F32,
                            kind="ExternalInput").ap(),
        woutT=nc.dram_tensor("woutT", [NE, HID, D], F32,
                             kind="ExternalInput").ap(),
        idin=nc.dram_tensor("idin", [128, 128], F32,
                            kind="ExternalInput").ap(),
        out=nc.dram_tensor("out", [OWN, D], F32, kind="ExternalOutput").ap(),
    )

    with ExitStack() as _st:
        AR_t = _st.enter_context(nc.sbuf_tensor("AR", [128, AR_N], BF16))
        xnTb = _st.enter_context(nc.sbuf_tensor("xnTb", [128, 8 * 2048], BF16))
        yat = _st.enter_context(nc.sbuf_tensor("yat", [128, NTO * D], BF16))
        yffn = _st.enter_context(nc.sbuf_tensor("yffn", [128, NTO * D], BF16))
        xstg = _st.enter_context(nc.sbuf_tensor("xstg", [128, 2 * D], BF16))
        wgsb = _st.enter_context(nc.sbuf_tensor("wgsb", [128, 8 * NE], BF16))
        identsb = _st.enter_context(
            nc.sbuf_tensor("identsb", [128, 128], BF16))
        onescol = _st.enter_context(nc.sbuf_tensor("onescol", [128, 1], F32))
        onesrow = _st.enter_context(nc.sbuf_tensor("onesrow", [1, 128], F32))
        consts = _st.enter_context(nc.sbuf_tensor("consts", [128, 4], F32))
        stats = _st.enter_context(nc.sbuf_tensor("stats", [128, 8], F32))
        st12 = _st.enter_context(nc.sbuf_tensor("st12", [1, 8], F32))
        lnm = _st.enter_context(nc.sbuf_tensor("lnm", [128, NT], F32))
        lnr = _st.enter_context(nc.sbuf_tensor("lnr", [128, NT], F32))
        gts = _st.enter_context(nc.sbuf_tensor("gts", [128, NTO * NE], F32))
        lgt = _st.enter_context(nc.sbuf_tensor("lgt", [128, NE], F32))
        gsc = _st.enter_context(nc.sbuf_tensor("gsc", [128, 40], F32))
        P_A = _st.enter_context(nc.psum_tensor("P_A", [128, 512], F32))
        P_B = _st.enter_context(nc.psum_tensor("P_B", [128, 512], F32))
        P_S0 = _st.enter_context(nc.psum_tensor("P_S0", [128, 1024], F32))
        P_S1 = _st.enter_context(nc.psum_tensor("P_S1", [128, 1024], F32))
        P_OT = _st.enter_context(nc.psum_tensor("P_OT", [128, 1024], F32))
        semnames = ("sLX0", "sLX1", "sLW", "sID", "sWI0", "sWI1", "sWO0",
                    "sWO1", "sOUT0", "sOUT1", "sPE", "sDV", "sAC", "sGP")
        sems = {n: _st.enter_context(nc.semaphore(n)) for n in semnames}
        block = _st.enter_context(nc.Block())

        base = dict(dram)
        base.update(
            AR=AR_t.ap(), ARF=AR_t.bitcast(F32).ap(),
            xnTb=xnTb.ap(), yat=yat.ap(), yffn=yffn.ap(), xstg=xstg.ap(),
            wgsb=wgsb.ap(), identsb=identsb.ap(), onescol=onescol.ap(),
            onesrow=onesrow.ap(), consts=consts.ap(), stats=stats.ap(),
            st12=st12.ap(), lnm=lnm.ap(), lnr=lnr.ap(), gts=gts.ap(),
            lgt=lgt.ap(), gsc=gsc.ap(),
            pA=P_A.ap(), pB=P_B.ap(), pS0=P_S0.ap(), pS1=P_S1.ap(),
            pOT=P_OT.ap(),
            pa_bf=P_A.bitcast(BF16).ap(), pb_bf=P_B.bitcast(BF16).ap(),
            ps0_bf=P_S0.bitcast(BF16).ap(), ps1_bf=P_S1.bitcast(BF16).ap(),
            pot_bf=P_OT.bitcast(BF16).ap(),
            sems=sems,
        )

        dry = _Dry()
        MS = {}
        Trec = dict(base, g=dry, sy=dry, pe=dry, dv=dry, ac=dry)
        _emit(Trec, True, MS, {})

        def one(real_name, eng):
            Tx = dict(base, g=dry, sy=dry, pe=dry, dv=dry, ac=dry)
            Tx[real_name] = eng
            _emit(Tx, False, MS, {})

        @block.gpsimd
        def _(eng):
            one("g", eng)

        @block.sync
        def _(eng):
            one("sy", eng)

        @block.tensor
        def _(eng):
            one("pe", eng)

        @block.vector
        def _(eng):
            one("dv", eng)

        @block.scalar
        def _(eng):
            one("ac", eng)

    return nc


def kernel(L0_attn_x, L0_attn_Wq, L0_attn_Wk, L0_attn_Wv,
           L0_ffn_w_gate, L0_ffn_w_in, L0_ffn_w_out):
    x = np.ascontiguousarray(np.asarray(L0_attn_x, dtype=np.float32))
    xf = x.reshape(16384, D)
    wqT = np.ascontiguousarray(np.asarray(L0_attn_Wq, np.float32).T)
    wkT = np.ascontiguousarray(np.asarray(L0_attn_Wk, np.float32).T)
    wvT = np.ascontiguousarray(np.asarray(L0_attn_Wv, np.float32).T)
    wgT = np.ascontiguousarray(np.asarray(L0_ffn_w_gate, np.float32).T)
    winN = np.ascontiguousarray(np.asarray(L0_ffn_w_in, np.float32))
    woutT = np.ascontiguousarray(
        np.asarray(L0_ffn_w_out, np.float32).transpose(0, 2, 1))
    ident = np.eye(128, dtype=np.float32)

    nc = _build()
    in_maps = []
    for c in range(8):
        own = xf[c * 2048:(c + 1) * 2048]
        pc = c ^ 1
        par = xf[pc * 2048:(pc + 1) * 2048]
        xb = np.ascontiguousarray(np.concatenate([own, par], axis=0))
        in_maps.append({
            "xb": xb, "wqT": wqT, "wkT": wkT, "wvT": wvT, "wgT": wgT,
            "winN": winN, "woutT": woutT, "idin": ident,
        })
    res = run_bass_kernel_spmd(nc, in_maps, core_ids=list(range(8)))
    out = np.empty((16384, D), np.float32)
    for c in range(8):
        out[c * 2048:(c + 1) * 2048] = np.asarray(res.results[c]["out"],
                                                  np.float32)
    return out.reshape(4, 4096, D)


# version 11
# speedup vs baseline: 1.1059x; 1.0110x over previous
import numpy as np
import concourse.bass as bass
import concourse.mybir as mybir
from concourse.bass_utils import run_bass_kernel_spmd

F32 = mybir.dt.float32
BF16 = mybir.dt.bfloat16
ALU = mybir.AluOpType
AF = mybir.ActivationFunctionType
AXX = mybir.AxisListType.X

D = 1024
HID = 4096
NE = 8
TOK = 4096
OWN = 2048
NT = 32
NTO = 16
NP = 16
EXPS = 12.0
EPS = 1e-5
INVD = 1.0 / D
INVN = 1.0 / (4096.0 * 1024.0)

R_WQKV = 0
R_QT = (24576, 42016)
R_KT = (28672, 46112)
R_VB = (32768, 50208)
R_PT = 34848
R_OUTS = 36896
R_XNAT = 37920
R_YSCR = 39968
R_OSTG = 34848
R_WI = 0
R_WO = 16384
R_M = 32768
AR_N = 53248

BANKS4 = ("S0a", "S0b", "S1a", "S1b")
PROC = list(range(8, 16)) + list(range(8))
TILES = list(range(16, 32)) + list(range(16))


class _DryInst:
    def then_inc(self, sem, n=1):
        return self


_DRY_INST = _DryInst()


class _Dry:
    def __getattr__(self, name):
        def f(*a, **k):
            return _DRY_INST
        return f


def _g_decode(g):
    if g < 16:
        return 0, g
    if g < 32:
        return 1, g - 16
    return 2, g - 32


_GL = [("glog", tt) for tt in range(NTO)]
_MM = [("mmq", 0, 0, 0), ("mmq", 0, 0, 1), ("mmq", 0, 1, 0),
       ("mmq", 0, 1, 1), ("mmq", 1, 0, 0), ("mmq", 1, 0, 1),
       ("mmq", 1, 1, 0), ("mmq", 1, 1, 1)]
DL15 = (_GL[0:6] + [_MM[0], _GL[6], _GL[7], _MM[1], _GL[8], _GL[9],
                    _MM[2], _GL[10], _GL[11], _MM[3], _GL[12], _GL[13],
                    _MM[4], _GL[14], _GL[15]] + _MM[5:8])
DEALT_MM = {(0, 0, 0), (0, 0, 1), (0, 1, 0), (0, 1, 1),
            (1, 0, 0), (1, 0, 1), (1, 1, 0), (1, 1, 1)}

_DEAL_GS = [gs for gs in range(36) if gs % 3 != 2]


def _dl15_qc(qc):
    return [_DEAL_GS.index(gs) for gs in range(9 * qc, 9 * qc + 9)
            if gs % 3 != 2]


def _ab_schedule():
    seq = []
    for half in (0, 1):
        first = 8 * half
        for g in range(36):
            seq.append(("qkvg", first, g))
        for pj in range(first, first + 8):
            for qc in range(4):
                if pj < first + 7:
                    for g in range(9 * qc, 9 * qc + 9):
                        seq.append(("qkvg", pj + 1, g))
                elif pj == 15:
                    for i in _dl15_qc(qc):
                        seq.append(("dl", i))
                for jb in range(8):
                    seq.append(("atr", pj, qc, jb))
    seq.append(("sum2x",))
    seq.append(("bcx",))
    return {k: ("A" if i % 2 == 0 else "B") for i, k in enumerate(seq)}


BANKOF = _ab_schedule()


def _moe_a_bank(i, cb, hh, tc):
    return BANKS4[(i * 64 + cb * 8 + hh * 4 + tc) % 4]


def _moe_b_bank(i, dc, tt):
    return ("OTa", "OTb", "A", "B")[(i * 32 + dc * NTO + tt) % 4]


def _emit(T, rec, MS, cnt):
    AR = T["AR"]
    ARF = T["ARF"]
    xnTb = T["xnTb"]
    yat = T["yat"]
    yffn = T["yffn"]
    xstg = T["xstg"]
    wgsb = T["wgsb"]
    identsb = T["identsb"]
    onescol = T["onescol"]
    onesrow = T["onesrow"]
    consts = T["consts"]
    stats = T["stats"]
    st12 = T["st12"]
    lnm = T["lnm"]
    lnr = T["lnr"]
    gts = T["gts"]
    lgt = T["lgt"]
    gsc = T["gsc"]
    pS0, pS1, pOT = T["pS0"], T["pS1"], T["pOT"]
    S = T["sems"]
    g, sy, pe, dv, ac = T["g"], T["sy"], T["pe"], T["dv"], T["ac"]

    BK = {
        "A": T["pA"][:, 0:512], "B": T["pB"][:, 0:512],
        "S0a": pS0[:, 0:512], "S0b": pS0[:, 512:1024],
        "S1a": pS1[:, 0:512], "S1b": pS1[:, 512:1024],
        "OTa": pOT[:, 0:512], "OTb": pOT[:, 512:1024],
    }
    BKBF = {
        "A": T["pa_bf"][:, 0:1024], "B": T["pb_bf"][:, 0:1024],
        "S0a": T["ps0_bf"][:, 0:1024], "S0b": T["ps0_bf"][:, 1024:2048],
        "S1a": T["ps1_bf"][:, 0:1024], "S1b": T["ps1_bf"][:, 1024:2048],
        "OTa": T["pot_bf"][:, 0:1024], "OTb": T["pot_bf"][:, 1024:2048],
    }

    def inc(inst, semname, label=None, n=1):
        inst.then_inc(S[semname], n)
        cnt[semname] = cnt.get(semname, 0) + n
        if label is not None:
            if rec:
                MS[label] = (semname, cnt[semname])
            else:
                assert MS[label] == (semname, cnt[semname]), (
                    label, MS[label], (semname, cnt[semname]))

    def alias(dst_label, src_label):
        if rec:
            MS[dst_label] = MS[src_label]

    def W(eng, label):
        if rec and label not in MS:
            return
        sn, v = MS[label]
        eng.wait_ge(S[sn], v)

    OWN_ = {}

    def bacq(eng, name):
        lb = OWN_.get(name)
        if lb is not None:
            W(eng, lb)

    def brel(name, label):
        OWN_[name] = label

    NLOAD = 16 + 16 + 16 + 16 + NTO

    def x_row_tile(idx):
        if idx < 16:
            return TILES[idx]
        if idx < 32:
            pj, th = divmod(idx - 16, 2)
            return 2 * PROC[pj] + th
        if idx < 48:
            return TILES[16 + (idx - 32)]
        if idx < 64:
            pj, th = divmod(idx - 48, 2)
            return 2 * PROC[8 + pj] + th
        return idx - 64

    def x_consumers(idx):
        if idx < 16:
            return [("p0xn", idx), ("p0sq", idx)]
        if idx < 32:
            pj, th = divmod(idx - 16, 2)
            return [("xnat", pj, th)]
        if idx < 48:
            seq = 16 + (idx - 32)
            return [("p0xn", seq), ("p0sq", seq)]
        if idx < 64:
            pj, th = divmod(idx - 48, 2)
            return [("xnat", 8 + pj, th)]
        return [("fin", idx - 64)]

    x_load_sem = []
    lx = [0, 0]
    for i in range(NLOAD):
        sl = i % 2
        lx[sl] += 16
        x_load_sem.append(("sLX%d" % sl, lx[sl]))

    def emit_xload(idx):
        sl = idx % 2
        if idx >= 2:
            for lb in x_consumers(idx - 2):
                W(g, lb)
        tt = x_row_tile(idx)
        inc(g.dma_start(out=xstg[:, sl * D:(sl + 1) * D],
                        in_=T["xb"][tt * 128:(tt + 1) * 128, :]),
            "sLX%d" % sl, ("xld", idx), 16)

    def emit_gpsimd():
        g.memset(consts[:, 0:1], -EXPS)
        g.memset(consts[:, 1:2], EPS)
        g.memset(onescol[:], 1.0)
        g.memset(onesrow[:], 1.0)
        g.memset(stats[:, 0:2], 0.0)
        for par in range(2):
            for kb in range(32):
                off = R_VB[par] + kb * 65 + 64
                g.memset(AR[:, off: off + 1], 1.0)
        inc(g.memset(st12[:, 0:8], 0.0), "sGP", ("init",))

        inc(g.dma_start(out=identsb[:], in_=T["idin"]), "sID", ("idld",), 16)
        for m, w in enumerate((T["wqT"], T["wkT"], T["wvT"])):
            src = w.rearrange("(kt p) n -> p kt n", p=128)
            dst = AR[:, R_WQKV + m * 8192: R_WQKV + (m + 1) * 8192].rearrange(
                "p (kt n) -> p kt n", kt=8)
            for hvf in range(2):
                inc(g.dma_start(out=dst[:, hvf * 4:(hvf + 1) * 4, :],
                                in_=src[:, hvf * 4:(hvf + 1) * 4, :]),
                    "sLW", None, 16)
        inc(g.dma_start(
            out=wgsb[:].rearrange("p (kt n) -> p kt n", kt=8),
            in_=T["wgT"].rearrange("(kt p) n -> p kt n", p=128)),
            "sLW", ("wld",), 16)

        for idx in range(64):
            emit_xload(idx)

        def mq_load(Q):
            e, q = Q // 4, Q % 4
            slot = Q % 2
            for kk, (ten, base) in enumerate(((T["winN"], R_WI),
                                             (T["woutT"], R_WO))):
                srcq = ten[e, q * 1024:(q + 1) * 1024, :]
                srcq = srcq.rearrange("(kt p) n -> p kt n", p=128)
                dst = AR[:, base + slot * 8192: base + (slot + 1) * 8192]
                dst = dst.rearrange("p (kt n) -> p kt n", kt=8)
                for hvf in range(2):
                    inc(g.dma_start(out=dst[:, hvf * 4:(hvf + 1) * 4, :],
                                    in_=srcq[:, hvf * 4:(hvf + 1) * 4, :]),
                        "sWI%d" % slot, ("mqld", Q, 2 * kk + hvf), 16)

        W(g, ("qkv", 15, 35))
        W(g, ("st", 14, 3, 31))
        mq_load(0)
        mq_load(1)
        W(g, ("atrlast",))
        W(g, ("statslast",))
        W(g, ("explast",))
        for Q in range(2, 32):
            W(g, ("mmlast", Q - 2))
            mq_load(Q)
        for idx in range(64, 64 + NTO):
            emit_xload(idx)

    def act_phase0(seq0):
        for s_ in range(16):
            seq = seq0 + s_
            idx = seq if seq < 16 else 32 + (seq - 16)
            sl = idx % 2
            W(ac, ("xld", idx))
            if seq >= 1:
                W(ac, ("p0var", seq - 1))
            ac.drain()
            inst = ac.activation(pOT[:, 0:1024], xstg[:, sl * D:(sl + 1) * D],
                                 AF.Square, accum_out=gsc[:, 0:1])
            inc(inst, "sAC", ("p0sq", seq))
            W(ac, ("p0var", seq))
            ac.drain()
            ac.activation(gsc[:, 5:6], gsc[:, 4:5], AF.Ln, bias=consts[:, 1:2])
            ac.drain()
            tt = TILES[seq]
            inc(ac.activation(lnr[:, tt:tt + 1], gsc[:, 5:6], AF.Exp,
                              scale=-0.5), "sAC", ("p0rst", seq))

    def emit_act():
        ac.wait_ge(S["sGP"], 1)
        for half in range(2):
            act_phase0(16 * half)
            for pj in range(8 * half, 8 * half + 8):
                for qc in range(4):
                    for k in range(32):
                        gk = (pj * 4 + qc) * 32 + k
                        W(ac, ("st", pj, qc, k))
                        if gk >= 2:
                            g2 = gk - 2
                            W(ac, ("av", g2 // 128, (g2 // 32) % 4, g2 % 32))
                        sl = k % 2
                        stb = pS0 if sl == 0 else pS1
                        inst = ac.activation(
                            AR[:, R_PT + sl * 1024: R_PT + (sl + 1) * 1024],
                            stb[:, 0:1024], AF.Exp, bias=consts[:, 0:1])
                        inc(inst, "sAC", ("exp", pj, qc, k))
                    W(ac, ("av", pj, qc, 31))
                    if qc == 0 and pj > 0:
                        W(ac, ("statsacc", pj - 1, 1))
                    if qc > 0:
                        W(ac, ("atr", pj, qc - 1, 7))
                    ac.drain()
                    inc(ac.copy(AR[0:65, R_OUTS: R_OUTS + 1024],
                                pOT[0:65, 0:1024]), "sAC", ("outse", pj, qc))
        alias(("explast",), ("exp", NP - 1, 3, 31))

        W(ac, ("var1",))
        ac.drain()
        ac.activation(st12[0:1, 4:5], st12[0:1, 5:6], AF.Ln,
                      bias=consts[0:1, 1:2])
        ac.drain()
        inc(ac.activation(st12[0:1, 6:7], st12[0:1, 4:5], AF.Exp,
                          scale=-0.5), "sAC", ("rsig",))

        for tt in range(NTO):
            W(ac, ("gm", tt))
            ac.drain()
            ac.activation(gsc[:, 24:32], lgt[:], AF.Exp, bias=gsc[:, 1:2])
            ac.drain()
            inc(ac.activation(gsc[:, 3:4], gsc[:, 2:3], AF.Exp,
                              bias=gsc[:, 1:2]), "sAC", ("ge", tt))

    def emit_pe():
        def pe_qkv_group(pj, gidx):
            j = PROC[pj]
            toff = (256 * j) % 2048
            par = pj % 2
            m, sub = _g_decode(gidx)
            bk = BANKOF[("qkvg", pj, gidx)]
            bacq(pe, bk)
            last = None
            if m < 2:
                for kt in range(8):
                    lhs = AR[:, R_WQKV + m * 8192 + kt * 1024 + sub * 64:
                             R_WQKV + m * 8192 + kt * 1024 + sub * 64 + 65]
                    rhs = xnTb[:, kt * 2048 + toff: kt * 2048 + toff + 256]
                    last = pe.matmul(BK[bk][0:65, 0:256], lhs, rhs,
                                     start=(kt == 0), stop=(kt == 7))
            else:
                th, dc = sub // 2, sub % 2
                for kt in range(8):
                    lhs = xnTb[:, kt * 2048 + toff + th * 128:
                               kt * 2048 + toff + (th + 1) * 128]
                    rhs = AR[:, R_WQKV + 16384 + kt * 1024 + dc * 512:
                             R_WQKV + 16384 + kt * 1024 + (dc + 1) * 512]
                    last = pe.matmul(BK[bk][:, 0:512], lhs, rhs,
                                     start=(kt == 0), stop=(kt == 7))
            inc(last, "sPE", ("qkv", pj, gidx))
            brel(bk, ("qkve", pj, gidx))

        def av_step(pj, qc, k):
            sl = k % 2
            par = pj % 2
            W(pe, ("exp", pj, qc, k))
            if k == 0:
                bacq(pe, "OTa")
                bacq(pe, "OTb")
            last = None
            for hq in range(2):
                lhs = AR[:, R_VB[par] + k * 65: R_VB[par] + k * 65 + 65]
                rhs = AR[:, R_PT + sl * 1024 + hq * 512:
                         R_PT + sl * 1024 + (hq + 1) * 512]
                last = pe.matmul(pOT[0:65, hq * 512:(hq + 1) * 512], lhs, rhs,
                                 start=(k == 0), stop=(k == 31))
            inc(last, "sPE", ("av", pj, qc, k))
            if k == 31:
                brel("OTa", ("outse", pj, qc))
                brel("OTb", ("outse", pj, qc))

        def pe_deal15(di):
            item = DL15[di]
            bk = BANKOF[("dl", di)]
            if item[0] == "glog":
                tt = item[1]
                if di == 0:
                    W(pe, ("wld",))
                bacq(pe, bk)
                last = None
                for kt in range(8):
                    lhs = xnTb[:, kt * 2048 + tt * 128:
                               kt * 2048 + (tt + 1) * 128]
                    rhs = wgsb[:, kt * 8:(kt + 1) * 8]
                    last = pe.matmul(BK[bk][:, 0:8], lhs, rhs,
                                     start=(kt == 0), stop=(kt == 7))
                inc(last, "sPE", ("glog", tt))
                brel(bk, ("gloge", tt))
            else:
                _, Q, ic, jc = item
                slot = Q % 2
                if ic == 0 and jc == 0:
                    W(pe, ("mqld", Q, 3))
                bacq(pe, bk)
                last = None
                for kt in range(8):
                    lhs = AR[:, R_WI + slot * 8192 + kt * 1024 + ic * 128:
                             R_WI + slot * 8192 + kt * 1024 + (ic + 1) * 128]
                    rhs = AR[:, R_WO + slot * 8192 + kt * 1024 + jc * 512:
                             R_WO + slot * 8192 + kt * 1024 + (jc + 1) * 512]
                    last = pe.matmul(BK[bk][:, 0:512], lhs, rhs,
                                     start=(kt == 0), stop=(kt == 7))
                inc(last, "sPE", ("mm", Q, ic, jc))
                brel(bk, ("mme", Q, ic, jc))

        def pe_phase0_slice(seq0, s_):
            seq = seq0 + s_
            idx = seq if seq < 16 else 32 + (seq - 16)
            sl = idx % 2
            W(pe, ("p0xn", seq))
            for jb in range(8):
                gtc = seq * 8 + jb
                bk = BANKS4[gtc % 4]
                bacq(pe, bk)
                src = AR[:, R_XNAT + sl * 1024 + jb * 128:
                         R_XNAT + sl * 1024 + (jb + 1) * 128]
                inc(pe.transpose(BKBF[bk][:, 0:128], src, identsb[:]),
                    "sPE", ("p0tr", gtc))
                brel(bk, ("p0ev", gtc))

        pe.wait_ge(S["sGP"], 1)
        W(pe, ("idld",))
        for half in range(2):
            first = 8 * half
            for s_ in range(16):
                pe_phase0_slice(16 * half, s_)
                if s_ >= 1:
                    if s_ == 1:
                        if half == 0:
                            W(pe, ("wld",))
                        W(pe, ("p0ev", (16 * half + 1) * 8 + 7))
                    for gidx in range(3 * (s_ - 1), min(36, 3 * s_)):
                        pe_qkv_group(first, gidx)
            W(pe, ("p0ev", 16 * half * 8 + 127))
            brel("OTa", ("p0sq", 16 * half + 15))
            brel("OTb", ("p0sq", 16 * half + 15))
            for pj in range(first, first + 8):
                W(pe, ("qkve", pj, 35))
                bg = ([(pj + 1, gg) for gg in range(36)]
                      if pj < first + 7 else [])
                bgi = 0
                for qc in range(4):
                    dealt = 0
                    for k in range(32):
                        sl = k % 2
                        sb = ("S0a", "S0b") if sl == 0 else ("S1a", "S1b")
                        bacq(pe, sb[0])
                        bacq(pe, sb[1])
                        stb = pS0 if sl == 0 else pS1
                        par = pj % 2
                        last = None
                        for hq in range(2):
                            lhs = AR[0:64, R_KT[par] + (k // 2) * 256
                                     + (k % 2) * 128:
                                     R_KT[par] + (k // 2) * 256
                                     + (k % 2) * 128 + 128]
                            rhs = AR[0:64, R_QT[par] + qc * 1024 + hq * 512:
                                     R_QT[par] + qc * 1024 + (hq + 1) * 512]
                            last = pe.matmul(
                                stb[0:128, hq * 512:(hq + 1) * 512], lhs, rhs,
                                start=True, stop=True)
                        inc(last, "sPE", ("st", pj, qc, k))
                        brel(sb[0], ("exp", pj, qc, k))
                        brel(sb[1], ("exp", pj, qc, k))
                        if dealt < 9 and bgi < len(bg) and k % 3 == 1:
                            pe_qkv_group(*bg[bgi])
                            bgi += 1
                            dealt += 1
                        elif pj == 15 and dealt < 9 and k % 3 == 1:
                            gs = 9 * qc + dealt
                            dealt += 1
                            if gs % 3 != 2:
                                pe_deal15(_DEAL_GS.index(gs))
                        if k >= 1:
                            av_step(pj, qc, k - 1)
                    av_step(pj, qc, 31)
                    for jb in range(8):
                        bk = BANKOF[("atr", pj, qc, jb)]
                        bacq(pe, bk)
                        if jb == 0:
                            W(pe, ("outse", pj, qc))
                        src = AR[0:65, R_OUTS + jb * 128:
                                 R_OUTS + (jb + 1) * 128]
                        inc(pe.transpose(BKBF[bk][:, 0:65], src,
                                         identsb[0:65, 0:65]),
                            "sPE", ("atr", pj, qc, jb))
                        brel(bk, ("ntt", pj, qc, jb))
        alias(("atrlast",), ("atr", NP - 1, 3, 7))

        W(pe, ("statsacc", NP - 1, 1))
        bk = BANKOF[("sum2x",)]
        bacq(pe, bk)
        inc(pe.matmul(BK[bk][0:1, 0:2], onescol[:], stats[:, 0:2],
                      start=True, stop=True), "sPE", ("sum2",))
        brel(bk, ("sum2e",))
        W(pe, ("ab",))
        bk = BANKOF[("bcx",)]
        bacq(pe, bk)
        inc(pe.matmul(BK[bk][0:128, 0:2], onesrow[:], st12[0:1, 6:8],
                      start=True, stop=True), "sPE", ("bc",))
        brel(bk, ("bcev",))

        for e in range(NE):
            for q in range(4):
                Q = 4 * e + q
                W(pe, ("mqld", Q, 3))
                slot = Q % 2
                for ic in range(8):
                    for jc in range(2):
                        if (Q, ic, jc) in DEALT_MM:
                            continue
                        bk = ("S0a", "S0b")[(Q * 16 + ic * 2 + jc) % 2]
                        bacq(pe, bk)
                        last = None
                        for kt in range(8):
                            lhs = AR[:, R_WI + slot * 8192 + kt * 1024
                                     + ic * 128:
                                     R_WI + slot * 8192 + kt * 1024
                                     + (ic + 1) * 128]
                            rhs = AR[:, R_WO + slot * 8192 + kt * 1024
                                     + jc * 512:
                                     R_WO + slot * 8192 + kt * 1024
                                     + (jc + 1) * 512]
                            last = pe.matmul(BK[bk][:, 0:512], lhs, rhs,
                                             start=(kt == 0), stop=(kt == 7))
                        inc(last, "sPE", ("mm", Q, ic, jc))
                        brel(bk, ("mme", Q, ic, jc))
                alias(("mmlast", Q), ("mm", Q, 7, 1))
            W(pe, ("mme", 4 * e + 3, 7, 1))
            for dc in range(2):
                for tt in range(NTO):
                    bk = _moe_b_bank(e, dc, tt)
                    bacq(pe, bk)
                    last = None
                    for kt in range(8):
                        lhs = xnTb[:, kt * 2048 + tt * 128:
                                   kt * 2048 + (tt + 1) * 128]
                        rhs = AR[:, R_M + (e % 2) * 8192 + kt * 1024
                                 + dc * 512:
                                 R_M + (e % 2) * 8192 + kt * 1024
                                 + (dc + 1) * 512]
                        last = pe.matmul(BK[bk][:, 0:512], lhs, rhs,
                                         start=(kt == 0), stop=(kt == 7))
                    inc(last, "sPE", ("yph", e, dc, tt))
                    brel(bk, ("ybe", e, dc, tt))

    def emit_dve():
        def dve_qkv_evac(pj, gidx):
            par = pj % 2
            m, sub = _g_decode(gidx)
            bk = BANKOF[("qkvg", pj, gidx)]
            if gidx == 0 and pj >= 2:
                W(dv, ("st", pj - 2, 3, 31))
                W(dv, ("av", pj - 2, 3, 31))
            W(dv, ("qkv", pj, gidx))
            if m < 2:
                base = R_QT[par] if m == 0 else R_KT[par]
                inc(dv.tensor_copy(
                    AR[0:64, base + sub * 256: base + (sub + 1) * 256],
                    BK[bk][0:64, 0:256]), "sDV", ("qkve", pj, gidx))
            else:
                th, dc = sub // 2, sub % 2
                last = None
                for gg in range(8):
                    kb = 2 * (dc * 8 + gg) + th
                    off = R_VB[par] + kb * 65
                    last = dv.tensor_copy(AR[:, off: off + 64],
                                          BK[bk][:, gg * 64:(gg + 1) * 64])
                inc(last, "sDV", ("qkve", pj, gidx))

        def dve_deal15(i):
            item = DL15[i]
            bk = BANKOF[("dl", i)]
            if item[0] == "glog":
                tt = item[1]
                W(dv, ("glog", tt))
                dv.drain()
                inc(dv.tensor_copy(
                    ARF[:, R_YSCR // 2 + tt * 8: R_YSCR // 2 + tt * 8 + 8],
                    BK[bk][:, 0:8]), "sDV", ("gloge", tt))
            else:
                _, Q, ic, jc = item
                if (Q, ic, jc) == (0, 0, 0):
                    W(dv, ("av", 14, 3, 31))
                W(dv, ("mm", Q, ic, jc))
                msl = AR[:, R_M + ic * 1024 + jc * 512:
                         R_M + ic * 1024 + (jc + 1) * 512]
                dv.drain()
                if Q == 0:
                    inst = dv.tensor_copy(msl, BK[bk][:, 0:512])
                else:
                    inst = dv.tensor_tensor(msl, BK[bk][:, 0:512], msl,
                                            ALU.add)
                inc(inst, "sDV", ("mme", Q, ic, jc))

        def dve_phase0_slice(seq0, s_):
                seq = seq0 + s_
                idx = seq if seq < 16 else 32 + (seq - 16)
                sl = idx % 2
                tt = TILES[seq]
                W(dv, ("xld", idx))
                if seq >= 2:
                    W(dv, ("p0tr", (seq - 2) * 8 + 7))
                dv.drain()
                dv.reduce_sum(gsc[:, 1:2], xstg[:, sl * D:(sl + 1) * D],
                              axis=AXX)
                dv.drain()
                dv.tensor_scalar(lnm[:, tt:tt + 1], gsc[:, 1:2], -INVD, None,
                                 ALU.mult)
                W(dv, ("p0sq", seq))
                dv.tensor_scalar(gsc[:, 2:3], gsc[:, 0:1], INVD, None,
                                 ALU.mult)
                dv.drain()
                dv.tensor_mul(gsc[:, 3:4], lnm[:, tt:tt + 1],
                              lnm[:, tt:tt + 1])
                dv.drain()
                inc(dv.tensor_tensor(gsc[:, 4:5], gsc[:, 2:3], gsc[:, 3:4],
                                     ALU.subtract), "sDV", ("p0var", seq))
                W(dv, ("p0rst", seq))
                inc(dv.tensor_scalar(
                    AR[:, R_XNAT + sl * 1024: R_XNAT + (sl + 1) * 1024],
                    xstg[:, sl * D:(sl + 1) * D],
                    lnm[:, tt:tt + 1], lnr[:, tt:tt + 1], ALU.add, ALU.mult),
                    "sDV", ("p0xn", seq))
                for jb in range(8):
                    gtc = seq * 8 + jb
                    bk = BANKS4[gtc % 4]
                    W(dv, ("p0tr", gtc))
                    dcol = jb * 2048 + (tt % 16) * 128
                    inc(dv.tensor_copy(xnTb[:, dcol: dcol + 128],
                                       BKBF[bk][:, 0:128]),
                        "sDV", ("p0ev", gtc))

        dv.wait_ge(S["sGP"], 1)
        for half in range(2):
            first = 8 * half
            for s_ in range(16):
                dve_phase0_slice(16 * half, s_)
                if s_ >= 1:
                    for gidx in range(3 * (s_ - 1), min(36, 3 * s_)):
                        dve_qkv_evac(first, gidx)
            for pj in range(first, first + 8):
                j = PROC[pj]
                dv.drain()
                for th in range(2):
                    idx = (16 if half == 0 else 48) + 2 * (pj - first) + th
                    sn, v = x_load_sem[idx]
                    dv.wait_ge(S[sn], v)
                    tt = 2 * j + th
                    slx = idx % 2
                    inc(dv.tensor_scalar(
                        AR[:, R_XNAT + th * 1024: R_XNAT + (th + 1) * 1024],
                        xstg[:, slx * D:(slx + 1) * D],
                        lnm[:, tt:tt + 1], lnr[:, tt:tt + 1],
                        ALU.add, ALU.mult), "sDV", ("xnat", pj, th))
                for qc in range(4):
                    if pj < first + 7:
                        for gg in range(9 * qc, 9 * qc + 9):
                            dve_qkv_evac(pj + 1, gg)
                    elif pj == 15:
                        for i in _dl15_qc(qc):
                            dve_deal15(i)
                    for jb in range(8):
                        W(dv, ("atr", pj, qc, jb))
                        bkn = BANKOF[("atr", pj, qc, jb)]
                        gq = qc * 4 + jb // 2
                        hb = jb % 2
                        dv.drain()
                        dv.reciprocal(gsc[:, 6:7], BKBF[bkn][:, 64:65])
                        dv.drain()
                        if j < 8:
                            ydst = yat[:, (2 * j + hb) * 1024 + gq * 64:
                                       (2 * j + hb) * 1024 + gq * 64 + 64]
                        else:
                            ydst = AR[:, R_YSCR + hb * 1024 + gq * 64:
                                      R_YSCR + hb * 1024 + gq * 64 + 64]
                        xn_in = AR[:, R_XNAT + hb * 1024 + gq * 64:
                                   R_XNAT + hb * 1024 + gq * 64 + 64]
                        inc(dv.scalar_tensor_tensor(
                            ydst, BKBF[bkn][:, 0:64], gsc[:, 6:7], xn_in,
                            ALU.mult, ALU.add), "sDV", ("ntt", pj, qc, jb))
                dv.drain()
                for th in range(2):
                    if j < 8:
                        src = yat[:, (2 * j + th) * 1024:
                                  (2 * j + th + 1) * 1024]
                    else:
                        src = AR[:, R_YSCR + th * 1024:
                                 R_YSCR + (th + 1) * 1024]
                    dv.reduce_sum(gsc[:, 7:8], src, axis=AXX)
                    dv.drain()
                    dv.tensor_add(stats[:, 0:1], stats[:, 0:1], gsc[:, 7:8])
                    dv.drain()
                    dv.tensor_mul(AR[:, R_OUTS: R_OUTS + 1024], src, src)
                    dv.drain()
                    dv.reduce_sum(gsc[:, 7:8], AR[:, R_OUTS: R_OUTS + 1024],
                                  axis=AXX)
                    dv.drain()
                    inc(dv.tensor_add(stats[:, 1:2], stats[:, 1:2],
                                      gsc[:, 7:8]),
                        "sDV", ("statsacc", pj, th))
                    dv.drain()
        alias(("statslast",), ("statsacc", NP - 1, 1))

        W(dv, ("sum2",))
        dv.drain()
        inc(dv.tensor_copy(st12[0:1, 0:2], BK[BANKOF[("sum2x",)]][0:1, 0:2]),
            "sDV", ("sum2e",))
        dv.drain()
        dv.tensor_scalar(st12[0:1, 2:3], st12[0:1, 0:1], INVN, None, ALU.mult)
        dv.drain()
        dv.tensor_scalar(st12[0:1, 3:4], st12[0:1, 1:2], INVN, None, ALU.mult)
        dv.drain()
        dv.tensor_mul(st12[0:1, 4:5], st12[0:1, 2:3], st12[0:1, 2:3])
        dv.drain()
        inc(dv.tensor_tensor(st12[0:1, 5:6], st12[0:1, 3:4], st12[0:1, 4:5],
                             ALU.subtract), "sDV", ("var1",))
        W(dv, ("rsig",))
        dv.drain()
        dv.tensor_mul(st12[0:1, 7:8], st12[0:1, 2:3], st12[0:1, 6:7])
        dv.drain()
        inc(dv.tensor_scalar(st12[0:1, 7:8], st12[0:1, 7:8], -1.0, None,
                             ALU.mult), "sDV", ("ab",))
        W(dv, ("bc",))
        dv.drain()
        inc(dv.tensor_copy(stats[:, 2:4], BK[BANKOF[("bcx",)]][:, 0:2]),
            "sDV", ("bcev",))

        for tt in range(NTO):
            dv.drain()
            dv.tensor_copy(lgt[:], ARF[:, R_YSCR // 2 + tt * 8:
                                       R_YSCR // 2 + tt * 8 + 8])
            dv.drain()
            dv.reduce_max(gsc[:, 0:1], lgt[:], axis=AXX)
            dv.drain()
            dv.tensor_scalar(gsc[:, 1:2], gsc[:, 0:1], -1.0, None, ALU.mult)
            dv.drain()
            dv.tensor_scalar(gsc[:, 8:16], lgt[:], gsc[:, 0:1], None,
                             ALU.is_equal)
            dv.drain()
            dv.scalar_tensor_tensor(gsc[:, 16:24], gsc[:, 8:16], -1e30,
                                    lgt[:], ALU.mult, ALU.add)
            dv.drain()
            inc(dv.reduce_max(gsc[:, 2:3], gsc[:, 16:24], axis=AXX),
                "sDV", ("gm", tt))
            W(dv, ("ge", tt))
            dv.drain()
            dv.tensor_scalar(gsc[:, 4:5], gsc[:, 3:4], 1.0, None, ALU.add)
            dv.drain()
            dv.reciprocal(gsc[:, 5:6], gsc[:, 4:5])
            dv.drain()
            dv.tensor_scalar(gsc[:, 8:16], lgt[:], gsc[:, 2:3], None,
                             ALU.is_ge)
            dv.drain()
            dv.tensor_mul(gsc[:, 24:32], gsc[:, 24:32], gsc[:, 8:16])
            dv.drain()
            inc(dv.tensor_scalar(gts[:, tt * 8:(tt + 1) * 8],
                                 gsc[:, 24:32], gsc[:, 5:6], None, ALU.mult),
                "sDV", ("gate", tt))
            dv.drain()

        for e in range(NE):
            for q in range(4):
                Q = 4 * e + q
                dv.drain()
                if Q == 0:
                    W(dv, ("av", 15, 3, 31))
                    W(dv, ("exp", 15, 3, 31))
                    W(dv, ("outse", 15, 3))
                    W(dv, ("atr", 15, 3, 7))
                for ic in range(8):
                    for jc in range(2):
                        if (Q, ic, jc) in DEALT_MM:
                            continue
                        if q == 0 and ic == 0 and jc == 0 and e >= 2:
                            W(dv, ("yph", e - 2, 1, NTO - 1))
                        W(dv, ("mm", Q, ic, jc))
                        bkn = ("S0a", "S0b")[(Q * 16 + ic * 2 + jc) % 2]
                        msl = AR[:, R_M + (e % 2) * 8192 + ic * 1024
                                 + jc * 512:
                                 R_M + (e % 2) * 8192 + ic * 1024
                                 + (jc + 1) * 512]
                        if q == 0:
                            inst = dv.tensor_copy(msl, BK[bkn][:, 0:512])
                        else:
                            inst = dv.tensor_tensor(msl, BK[bkn][:, 0:512],
                                                    msl, ALU.add)
                        inc(inst, "sDV", ("mme", Q, ic, jc))
            dv.drain()
            for dc in range(2):
                for tt in range(NTO):
                    W(dv, ("yph", e, dc, tt))
                    bkn = _moe_b_bank(e, dc, tt)
                    ysl = yffn[:, tt * 1024 + dc * 512:
                               tt * 1024 + (dc + 1) * 512]
                    gcol = gts[:, tt * 8 + e: tt * 8 + e + 1]
                    if e == 0:
                        inst = dv.tensor_scalar(ysl, BK[bkn][:, 0:512], gcol,
                                                None, ALU.mult)
                    else:
                        inst = dv.scalar_tensor_tensor(
                            ysl, BK[bkn][:, 0:512], gcol, ysl,
                            ALU.mult, ALU.add)
                    inc(inst, "sDV", ("ybe", e, dc, tt))

        W(dv, ("yph", NE - 1, 1, NTO - 1))
        dv.drain()
        for tt in range(NTO):
            W(dv, ("ybe", NE - 1, 1, tt))
            idx = 64 + tt
            sn, v = x_load_sem[idx]
            dv.wait_ge(S[sn], v)
            if tt >= 2:
                dv.wait_ge(S["sOUT%d" % (tt % 2)], 16 * (tt // 2))
            sl = idx % 2
            ost = ARF[:, R_OSTG // 2 + (tt % 2) * 1024:
                      R_OSTG // 2 + ((tt % 2) + 1) * 1024]
            dv.drain()
            dv.tensor_scalar(ost, yat[:, tt * 1024:(tt + 1) * 1024],
                             stats[:, 2:3], stats[:, 3:4], ALU.mult, ALU.add)
            dv.drain()
            dv.tensor_add(ost, ost, yffn[:, tt * 1024:(tt + 1) * 1024])
            dv.drain()
            inc(dv.tensor_add(ost, ost, xstg[:, sl * D:(sl + 1) * D]),
                "sDV", ("fin", tt))

    def emit_sync():
        for tt in range(NTO):
            W(sy, ("fin", tt))
            inc(sy.dma_start(
                out=T["out"][tt * 128:(tt + 1) * 128, :],
                in_=ARF[:, R_OSTG // 2 + (tt % 2) * 1024:
                        R_OSTG // 2 + ((tt % 2) + 1) * 1024]),
                "sOUT%d" % (tt % 2), ("outst", tt), 16)
        sy.wait_ge(S["sOUT0"], 16 * (NTO // 2))
        sy.wait_ge(S["sOUT1"], 16 * (NTO // 2))

    emit_gpsimd()
    emit_act()
    emit_pe()
    emit_dve()
    emit_sync()


def _build():
    from contextlib import ExitStack
    nc = bass.Bass("TRN2", target_bir_lowering=False, debug=False)
    dram = dict(
        xb=nc.dram_tensor("xb", [TOK, D], F32, kind="ExternalInput").ap(),
        wqT=nc.dram_tensor("wqT", [D, D], F32, kind="ExternalInput").ap(),
        wkT=nc.dram_tensor("wkT", [D, D], F32, kind="ExternalInput").ap(),
        wvT=nc.dram_tensor("wvT", [D, D], F32, kind="ExternalInput").ap(),
        wgT=nc.dram_tensor("wgT", [D, NE], F32, kind="ExternalInput").ap(),
        winN=nc.dram_tensor("winN", [NE, HID, D], F32,
                            kind="ExternalInput").ap(),
        woutT=nc.dram_tensor("woutT", [NE, HID, D], F32,
                             kind="ExternalInput").ap(),
        idin=nc.dram_tensor("idin", [128, 128], F32,
                            kind="ExternalInput").ap(),
        out=nc.dram_tensor("out", [OWN, D], F32, kind="ExternalOutput").ap(),
    )

    with ExitStack() as _st:
        AR_t = _st.enter_context(nc.sbuf_tensor("AR", [128, AR_N], BF16))
        xnTb = _st.enter_context(nc.sbuf_tensor("xnTb", [128, 8 * 2048], BF16))
        yat = _st.enter_context(nc.sbuf_tensor("yat", [128, NTO * D], BF16))
        yffn = _st.enter_context(nc.sbuf_tensor("yffn", [128, NTO * D], BF16))
        xstg = _st.enter_context(nc.sbuf_tensor("xstg", [128, 2 * D], BF16))
        wgsb = _st.enter_context(nc.sbuf_tensor("wgsb", [128, 8 * NE], BF16))
        identsb = _st.enter_context(
            nc.sbuf_tensor("identsb", [128, 128], BF16))
        onescol = _st.enter_context(nc.sbuf_tensor("onescol", [128, 1], F32))
        onesrow = _st.enter_context(nc.sbuf_tensor("onesrow", [1, 128], F32))
        consts = _st.enter_context(nc.sbuf_tensor("consts", [128, 4], F32))
        stats = _st.enter_context(nc.sbuf_tensor("stats", [128, 8], F32))
        st12 = _st.enter_context(nc.sbuf_tensor("st12", [1, 8], F32))
        lnm = _st.enter_context(nc.sbuf_tensor("lnm", [128, NT], F32))
        lnr = _st.enter_context(nc.sbuf_tensor("lnr", [128, NT], F32))
        gts = _st.enter_context(nc.sbuf_tensor("gts", [128, NTO * NE], F32))
        lgt = _st.enter_context(nc.sbuf_tensor("lgt", [128, NE], F32))
        gsc = _st.enter_context(nc.sbuf_tensor("gsc", [128, 40], F32))
        P_A = _st.enter_context(nc.psum_tensor("P_A", [128, 512], F32))
        P_B = _st.enter_context(nc.psum_tensor("P_B", [128, 512], F32))
        P_S0 = _st.enter_context(nc.psum_tensor("P_S0", [128, 1024], F32))
        P_S1 = _st.enter_context(nc.psum_tensor("P_S1", [128, 1024], F32))
        P_OT = _st.enter_context(nc.psum_tensor("P_OT", [128, 1024], F32))
        semnames = ("sLX0", "sLX1", "sLW", "sID", "sWI0", "sWI1", "sWO0",
                    "sWO1", "sOUT0", "sOUT1", "sPE", "sDV", "sAC", "sGP")
        sems = {n: _st.enter_context(nc.semaphore(n)) for n in semnames}
        block = _st.enter_context(nc.Block())

        base = dict(dram)
        base.update(
            AR=AR_t.ap(), ARF=AR_t.bitcast(F32).ap(),
            xnTb=xnTb.ap(), yat=yat.ap(), yffn=yffn.ap(), xstg=xstg.ap(),
            wgsb=wgsb.ap(), identsb=identsb.ap(), onescol=onescol.ap(),
            onesrow=onesrow.ap(), consts=consts.ap(), stats=stats.ap(),
            st12=st12.ap(), lnm=lnm.ap(), lnr=lnr.ap(), gts=gts.ap(),
            lgt=lgt.ap(), gsc=gsc.ap(),
            pA=P_A.ap(), pB=P_B.ap(), pS0=P_S0.ap(), pS1=P_S1.ap(),
            pOT=P_OT.ap(),
            pa_bf=P_A.bitcast(BF16).ap(), pb_bf=P_B.bitcast(BF16).ap(),
            ps0_bf=P_S0.bitcast(BF16).ap(), ps1_bf=P_S1.bitcast(BF16).ap(),
            pot_bf=P_OT.bitcast(BF16).ap(),
            sems=sems,
        )

        dry = _Dry()
        MS = {}
        Trec = dict(base, g=dry, sy=dry, pe=dry, dv=dry, ac=dry)
        _emit(Trec, True, MS, {})

        def one(real_name, eng):
            Tx = dict(base, g=dry, sy=dry, pe=dry, dv=dry, ac=dry)
            Tx[real_name] = eng
            _emit(Tx, False, MS, {})

        @block.gpsimd
        def _(eng):
            one("g", eng)

        @block.sync
        def _(eng):
            one("sy", eng)

        @block.tensor
        def _(eng):
            one("pe", eng)

        @block.vector
        def _(eng):
            one("dv", eng)

        @block.scalar
        def _(eng):
            one("ac", eng)

    return nc


def kernel(L0_attn_x, L0_attn_Wq, L0_attn_Wk, L0_attn_Wv,
           L0_ffn_w_gate, L0_ffn_w_in, L0_ffn_w_out):
    x = np.ascontiguousarray(np.asarray(L0_attn_x, dtype=np.float32))
    xf = x.reshape(16384, D)
    wqT = np.ascontiguousarray(np.asarray(L0_attn_Wq, np.float32).T)
    wkT = np.ascontiguousarray(np.asarray(L0_attn_Wk, np.float32).T)
    wvT = np.ascontiguousarray(np.asarray(L0_attn_Wv, np.float32).T)
    wgT = np.ascontiguousarray(np.asarray(L0_ffn_w_gate, np.float32).T)
    winN = np.ascontiguousarray(np.asarray(L0_ffn_w_in, np.float32))
    woutT = np.ascontiguousarray(
        np.asarray(L0_ffn_w_out, np.float32).transpose(0, 2, 1))
    ident = np.eye(128, dtype=np.float32)

    nc = _build()
    in_maps = []
    for c in range(8):
        own = xf[c * 2048:(c + 1) * 2048]
        pc = c ^ 1
        par = xf[pc * 2048:(pc + 1) * 2048]
        xb = np.ascontiguousarray(np.concatenate([own, par], axis=0))
        in_maps.append({
            "xb": xb, "wqT": wqT, "wkT": wkT, "wvT": wvT, "wgT": wgT,
            "winN": winN, "woutT": woutT, "idin": ident,
        })
    res = run_bass_kernel_spmd(nc, in_maps, core_ids=list(range(8)))
    out = np.empty((16384, D), np.float32)
    for c in range(8):
        out[c * 2048:(c + 1) * 2048] = np.asarray(res.results[c]["out"],
                                                  np.float32)
    return out.reshape(4, 4096, D)


# version 12
# speedup vs baseline: 1.1198x; 1.0126x over previous
import numpy as np
import concourse.bass as bass
import concourse.mybir as mybir
from concourse.bass_utils import run_bass_kernel_spmd

F32 = mybir.dt.float32
BF16 = mybir.dt.bfloat16
ALU = mybir.AluOpType
AF = mybir.ActivationFunctionType
AXX = mybir.AxisListType.X

D = 1024
HID = 4096
NE = 8
TOK = 4096
OWN = 2048
NT = 32
NTO = 16
NP = 16
EXPS = 12.0
EPS = 1e-5
INVD = 1.0 / D
INVN = 1.0 / (4096.0 * 1024.0)

R_WQKV = 0
R_QT = (24576, 42016)
R_KT = (28672, 46112)
R_VB = (32768, 50208)
R_PT = 34848
R_OUTS = 36896
R_XNAT = 37920
R_YSCR = 39968
R_OSTG = 34848
R_WI = 0
R_WO = 16384
R_M = 32768
AR_N = 53248

BANKS4 = ("S0a", "S0b", "S1a", "S1b")
PROC = list(range(8, 16)) + list(range(8))
TILES = list(range(16, 32)) + list(range(16))


class _DryInst:
    def then_inc(self, sem, n=1):
        return self


_DRY_INST = _DryInst()


class _Dry:
    def __getattr__(self, name):
        def f(*a, **k):
            return _DRY_INST
        return f


def _g_decode(g):
    if g < 16:
        return 0, g
    if g < 32:
        return 1, g - 16
    return 2, g - 32


_GL = [("glog", tt) for tt in range(NTO)]
_MM = [("mmq", 0, 0, 0), ("mmq", 0, 0, 1), ("mmq", 0, 1, 0),
       ("mmq", 0, 1, 1), ("mmq", 1, 0, 0), ("mmq", 1, 0, 1),
       ("mmq", 1, 1, 0), ("mmq", 1, 1, 1)]
DL15 = (_GL[0:6] + [_MM[0], _GL[6], _GL[7], _MM[1], _GL[8], _GL[9],
                    _MM[2], _GL[10], _GL[11], _MM[3], _GL[12], _GL[13],
                    _MM[4], _GL[14], _GL[15]] + _MM[5:8])
DEALT_MM = {(0, 0, 0), (0, 0, 1), (0, 1, 0), (0, 1, 1),
            (1, 0, 0), (1, 0, 1), (1, 1, 0), (1, 1, 1)}

_DEAL_GS = [gs for gs in range(36) if gs % 3 != 2]


def _dl15_qc(qc):
    return [_DEAL_GS.index(gs) for gs in range(9 * qc, 9 * qc + 9)
            if gs % 3 != 2]


def _ab_schedule():
    seq = []
    for half in (0, 1):
        first = 8 * half
        for g in range(36):
            seq.append(("qkvg", first, g))
        for pj in range(first, first + 8):
            for qc in range(4):
                if pj < first + 7:
                    for g in range(9 * qc, 9 * qc + 9):
                        seq.append(("qkvg", pj + 1, g))
                elif pj == 15:
                    for i in _dl15_qc(qc):
                        seq.append(("dl", i))
                for jb in range(8):
                    seq.append(("atr", pj, qc, jb))
    seq.append(("sum2x",))
    seq.append(("bcx",))
    return {k: ("A" if i % 2 == 0 else "B") for i, k in enumerate(seq)}


BANKOF = _ab_schedule()


def _moe_a_bank(i, cb, hh, tc):
    return BANKS4[(i * 64 + cb * 8 + hh * 4 + tc) % 4]


def _moe_b_bank(i, dc, tt):
    return ("OTa", "OTb", "A", "B")[(i * 32 + dc * NTO + tt) % 4]


def _emit(T, rec, MS, cnt):
    AR = T["AR"]
    ARF = T["ARF"]
    xnTb = T["xnTb"]
    yat = T["yat"]
    yffn = T["yffn"]
    xstg = T["xstg"]
    wgsb = T["wgsb"]
    identsb = T["identsb"]
    onescol = T["onescol"]
    onesrow = T["onesrow"]
    consts = T["consts"]
    stats = T["stats"]
    st12 = T["st12"]
    lnm = T["lnm"]
    lnr = T["lnr"]
    gts = T["gts"]
    lgt = T["lgt"]
    gsc = T["gsc"]
    pS0, pS1, pOT = T["pS0"], T["pS1"], T["pOT"]
    S = T["sems"]
    g, sy, pe, dv, ac = T["g"], T["sy"], T["pe"], T["dv"], T["ac"]

    BK = {
        "A": T["pA"][:, 0:512], "B": T["pB"][:, 0:512],
        "S0a": pS0[:, 0:512], "S0b": pS0[:, 512:1024],
        "S1a": pS1[:, 0:512], "S1b": pS1[:, 512:1024],
        "OTa": pOT[:, 0:512], "OTb": pOT[:, 512:1024],
    }
    BKBF = {
        "A": T["pa_bf"][:, 0:1024], "B": T["pb_bf"][:, 0:1024],
        "S0a": T["ps0_bf"][:, 0:1024], "S0b": T["ps0_bf"][:, 1024:2048],
        "S1a": T["ps1_bf"][:, 0:1024], "S1b": T["ps1_bf"][:, 1024:2048],
        "OTa": T["pot_bf"][:, 0:1024], "OTb": T["pot_bf"][:, 1024:2048],
    }

    def inc(inst, semname, label=None, n=1):
        inst.then_inc(S[semname], n)
        cnt[semname] = cnt.get(semname, 0) + n
        if label is not None:
            if rec:
                MS[label] = (semname, cnt[semname])
            else:
                assert MS[label] == (semname, cnt[semname]), (
                    label, MS[label], (semname, cnt[semname]))

    def alias(dst_label, src_label):
        if rec:
            MS[dst_label] = MS[src_label]

    def W(eng, label):
        if rec and label not in MS:
            return
        sn, v = MS[label]
        eng.wait_ge(S[sn], v)

    OWN_ = {}

    def bacq(eng, name):
        lb = OWN_.get(name)
        if lb is not None:
            W(eng, lb)

    def brel(name, label):
        OWN_[name] = label

    NLOAD = 16 + 16 + 16 + 16 + NTO

    def x_row_tile(idx):
        if idx < 16:
            return TILES[idx]
        if idx < 32:
            pj, th = divmod(idx - 16, 2)
            return 2 * PROC[pj] + th
        if idx < 48:
            return TILES[16 + (idx - 32)]
        if idx < 64:
            pj, th = divmod(idx - 48, 2)
            return 2 * PROC[8 + pj] + th
        return idx - 64

    def x_consumers(idx):
        if idx < 16:
            return [("p0xn", idx), ("p0sq", idx)]
        if idx < 32:
            pj, th = divmod(idx - 16, 2)
            return [("xnat", pj, th)]
        if idx < 48:
            seq = 16 + (idx - 32)
            return [("p0xn", seq), ("p0sq", seq)]
        if idx < 64:
            pj, th = divmod(idx - 48, 2)
            return [("xnat", 8 + pj, th)]
        return [("fin", idx - 64)]

    x_load_sem = []
    lx = [0, 0]
    for i in range(NLOAD):
        sl = i % 2
        lx[sl] += 16
        x_load_sem.append(("sLX%d" % sl, lx[sl]))

    def emit_xload(idx):
        sl = idx % 2
        if idx >= 2:
            for lb in x_consumers(idx - 2):
                W(g, lb)
        tt = x_row_tile(idx)
        inc(g.dma_start(out=xstg[:, sl * D:(sl + 1) * D],
                        in_=T["xb"][tt * 128:(tt + 1) * 128, :]),
            "sLX%d" % sl, ("xld", idx), 16)

    def emit_gpsimd():
        g.memset(consts[:, 0:1], -EXPS)
        g.memset(consts[:, 1:2], EPS)
        g.memset(onescol[:], 1.0)
        g.memset(onesrow[:], 1.0)
        g.memset(stats[:, 0:2], 0.0)
        for par in range(2):
            for kb in range(32):
                off = R_VB[par] + kb * 65 + 64
                g.memset(AR[:, off: off + 1], 1.0)
        inc(g.memset(st12[:, 0:8], 0.0), "sGP", ("init",))

        inc(g.dma_start(out=identsb[:], in_=T["idin"]), "sID", ("idld",), 16)
        for m, w in enumerate((T["wqT"], T["wkT"], T["wvT"])):
            src = w.rearrange("(kt p) n -> p kt n", p=128)
            dst = AR[:, R_WQKV + m * 8192: R_WQKV + (m + 1) * 8192].rearrange(
                "p (kt n) -> p kt n", kt=8)
            for hvf in range(2):
                inc(g.dma_start(out=dst[:, hvf * 4:(hvf + 1) * 4, :],
                                in_=src[:, hvf * 4:(hvf + 1) * 4, :]),
                    "sLW", None, 16)
        inc(g.dma_start(
            out=wgsb[:].rearrange("p (kt n) -> p kt n", kt=8),
            in_=T["wgT"].rearrange("(kt p) n -> p kt n", p=128)),
            "sLW", ("wld",), 16)

        for idx in range(64):
            emit_xload(idx)

        def mq_load(Q):
            e, q = Q // 4, Q % 4
            slot = Q % 2
            for kk, (ten, base) in enumerate(((T["winN"], R_WI),
                                             (T["woutT"], R_WO))):
                srcq = ten[e, q * 1024:(q + 1) * 1024, :]
                srcq = srcq.rearrange("(kt p) n -> p kt n", p=128)
                dst = AR[:, base + slot * 8192: base + (slot + 1) * 8192]
                dst = dst.rearrange("p (kt n) -> p kt n", kt=8)
                for hvf in range(2):
                    inc(g.dma_start(out=dst[:, hvf * 4:(hvf + 1) * 4, :],
                                    in_=srcq[:, hvf * 4:(hvf + 1) * 4, :]),
                        "sWI%d" % slot, ("mqld", Q, 2 * kk + hvf), 16)

        W(g, ("qkv", 15, 35))
        W(g, ("st", 14, 3, 31))
        mq_load(0)
        mq_load(1)
        W(g, ("atrlast",))
        W(g, ("statslast",))
        W(g, ("explast",))
        for Q in range(2, 32):
            W(g, ("mmlast", Q - 2))
            mq_load(Q)
        for idx in range(64, 64 + NTO):
            emit_xload(idx)

    def act_phase0(seq0):
        for s_ in range(16):
            seq = seq0 + s_
            idx = seq if seq < 16 else 32 + (seq - 16)
            sl = idx % 2
            W(ac, ("xld", idx))
            if seq >= 1:
                W(ac, ("p0var", seq - 1))
            ac.drain()
            inst = ac.activation(pOT[:, 0:1024], xstg[:, sl * D:(sl + 1) * D],
                                 AF.Square, accum_out=gsc[:, 0:1])
            inc(inst, "sAC", ("p0sq", seq))
            W(ac, ("p0var", seq))
            ac.drain()
            ac.activation(gsc[:, 5:6], gsc[:, 4:5], AF.Ln, bias=consts[:, 1:2])
            ac.drain()
            tt = TILES[seq]
            inc(ac.activation(lnr[:, tt:tt + 1], gsc[:, 5:6], AF.Exp,
                              scale=-0.5), "sAC", ("p0rst", seq))

    def emit_act():
        ac.wait_ge(S["sGP"], 1)
        for half in range(2):
            act_phase0(16 * half)
            for pj in range(8 * half, 8 * half + 8):
                for qc in range(4):
                    for k in range(32):
                        gk = (pj * 4 + qc) * 32 + k
                        W(ac, ("st", pj, qc, k))
                        if gk >= 2:
                            g2 = gk - 2
                            W(ac, ("av", g2 // 128, (g2 // 32) % 4, g2 % 32))
                        sl = k % 2
                        stb = pS0 if sl == 0 else pS1
                        inst = ac.activation(
                            AR[:, R_PT + sl * 1024: R_PT + (sl + 1) * 1024],
                            stb[:, 0:1024], AF.Exp, bias=consts[:, 0:1])
                        inc(inst, "sAC", ("exp", pj, qc, k))
                    W(ac, ("av", pj, qc, 31))
                    if qc == 0 and pj > 0:
                        W(ac, ("statsacc", pj - 1, 1))
                    if qc > 0:
                        W(ac, ("atr", pj, qc - 1, 7))
                    ac.drain()
                    inc(ac.copy(AR[0:65, R_OUTS: R_OUTS + 1024],
                                pOT[0:65, 0:1024]), "sAC", ("outse", pj, qc))
        alias(("explast",), ("exp", NP - 1, 3, 31))

        W(ac, ("var1",))
        ac.drain()
        ac.activation(st12[0:1, 4:5], st12[0:1, 5:6], AF.Ln,
                      bias=consts[0:1, 1:2])
        ac.drain()
        inc(ac.activation(st12[0:1, 6:7], st12[0:1, 4:5], AF.Exp,
                          scale=-0.5), "sAC", ("rsig",))

        for tt in range(NTO):
            W(ac, ("gm", tt))
            ac.drain()
            ac.activation(gsc[:, 24:32], lgt[:], AF.Exp, bias=gsc[:, 1:2])
            ac.drain()
            inc(ac.activation(gsc[:, 3:4], gsc[:, 2:3], AF.Exp,
                              bias=gsc[:, 1:2]), "sAC", ("ge", tt))

    def emit_pe():
        def pe_qkv_group(pj, gidx):
            j = PROC[pj]
            toff = (256 * j) % 2048
            par = pj % 2
            m, sub = _g_decode(gidx)
            bk = BANKOF[("qkvg", pj, gidx)]
            bacq(pe, bk)
            last = None
            if m < 2:
                for kt in range(8):
                    lhs = AR[:, R_WQKV + m * 8192 + kt * 1024 + sub * 64:
                             R_WQKV + m * 8192 + kt * 1024 + sub * 64 + 65]
                    rhs = xnTb[:, kt * 2048 + toff: kt * 2048 + toff + 256]
                    last = pe.matmul(BK[bk][0:65, 0:256], lhs, rhs,
                                     start=(kt == 0), stop=(kt == 7))
            else:
                th, dc = sub // 2, sub % 2
                for kt in range(8):
                    lhs = xnTb[:, kt * 2048 + toff + th * 128:
                               kt * 2048 + toff + (th + 1) * 128]
                    rhs = AR[:, R_WQKV + 16384 + kt * 1024 + dc * 512:
                             R_WQKV + 16384 + kt * 1024 + (dc + 1) * 512]
                    last = pe.matmul(BK[bk][:, 0:512], lhs, rhs,
                                     start=(kt == 0), stop=(kt == 7))
            inc(last, "sPE", ("qkv", pj, gidx))
            brel(bk, ("qkve", pj, gidx))

        def av_step(pj, qc, k):
            sl = k % 2
            par = pj % 2
            W(pe, ("exp", pj, qc, k))
            if k == 0:
                bacq(pe, "OTa")
                bacq(pe, "OTb")
            last = None
            for hq in range(2):
                lhs = AR[:, R_VB[par] + k * 65: R_VB[par] + k * 65 + 65]
                rhs = AR[:, R_PT + sl * 1024 + hq * 512:
                         R_PT + sl * 1024 + (hq + 1) * 512]
                last = pe.matmul(pOT[0:65, hq * 512:(hq + 1) * 512], lhs, rhs,
                                 start=(k == 0), stop=(k == 31))
            inc(last, "sPE", ("av", pj, qc, k))
            if k == 31:
                brel("OTa", ("outse", pj, qc))
                brel("OTb", ("outse", pj, qc))

        def pe_deal15(di):
            item = DL15[di]
            bk = BANKOF[("dl", di)]
            if item[0] == "glog":
                tt = item[1]
                if di == 0:
                    W(pe, ("wld",))
                bacq(pe, bk)
                last = None
                for kt in range(8):
                    lhs = xnTb[:, kt * 2048 + tt * 128:
                               kt * 2048 + (tt + 1) * 128]
                    rhs = wgsb[:, kt * 8:(kt + 1) * 8]
                    last = pe.matmul(BK[bk][:, 0:8], lhs, rhs,
                                     start=(kt == 0), stop=(kt == 7))
                inc(last, "sPE", ("glog", tt))
                brel(bk, ("gloge", tt))
            else:
                _, Q, ic, jc = item
                slot = Q % 2
                if ic == 0 and jc == 0:
                    W(pe, ("mqld", Q, 3))
                bacq(pe, bk)
                last = None
                for kt in range(8):
                    lhs = AR[:, R_WI + slot * 8192 + kt * 1024 + ic * 128:
                             R_WI + slot * 8192 + kt * 1024 + (ic + 1) * 128]
                    rhs = AR[:, R_WO + slot * 8192 + kt * 1024 + jc * 512:
                             R_WO + slot * 8192 + kt * 1024 + (jc + 1) * 512]
                    last = pe.matmul(BK[bk][:, 0:512], lhs, rhs,
                                     start=(kt == 0), stop=(kt == 7))
                inc(last, "sPE", ("mm", Q, ic, jc))
                brel(bk, ("mme", Q, ic, jc))

        def pe_phase0_slice(seq0, s_):
            seq = seq0 + s_
            idx = seq if seq < 16 else 32 + (seq - 16)
            sl = idx % 2
            W(pe, ("p0xn", seq))
            for jb in range(8):
                gtc = seq * 8 + jb
                bk = BANKS4[gtc % 4]
                bacq(pe, bk)
                src = AR[:, R_XNAT + sl * 1024 + jb * 128:
                         R_XNAT + sl * 1024 + (jb + 1) * 128]
                inc(pe.transpose(BKBF[bk][:, 0:128], src, identsb[:]),
                    "sPE", ("p0tr", gtc))
                brel(bk, ("p0ev", gtc))

        pe.wait_ge(S["sGP"], 1)
        W(pe, ("idld",))
        for half in range(2):
            first = 8 * half
            for s_ in range(16):
                pe_phase0_slice(16 * half, s_)
                if s_ >= 1:
                    if s_ == 1:
                        if half == 0:
                            W(pe, ("wld",))
                        W(pe, ("p0ev", (16 * half + 1) * 8 + 7))
                    for gidx in range(3 * (s_ - 1), min(36, 3 * s_)):
                        pe_qkv_group(first, gidx)
            W(pe, ("p0ev", 16 * half * 8 + 127))
            brel("OTa", ("p0sq", 16 * half + 15))
            brel("OTb", ("p0sq", 16 * half + 15))
            for pj in range(first, first + 8):
                W(pe, ("qkve", pj, 35))
                bg = ([(pj + 1, gg) for gg in range(36)]
                      if pj < first + 7 else [])
                bgi = 0
                for qc in range(4):
                    dealt = 0
                    for k in range(32):
                        sl = k % 2
                        sb = ("S0a", "S0b") if sl == 0 else ("S1a", "S1b")
                        bacq(pe, sb[0])
                        bacq(pe, sb[1])
                        stb = pS0 if sl == 0 else pS1
                        par = pj % 2
                        last = None
                        for hq in range(2):
                            lhs = AR[0:64, R_KT[par] + (k // 2) * 256
                                     + (k % 2) * 128:
                                     R_KT[par] + (k // 2) * 256
                                     + (k % 2) * 128 + 128]
                            rhs = AR[0:64, R_QT[par] + qc * 1024 + hq * 512:
                                     R_QT[par] + qc * 1024 + (hq + 1) * 512]
                            last = pe.matmul(
                                stb[0:128, hq * 512:(hq + 1) * 512], lhs, rhs,
                                start=True, stop=True)
                        inc(last, "sPE", ("st", pj, qc, k))
                        brel(sb[0], ("exp", pj, qc, k))
                        brel(sb[1], ("exp", pj, qc, k))
                        if (bgi < len(bg)
                                and k in (1, 4, 7, 10, 13, 17, 21, 25, 29)):
                            pe_qkv_group(*bg[bgi])
                            bgi += 1
                        elif pj == 15 and dealt < 9 and k % 3 == 1:
                            gs = 9 * qc + dealt
                            dealt += 1
                            if gs % 3 != 2:
                                pe_deal15(_DEAL_GS.index(gs))
                        if k >= 1:
                            av_step(pj, qc, k - 1)
                    av_step(pj, qc, 31)
                    for jb in range(8):
                        bk = BANKOF[("atr", pj, qc, jb)]
                        bacq(pe, bk)
                        if jb == 0:
                            W(pe, ("outse", pj, qc))
                        src = AR[0:65, R_OUTS + jb * 128:
                                 R_OUTS + (jb + 1) * 128]
                        inc(pe.transpose(BKBF[bk][:, 0:65], src,
                                         identsb[0:65, 0:65]),
                            "sPE", ("atr", pj, qc, jb))
                        brel(bk, ("ntt", pj, qc, jb))
        alias(("atrlast",), ("atr", NP - 1, 3, 7))

        W(pe, ("statsacc", NP - 1, 1))
        bk = BANKOF[("sum2x",)]
        bacq(pe, bk)
        inc(pe.matmul(BK[bk][0:1, 0:2], onescol[:], stats[:, 0:2],
                      start=True, stop=True), "sPE", ("sum2",))
        brel(bk, ("sum2e",))
        W(pe, ("ab",))
        bk = BANKOF[("bcx",)]
        bacq(pe, bk)
        inc(pe.matmul(BK[bk][0:128, 0:2], onesrow[:], st12[0:1, 6:8],
                      start=True, stop=True), "sPE", ("bc",))
        brel(bk, ("bcev",))

        for e in range(NE):
            for q in range(4):
                Q = 4 * e + q
                W(pe, ("mqld", Q, 3))
                slot = Q % 2
                for ic in range(8):
                    for jc in range(2):
                        if (Q, ic, jc) in DEALT_MM:
                            continue
                        bk = ("S0a", "S0b")[(Q * 16 + ic * 2 + jc) % 2]
                        bacq(pe, bk)
                        last = None
                        for kt in range(8):
                            lhs = AR[:, R_WI + slot * 8192 + kt * 1024
                                     + ic * 128:
                                     R_WI + slot * 8192 + kt * 1024
                                     + (ic + 1) * 128]
                            rhs = AR[:, R_WO + slot * 8192 + kt * 1024
                                     + jc * 512:
                                     R_WO + slot * 8192 + kt * 1024
                                     + (jc + 1) * 512]
                            last = pe.matmul(BK[bk][:, 0:512], lhs, rhs,
                                             start=(kt == 0), stop=(kt == 7))
                        inc(last, "sPE", ("mm", Q, ic, jc))
                        brel(bk, ("mme", Q, ic, jc))
                alias(("mmlast", Q), ("mm", Q, 7, 1))
            W(pe, ("mme", 4 * e + 3, 7, 1))
            for dc in range(2):
                for tt in range(NTO):
                    bk = _moe_b_bank(e, dc, tt)
                    bacq(pe, bk)
                    last = None
                    for kt in range(8):
                        lhs = xnTb[:, kt * 2048 + tt * 128:
                                   kt * 2048 + (tt + 1) * 128]
                        rhs = AR[:, R_M + (e % 2) * 8192 + kt * 1024
                                 + dc * 512:
                                 R_M + (e % 2) * 8192 + kt * 1024
                                 + (dc + 1) * 512]
                        last = pe.matmul(BK[bk][:, 0:512], lhs, rhs,
                                         start=(kt == 0), stop=(kt == 7))
                    inc(last, "sPE", ("yph", e, dc, tt))
                    brel(bk, ("ybe", e, dc, tt))

    def emit_dve():
        def dve_qkv_evac(pj, gidx):
            par = pj % 2
            m, sub = _g_decode(gidx)
            bk = BANKOF[("qkvg", pj, gidx)]
            if gidx == 0 and pj >= 2:
                W(dv, ("st", pj - 2, 3, 31))
                W(dv, ("av", pj - 2, 3, 31))
            W(dv, ("qkv", pj, gidx))
            if m < 2:
                base = R_QT[par] if m == 0 else R_KT[par]
                inc(dv.tensor_copy(
                    AR[0:64, base + sub * 256: base + (sub + 1) * 256],
                    BK[bk][0:64, 0:256]), "sDV", ("qkve", pj, gidx))
            else:
                th, dc = sub // 2, sub % 2
                last = None
                for gg in range(8):
                    kb = 2 * (dc * 8 + gg) + th
                    off = R_VB[par] + kb * 65
                    last = dv.tensor_copy(AR[:, off: off + 64],
                                          BK[bk][:, gg * 64:(gg + 1) * 64])
                inc(last, "sDV", ("qkve", pj, gidx))

        def dve_deal15(i):
            item = DL15[i]
            bk = BANKOF[("dl", i)]
            if item[0] == "glog":
                tt = item[1]
                W(dv, ("glog", tt))
                dv.drain()
                inc(dv.tensor_copy(
                    ARF[:, R_YSCR // 2 + tt * 8: R_YSCR // 2 + tt * 8 + 8],
                    BK[bk][:, 0:8]), "sDV", ("gloge", tt))
            else:
                _, Q, ic, jc = item
                if (Q, ic, jc) == (0, 0, 0):
                    W(dv, ("av", 14, 3, 31))
                W(dv, ("mm", Q, ic, jc))
                msl = AR[:, R_M + ic * 1024 + jc * 512:
                         R_M + ic * 1024 + (jc + 1) * 512]
                dv.drain()
                if Q == 0:
                    inst = dv.tensor_copy(msl, BK[bk][:, 0:512])
                else:
                    inst = dv.tensor_tensor(msl, BK[bk][:, 0:512], msl,
                                            ALU.add)
                inc(inst, "sDV", ("mme", Q, ic, jc))

        def dve_phase0_slice(seq0, s_):
                seq = seq0 + s_
                idx = seq if seq < 16 else 32 + (seq - 16)
                sl = idx % 2
                tt = TILES[seq]
                W(dv, ("xld", idx))
                if seq >= 2:
                    W(dv, ("p0tr", (seq - 2) * 8 + 7))
                dv.drain()
                dv.reduce_sum(gsc[:, 1:2], xstg[:, sl * D:(sl + 1) * D],
                              axis=AXX)
                dv.drain()
                dv.tensor_scalar(lnm[:, tt:tt + 1], gsc[:, 1:2], -INVD, None,
                                 ALU.mult)
                W(dv, ("p0sq", seq))
                dv.tensor_scalar(gsc[:, 2:3], gsc[:, 0:1], INVD, None,
                                 ALU.mult)
                dv.drain()
                dv.tensor_mul(gsc[:, 3:4], lnm[:, tt:tt + 1],
                              lnm[:, tt:tt + 1])
                dv.drain()
                inc(dv.tensor_tensor(gsc[:, 4:5], gsc[:, 2:3], gsc[:, 3:4],
                                     ALU.subtract), "sDV", ("p0var", seq))
                W(dv, ("p0rst", seq))
                inc(dv.tensor_scalar(
                    AR[:, R_XNAT + sl * 1024: R_XNAT + (sl + 1) * 1024],
                    xstg[:, sl * D:(sl + 1) * D],
                    lnm[:, tt:tt + 1], lnr[:, tt:tt + 1], ALU.add, ALU.mult),
                    "sDV", ("p0xn", seq))
                for jb in range(8):
                    gtc = seq * 8 + jb
                    bk = BANKS4[gtc % 4]
                    W(dv, ("p0tr", gtc))
                    dcol = jb * 2048 + (tt % 16) * 128
                    inc(dv.tensor_copy(xnTb[:, dcol: dcol + 128],
                                       BKBF[bk][:, 0:128]),
                        "sDV", ("p0ev", gtc))

        dv.wait_ge(S["sGP"], 1)
        for half in range(2):
            first = 8 * half
            for s_ in range(16):
                dve_phase0_slice(16 * half, s_)
                if s_ >= 1:
                    for gidx in range(3 * (s_ - 1), min(36, 3 * s_)):
                        dve_qkv_evac(first, gidx)
            for pj in range(first, first + 8):
                j = PROC[pj]
                dv.drain()
                for th in range(2):
                    idx = (16 if half == 0 else 48) + 2 * (pj - first) + th
                    sn, v = x_load_sem[idx]
                    dv.wait_ge(S[sn], v)
                    tt = 2 * j + th
                    slx = idx % 2
                    inc(dv.tensor_scalar(
                        AR[:, R_XNAT + th * 1024: R_XNAT + (th + 1) * 1024],
                        xstg[:, slx * D:(slx + 1) * D],
                        lnm[:, tt:tt + 1], lnr[:, tt:tt + 1],
                        ALU.add, ALU.mult), "sDV", ("xnat", pj, th))
                for qc in range(4):
                    if pj < first + 7:
                        for gg in range(9 * qc, 9 * qc + 9):
                            dve_qkv_evac(pj + 1, gg)
                    elif pj == 15:
                        for i in _dl15_qc(qc):
                            dve_deal15(i)
                    for jb in range(8):
                        W(dv, ("atr", pj, qc, jb))
                        bkn = BANKOF[("atr", pj, qc, jb)]
                        gq = qc * 4 + jb // 2
                        hb = jb % 2
                        dv.drain()
                        dv.reciprocal(gsc[:, 6:7], BKBF[bkn][:, 64:65])
                        dv.drain()
                        if j < 8:
                            ydst = yat[:, (2 * j + hb) * 1024 + gq * 64:
                                       (2 * j + hb) * 1024 + gq * 64 + 64]
                        else:
                            ydst = AR[:, R_YSCR + hb * 1024 + gq * 64:
                                      R_YSCR + hb * 1024 + gq * 64 + 64]
                        xn_in = AR[:, R_XNAT + hb * 1024 + gq * 64:
                                   R_XNAT + hb * 1024 + gq * 64 + 64]
                        inc(dv.scalar_tensor_tensor(
                            ydst, BKBF[bkn][:, 0:64], gsc[:, 6:7], xn_in,
                            ALU.mult, ALU.add), "sDV", ("ntt", pj, qc, jb))
                dv.drain()
                for th in range(2):
                    if j < 8:
                        src = yat[:, (2 * j + th) * 1024:
                                  (2 * j + th + 1) * 1024]
                    else:
                        src = AR[:, R_YSCR + th * 1024:
                                 R_YSCR + (th + 1) * 1024]
                    dv.reduce_sum(gsc[:, 7:8], src, axis=AXX)
                    dv.drain()
                    dv.tensor_add(stats[:, 0:1], stats[:, 0:1], gsc[:, 7:8])
                    dv.drain()
                    dv.tensor_mul(AR[:, R_OUTS: R_OUTS + 1024], src, src)
                    dv.drain()
                    dv.reduce_sum(gsc[:, 7:8], AR[:, R_OUTS: R_OUTS + 1024],
                                  axis=AXX)
                    dv.drain()
                    inc(dv.tensor_add(stats[:, 1:2], stats[:, 1:2],
                                      gsc[:, 7:8]),
                        "sDV", ("statsacc", pj, th))
                    dv.drain()
        alias(("statslast",), ("statsacc", NP - 1, 1))

        W(dv, ("sum2",))
        dv.drain()
        inc(dv.tensor_copy(st12[0:1, 0:2], BK[BANKOF[("sum2x",)]][0:1, 0:2]),
            "sDV", ("sum2e",))
        dv.drain()
        dv.tensor_scalar(st12[0:1, 2:3], st12[0:1, 0:1], INVN, None, ALU.mult)
        dv.drain()
        dv.tensor_scalar(st12[0:1, 3:4], st12[0:1, 1:2], INVN, None, ALU.mult)
        dv.drain()
        dv.tensor_mul(st12[0:1, 4:5], st12[0:1, 2:3], st12[0:1, 2:3])
        dv.drain()
        inc(dv.tensor_tensor(st12[0:1, 5:6], st12[0:1, 3:4], st12[0:1, 4:5],
                             ALU.subtract), "sDV", ("var1",))
        W(dv, ("rsig",))
        dv.drain()
        dv.tensor_mul(st12[0:1, 7:8], st12[0:1, 2:3], st12[0:1, 6:7])
        dv.drain()
        inc(dv.tensor_scalar(st12[0:1, 7:8], st12[0:1, 7:8], -1.0, None,
                             ALU.mult), "sDV", ("ab",))
        W(dv, ("bc",))
        dv.drain()
        inc(dv.tensor_copy(stats[:, 2:4], BK[BANKOF[("bcx",)]][:, 0:2]),
            "sDV", ("bcev",))

        for tt in range(NTO):
            dv.drain()
            dv.tensor_copy(lgt[:], ARF[:, R_YSCR // 2 + tt * 8:
                                       R_YSCR // 2 + tt * 8 + 8])
            dv.drain()
            dv.reduce_max(gsc[:, 0:1], lgt[:], axis=AXX)
            dv.drain()
            dv.tensor_scalar(gsc[:, 1:2], gsc[:, 0:1], -1.0, None, ALU.mult)
            dv.drain()
            dv.tensor_scalar(gsc[:, 8:16], lgt[:], gsc[:, 0:1], None,
                             ALU.is_equal)
            dv.drain()
            dv.scalar_tensor_tensor(gsc[:, 16:24], gsc[:, 8:16], -1e30,
                                    lgt[:], ALU.mult, ALU.add)
            dv.drain()
            inc(dv.reduce_max(gsc[:, 2:3], gsc[:, 16:24], axis=AXX),
                "sDV", ("gm", tt))
            W(dv, ("ge", tt))
            dv.drain()
            dv.tensor_scalar(gsc[:, 4:5], gsc[:, 3:4], 1.0, None, ALU.add)
            dv.drain()
            dv.reciprocal(gsc[:, 5:6], gsc[:, 4:5])
            dv.drain()
            dv.tensor_scalar(gsc[:, 8:16], lgt[:], gsc[:, 2:3], None,
                             ALU.is_ge)
            dv.drain()
            dv.tensor_mul(gsc[:, 24:32], gsc[:, 24:32], gsc[:, 8:16])
            dv.drain()
            inc(dv.tensor_scalar(gts[:, tt * 8:(tt + 1) * 8],
                                 gsc[:, 24:32], gsc[:, 5:6], None, ALU.mult),
                "sDV", ("gate", tt))
            dv.drain()

        for e in range(NE):
            for q in range(4):
                Q = 4 * e + q
                dv.drain()
                if Q == 0:
                    W(dv, ("av", 15, 3, 31))
                    W(dv, ("exp", 15, 3, 31))
                    W(dv, ("outse", 15, 3))
                    W(dv, ("atr", 15, 3, 7))
                for ic in range(8):
                    for jc in range(2):
                        if (Q, ic, jc) in DEALT_MM:
                            continue
                        if q == 0 and ic == 0 and jc == 0 and e >= 2:
                            W(dv, ("yph", e - 2, 1, NTO - 1))
                        W(dv, ("mm", Q, ic, jc))
                        bkn = ("S0a", "S0b")[(Q * 16 + ic * 2 + jc) % 2]
                        msl = AR[:, R_M + (e % 2) * 8192 + ic * 1024
                                 + jc * 512:
                                 R_M + (e % 2) * 8192 + ic * 1024
                                 + (jc + 1) * 512]
                        if q == 0:
                            inst = dv.tensor_copy(msl, BK[bkn][:, 0:512])
                        else:
                            inst = dv.tensor_tensor(msl, BK[bkn][:, 0:512],
                                                    msl, ALU.add)
                        inc(inst, "sDV", ("mme", Q, ic, jc))
            dv.drain()
            for dc in range(2):
                for tt in range(NTO):
                    W(dv, ("yph", e, dc, tt))
                    bkn = _moe_b_bank(e, dc, tt)
                    ysl = yffn[:, tt * 1024 + dc * 512:
                               tt * 1024 + (dc + 1) * 512]
                    gcol = gts[:, tt * 8 + e: tt * 8 + e + 1]
                    if e == 0:
                        inst = dv.tensor_scalar(ysl, BK[bkn][:, 0:512], gcol,
                                                None, ALU.mult)
                    else:
                        inst = dv.scalar_tensor_tensor(
                            ysl, BK[bkn][:, 0:512], gcol, ysl,
                            ALU.mult, ALU.add)
                    inc(inst, "sDV", ("ybe", e, dc, tt))

        W(dv, ("yph", NE - 1, 1, NTO - 1))
        dv.drain()
        for tt in range(NTO):
            W(dv, ("ybe", NE - 1, 1, tt))
            idx = 64 + tt
            sn, v = x_load_sem[idx]
            dv.wait_ge(S[sn], v)
            if tt >= 2:
                dv.wait_ge(S["sOUT%d" % (tt % 2)], 16 * (tt // 2))
            sl = idx % 2
            ost = ARF[:, R_OSTG // 2 + (tt % 2) * 1024:
                      R_OSTG // 2 + ((tt % 2) + 1) * 1024]
            dv.drain()
            dv.tensor_scalar(ost, yat[:, tt * 1024:(tt + 1) * 1024],
                             stats[:, 2:3], stats[:, 3:4], ALU.mult, ALU.add)
            dv.drain()
            dv.tensor_add(ost, ost, yffn[:, tt * 1024:(tt + 1) * 1024])
            dv.drain()
            inc(dv.tensor_add(ost, ost, xstg[:, sl * D:(sl + 1) * D]),
                "sDV", ("fin", tt))

    def emit_sync():
        for tt in range(NTO):
            W(sy, ("fin", tt))
            inc(sy.dma_start(
                out=T["out"][tt * 128:(tt + 1) * 128, :],
                in_=ARF[:, R_OSTG // 2 + (tt % 2) * 1024:
                        R_OSTG // 2 + ((tt % 2) + 1) * 1024]),
                "sOUT%d" % (tt % 2), ("outst", tt), 16)
        sy.wait_ge(S["sOUT0"], 16 * (NTO // 2))
        sy.wait_ge(S["sOUT1"], 16 * (NTO // 2))

    emit_gpsimd()
    emit_act()
    emit_pe()
    emit_dve()
    emit_sync()


def _build():
    from contextlib import ExitStack
    nc = bass.Bass("TRN2", target_bir_lowering=False, debug=False)
    dram = dict(
        xb=nc.dram_tensor("xb", [TOK, D], F32, kind="ExternalInput").ap(),
        wqT=nc.dram_tensor("wqT", [D, D], F32, kind="ExternalInput").ap(),
        wkT=nc.dram_tensor("wkT", [D, D], F32, kind="ExternalInput").ap(),
        wvT=nc.dram_tensor("wvT", [D, D], F32, kind="ExternalInput").ap(),
        wgT=nc.dram_tensor("wgT", [D, NE], F32, kind="ExternalInput").ap(),
        winN=nc.dram_tensor("winN", [NE, HID, D], F32,
                            kind="ExternalInput").ap(),
        woutT=nc.dram_tensor("woutT", [NE, HID, D], F32,
                             kind="ExternalInput").ap(),
        idin=nc.dram_tensor("idin", [128, 128], F32,
                            kind="ExternalInput").ap(),
        out=nc.dram_tensor("out", [OWN, D], F32, kind="ExternalOutput").ap(),
    )

    with ExitStack() as _st:
        AR_t = _st.enter_context(nc.sbuf_tensor("AR", [128, AR_N], BF16))
        xnTb = _st.enter_context(nc.sbuf_tensor("xnTb", [128, 8 * 2048], BF16))
        yat = _st.enter_context(nc.sbuf_tensor("yat", [128, NTO * D], BF16))
        yffn = _st.enter_context(nc.sbuf_tensor("yffn", [128, NTO * D], BF16))
        xstg = _st.enter_context(nc.sbuf_tensor("xstg", [128, 2 * D], BF16))
        wgsb = _st.enter_context(nc.sbuf_tensor("wgsb", [128, 8 * NE], BF16))
        identsb = _st.enter_context(
            nc.sbuf_tensor("identsb", [128, 128], BF16))
        onescol = _st.enter_context(nc.sbuf_tensor("onescol", [128, 1], F32))
        onesrow = _st.enter_context(nc.sbuf_tensor("onesrow", [1, 128], F32))
        consts = _st.enter_context(nc.sbuf_tensor("consts", [128, 4], F32))
        stats = _st.enter_context(nc.sbuf_tensor("stats", [128, 8], F32))
        st12 = _st.enter_context(nc.sbuf_tensor("st12", [1, 8], F32))
        lnm = _st.enter_context(nc.sbuf_tensor("lnm", [128, NT], F32))
        lnr = _st.enter_context(nc.sbuf_tensor("lnr", [128, NT], F32))
        gts = _st.enter_context(nc.sbuf_tensor("gts", [128, NTO * NE], F32))
        lgt = _st.enter_context(nc.sbuf_tensor("lgt", [128, NE], F32))
        gsc = _st.enter_context(nc.sbuf_tensor("gsc", [128, 40], F32))
        P_A = _st.enter_context(nc.psum_tensor("P_A", [128, 512], F32))
        P_B = _st.enter_context(nc.psum_tensor("P_B", [128, 512], F32))
        P_S0 = _st.enter_context(nc.psum_tensor("P_S0", [128, 1024], F32))
        P_S1 = _st.enter_context(nc.psum_tensor("P_S1", [128, 1024], F32))
        P_OT = _st.enter_context(nc.psum_tensor("P_OT", [128, 1024], F32))
        semnames = ("sLX0", "sLX1", "sLW", "sID", "sWI0", "sWI1", "sWO0",
                    "sWO1", "sOUT0", "sOUT1", "sPE", "sDV", "sAC", "sGP")
        sems = {n: _st.enter_context(nc.semaphore(n)) for n in semnames}
        block = _st.enter_context(nc.Block())

        base = dict(dram)
        base.update(
            AR=AR_t.ap(), ARF=AR_t.bitcast(F32).ap(),
            xnTb=xnTb.ap(), yat=yat.ap(), yffn=yffn.ap(), xstg=xstg.ap(),
            wgsb=wgsb.ap(), identsb=identsb.ap(), onescol=onescol.ap(),
            onesrow=onesrow.ap(), consts=consts.ap(), stats=stats.ap(),
            st12=st12.ap(), lnm=lnm.ap(), lnr=lnr.ap(), gts=gts.ap(),
            lgt=lgt.ap(), gsc=gsc.ap(),
            pA=P_A.ap(), pB=P_B.ap(), pS0=P_S0.ap(), pS1=P_S1.ap(),
            pOT=P_OT.ap(),
            pa_bf=P_A.bitcast(BF16).ap(), pb_bf=P_B.bitcast(BF16).ap(),
            ps0_bf=P_S0.bitcast(BF16).ap(), ps1_bf=P_S1.bitcast(BF16).ap(),
            pot_bf=P_OT.bitcast(BF16).ap(),
            sems=sems,
        )

        dry = _Dry()
        MS = {}
        Trec = dict(base, g=dry, sy=dry, pe=dry, dv=dry, ac=dry)
        _emit(Trec, True, MS, {})

        def one(real_name, eng):
            Tx = dict(base, g=dry, sy=dry, pe=dry, dv=dry, ac=dry)
            Tx[real_name] = eng
            _emit(Tx, False, MS, {})

        @block.gpsimd
        def _(eng):
            one("g", eng)

        @block.sync
        def _(eng):
            one("sy", eng)

        @block.tensor
        def _(eng):
            one("pe", eng)

        @block.vector
        def _(eng):
            one("dv", eng)

        @block.scalar
        def _(eng):
            one("ac", eng)

    return nc


def kernel(L0_attn_x, L0_attn_Wq, L0_attn_Wk, L0_attn_Wv,
           L0_ffn_w_gate, L0_ffn_w_in, L0_ffn_w_out):
    x = np.ascontiguousarray(np.asarray(L0_attn_x, dtype=np.float32))
    xf = x.reshape(16384, D)
    wqT = np.ascontiguousarray(np.asarray(L0_attn_Wq, np.float32).T)
    wkT = np.ascontiguousarray(np.asarray(L0_attn_Wk, np.float32).T)
    wvT = np.ascontiguousarray(np.asarray(L0_attn_Wv, np.float32).T)
    wgT = np.ascontiguousarray(np.asarray(L0_ffn_w_gate, np.float32).T)
    winN = np.ascontiguousarray(np.asarray(L0_ffn_w_in, np.float32))
    woutT = np.ascontiguousarray(
        np.asarray(L0_ffn_w_out, np.float32).transpose(0, 2, 1))
    ident = np.eye(128, dtype=np.float32)

    nc = _build()
    in_maps = []
    for c in range(8):
        own = xf[c * 2048:(c + 1) * 2048]
        pc = c ^ 1
        par = xf[pc * 2048:(pc + 1) * 2048]
        xb = np.ascontiguousarray(np.concatenate([own, par], axis=0))
        in_maps.append({
            "xb": xb, "wqT": wqT, "wkT": wkT, "wvT": wvT, "wgT": wgT,
            "winN": winN, "woutT": woutT, "idin": ident,
        })
    res = run_bass_kernel_spmd(nc, in_maps, core_ids=list(range(8)))
    out = np.empty((16384, D), np.float32)
    for c in range(8):
        out[c * 2048:(c + 1) * 2048] = np.asarray(res.results[c]["out"],
                                                  np.float32)
    return out.reshape(4, 4096, D)
